# version 12
# baseline (speedup 1.0000x reference)
import numpy as np
import concourse.bass as bass
ABLATE = set()
import concourse.mybir as mybir
from concourse.bass_utils import run_bass_kernel_spmd
from contextlib import ExitStack

F32 = mybir.dt.float32
BF16 = mybir.dt.bfloat16
F8 = mybir.dt.float8e4
I32 = mybir.dt.int32
AF = mybir.ActivationFunctionType
ALU = mybir.AluOpType
AX = mybir.AxisListType

E = 1024
H = 8
D = 128
ECH = 8
ICH = 8
LN_EPS = 1e-5
RMS_EPS = 1e-8
FP8_MAX_HALF = 224.0
LN2C = float(np.log(2.0))


class _Op:
    __slots__ = ("eng", "fn", "deps", "dma", "skey", "needs_inc", "val", "tag", "rdn", "wrn")

    def __init__(self, eng, fn, deps, dma, skey, tag):
        self.eng, self.fn, self.deps, self.dma, self.skey, self.tag = eng, fn, deps, dma, skey, tag
        self.needs_inc = False
        self.val = None
        self.rdn = ()
        self.wrn = ()


class Buf:
    __slots__ = ("w", "r", "name", "psum")

    def __init__(self, name="", psum=False):
        self.w, self.r, self.name, self.psum = None, [], name, psum


class Sched:
    SERIAL = False
    SERIAL_ENGS = set()

    def __init__(self):
        self.ops = []

    def op(self, eng, fn, rd=(), wr=(), dma=False, deps=(), skey=None, tag=""):
        dep = set()
        for b in rd:
            if b.w is not None:
                dep.add(b.w)
            if b.psum:
                for r_ in b.r:
                    if r_.eng != eng:
                        dep.add(r_)
        for b in wr:
            if b.w is not None:
                dep.add(b.w)
            dep.update(b.r)
        for d in deps:
            if d is not None:
                dep.add(d)
        o = _Op(eng, fn, dep, dma, skey or (eng, dma), tag)
        o.rdn = tuple(b.name for b in rd)
        o.wrn = tuple(b.name for b in wr)
        self.ops.append(o)
        for b in rd:
            b.r.append(o)
        for b in wr:
            b.w = o
            b.r = []
        return o

    NDMASEM = 6
    RELAX_SAME_ENGINE = {("pe", False), ("sc", False)}

    def _skip(self, d, o):
        return (d.skey == o.skey and d.skey in self.RELAX_SAME_ENGINE
                and not d.dma and not o.dma)

    def finalize(self):
        if Sched.SERIAL:
            prev = None
            for o in self.ops:
                if o.fn is not None and prev is not None:
                    o.deps.add(prev)
                if o.fn is not None:
                    prev = o
        if Sched.SERIAL_ENGS:
            prev_any = None
            last_chained = None
            for o in self.ops:
                if o.fn is None:
                    continue
                if o.eng in Sched.SERIAL_ENGS:
                    if prev_any is not None:
                        o.deps.add(prev_any)
                    last_chained = o
                elif last_chained is not None:
                    o.deps.add(last_chained)
                prev_any = o
        rot = {}
        for o in self.ops:
            if o.dma:
                q = o.skey
                j = rot.get(q, 0)
                rot[q] = (j + 1) % self.NDMASEM
                o.skey = (q[0], "dma", j)
                o.needs_inc = True
        for o in self.ops:
            o.deps = {d for d in o.deps if not self._skip(d, o)}
        for o in self.ops:
            o.needs_inc = o.needs_inc and False or o.needs_inc
        for o in self.ops:
            o.needs_inc = bool(o.dma)
        for o in self.ops:
            for d in o.deps:
                d.needs_inc = True
        cnt = {}
        for o in self.ops:
            if o.needs_inc:
                cnt[o.skey] = cnt.get(o.skey, 0) + (16 if o.dma else 1)
                o.val = cnt[o.skey]

    def emit(self, eng_name, handle, sems):
        water = {}
        for o in self.ops:
            if o.eng != eng_name:
                continue
            waits = {}
            for d in o.deps:
                if d.val is None:
                    raise RuntimeError(f"dep without inc: {d.tag} <- {o.tag}")
                waits[d.skey] = max(waits.get(d.skey, 0), d.val)
            if o.dma and o.val >= 32:
                waits[o.skey] = max(waits.get(o.skey, 0), o.val - 16)
            for key in sorted(waits, key=str):
                v = waits[key]
                if water.get(key, 0) < v:
                    handle.wait_ge(sems[key], v)
                    water[key] = v
            if o.fn is None:
                continue
            inst = o.fn(handle)
            if o.needs_inc:
                inst.then_inc(sems[o.skey], 16 if o.dma else 1)


def build_program(T=2048, C=512, n_cores=8, max_ops=None):
    NCH = T // C
    NTT = C // 128
    NSUB = T // 128
    nc = bass.Bass("TRN2", target_bir_lowering=False, detect_race_conditions=False)

    x_tm_d = nc.dram_tensor("x_tm", [T, E], F32, kind="ExternalInput")
    x_fm_d = nc.dram_tensor("x_fm", [E, T], F32, kind="ExternalInput")
    w_d = {n: nc.dram_tensor(n, [E, E], F32, kind="ExternalInput")
           for n in ("wq_t", "wk_t", "wv_t", "wu_t", "wo_t", "wf_t")}
    bias_d = {n: nc.dram_tensor(n, [1, E], F32, kind="ExternalInput")
              for n in ("bq_row", "bk_row", "bv_row", "bu_row")}
    bo_row_d = nc.dram_tensor("bo_row", [1, E], F32, kind="ExternalInput")
    bf_row_d = nc.dram_tensor("bf_row", [1, E], F32, kind="ExternalInput")
    scale_d = nc.dram_tensor("scale_pt", [128, H], F32, kind="ExternalInput")
    ident_d = nc.dram_tensor("ident", [128, 128], F32, kind="ExternalInput")
    triu_d = nc.dram_tensor("triu", [128, 128], F32, kind="ExternalInput")
    ones_d = nc.dram_tensor("ones_row", [1, 512], F32, kind="ExternalInput")
    onescol_d = nc.dram_tensor("onescol", [128, 1], F32, kind="ExternalInput")
    ones11_d = nc.dram_tensor("ones11", [1, 128], F32, kind="ExternalInput")
    emask_d = nc.dram_tensor("emask", [128, 1], F32, kind="ExternalInput")
    omask_d = nc.dram_tensor("omask", [128, 1], F32, kind="ExternalInput")
    y_d = nc.dram_tensor("y", [T, E], F32, kind="ExternalOutput")

    res2_d = nc.dram_tensor("res2_scratch", [T, E], F32)
    ccs_in_d = nc.dram_tensor("ccs_in", [128, H * D], BF16)
    ccs_out_d = nc.dram_tensor("ccs_out", [128, H * D], BF16)
    ccm_in_d = nc.dram_tensor("ccm_in", [1, 128], F32)
    ccm_out_d = nc.dram_tensor("ccm_out", [1, 128], F32)

    es = ExitStack()
    sbt = lambda name, shape, dt: es.enter_context(nc.sbuf_tensor(name, shape, dt))
    pst = lambda name, shape: es.enter_context(nc.psum_tensor(name, shape, F32))

    ktm_st = sbt("ktm_st", [128, NSUB * E], BF16)
    vtm_st = sbt("vtm_st", [128, NSUB * E], BF16)
    wA = sbt("wA", [128, ECH * E], BF16)
    wB = sbt("wB", [128, ECH * E], BF16)
    wC = sbt("wC", [128, ECH * E], BF16)
    wf8 = sbt("wf8", [128, ECH * E], F8)
    wf32 = sbt("wf32", [128, 256], F32)
    pkA = sbt("pkA", [34, E], BF16)
    pkB = sbt("pkB", [34, E], BF16)
    pk = {"q": (pkA, 0), "k": (pkA, 32), "v": (pkB, 0), "u": (pkB, 32)}
    rows_dyn = [sbt(f"rows_dyn{i}", [34, 512], BF16) for i in range(2)]
    ones_bf = sbt("ones_bf", [1, 128], BF16)
    bo_row = sbt("bo_row_s", [1, E], BF16)
    bf_row = sbt("bf_row_s", [1, E], BF16)
    scale_pt = sbt("scale_pt_s", [128, H], F32)
    sclh = sbt("sclh", [128, H], F32)
    ident_bf = sbt("ident_bf", [128, 128], BF16)
    ident_f32 = sbt("ident_f32", [128, 128], F32)
    triu_f = sbt("triu_f", [128, 128], BF16)
    onescol = sbt("onescol_s", [128, 1], BF16)
    ones11 = sbt("ones11_s", [1, 128], F32)
    emask = sbt("emask_s", [128, 1], F32)
    omask = sbt("omask_s", [128, 1], F32)

    xf32 = [sbt(f"xf32_{i}", [128, 512], F32) for i in range(2)]
    xn = sbt("xn", [128, ECH * C], BF16)
    rbc = sbt("rbc", [128, C], BF16)
    x_tile = [sbt(f"x_tile{i}", [128, E], F32) for i in range(2)]
    q_s = sbt("q_s", [128, H * C], BF16)
    us_s = sbt("us_s", [128, H * C], BF16)
    e_scr = sbt("e_scr", [128, 512], F32)
    t2_scr = sbt("t2_scr", [128, 512], BF16)
    rrow = sbt("rrow", [1, C], BF16)
    statc = sbt("statc", [128, 1], BF16)

    kfm_sub = [sbt("kfm0", [128, E], BF16)]
    eT_s = [sbt("eT0", [128, H * 128], BF16)]
    S_acc = sbt("S_acc", [128, H * D], F32)
    ssb = [sbt("ssb0", [128, H * D], BF16)]
    sinit = sbt("sinit", [128, H * D], BF16)
    g1_s = [sbt(f"g1_{i}", [128, E], BF16) for i in range(NTT)]
    res2_s = [sbt(f"res2_{i}", [128, E], F32) for i in range(2)]
    resin = x_tile

    xq_s = [sbt(f"xq{i}", [128, E], F8) for i in range(2)]
    xqf_s = [sbt("xqf0", [128, E], F8)]

    bn6 = sbt("bn6", [128, 12], F32)
    bn2 = sbt("bn2", [128, 2], F32)
    l1m = sbt("l1m", [128, NSUB], F32)
    l1v = sbt("l1v", [128, NSUB], F32)
    l1r = sbt("l1r", [128, NSUB], F32)
    l1mr = sbt("l1mr", [128, NSUB], F32)
    l2sum = sbt("l2sum", [128, NSUB], F32)
    l2sq = sbt("l2sq", [128, NSUB], F32)
    l2max = sbt("l2max", [128, NSUB], F32)
    l2min = sbt("l2min", [128, NSUB], F32)
    l2m = sbt("l2m", [128, NSUB], F32)
    l2r = sbt("l2r", [128, NSUB], F32)
    qa = sbt("qa", [128, NSUB], F32)
    qb = sbt("qb", [128, NSUB], F32)
    nsc = sbt("nsc", [128, NSUB], F32)
    nsc2 = sbt("nsc2", [128, NSUB], F32)
    nsc3 = sbt("nsc3", [128, NSUB], F32)
    exp_in = nsc2
    rcp_t = sbt("rcp_t", [128, NTT], F32)
    rmse_bf = sbt("rmse_bf", [128, NTT], BF16)
    wmax = sbt("wmax", [128, 4 * ECH], F32)
    wmax1 = sbt("wmax1", [128, 1], F32)
    wrow = sbt("wrow", [1, 128], F32)
    scalar_sb = sbt("scalar_sb", [1, 8], F32)
    swh_col = sbt("swh_col", [128, 1], F32)
    sxcol = sbt("sxcol", [128, 1], F32)
    d_col = sbt("d_col", [128, 1], F32)
    amrow = sbt("amrow", [128, 1], F32)
    seedb = sbt("seedb", [128, 1], F32)

    S_ps = pst("S_ps", [128, 1024])
    pBig = pst("pBig", [128, 1024])
    pS1 = pst("pS1", [128, 512])
    pS2 = pst("pS2", [128, 512])
    pT1 = pst("pT1", [128, 512])
    pT2 = pst("pT2", [128, 512])

    B = {}
    PSUM_BUFS = {"S_ps", "pBig", "pS1", "pS2", "pT1", "pT2"}

    def buf(name):
        if name not in B:
            B[name] = Buf(name, psum=(name in PSUM_BUFS))
        return B[name]

    sch = Sched()
    OP = sch.op

    def dma(eng, out_ap, in_ap, rd, wr, deps=(), tag=""):
        return OP(eng, (lambda o, i: (lambda e: e.dma_start(out=o, in_=i)))(out_ap, in_ap),
                  rd=rd, wr=wr, dma=True, deps=deps, tag=tag)

    for n, key in (("q", "bq_row"), ("k", "bk_row"), ("v", "bv_row"), ("u", "bu_row")):
        t, po = pk[n]
        dma("gp", t[po + 1:po + 2, :], bias_d[key][:, :], [], [buf(f"pk_{n}_b")], tag=f"ldb{n}")
    for i in range(2):
        dma("gp", rows_dyn[i][1:2, :], ones_d[:, :], [], [buf(f"rows_ones{i}")], tag=f"ldro{i}")
        dma("gp", rows_dyn[i][33:34, :], ones_d[:, :], [], [buf(f"rows_ones{i}")], tag=f"ldro32_{i}")
    dma("gp", bo_row[:, :], bo_row_d[:, :], [], [buf("bo_row")], tag="ldbo")
    dma("gp", bf_row[:, :], bf_row_d[:, :], [], [buf("bf_row")], tag="ldbf")
    dma("gp", ident_bf[:, :], ident_d[:, :], [], [buf("ident_bf")], tag="ldidb")
    dma("gp", ones_bf[:, :], ones_d[0:1, 0:128], [], [buf("ones_bf")], tag="ldonesbf")
    dma("gp", onescol[:, :], onescol_d[:, :], [], [buf("onescol")], tag="ldoc")
    dma("sy", ident_f32[:, :], ident_d[:, :], [], [buf("ident_f32")], tag="ldidf")
    dma("gp", triu_f[:, :], triu_d[:, :], [], [buf("triu_f")], tag="ldtriu")
    dma("sy", scale_pt[:, :], scale_d[:, :], [], [buf("scale_pt")], tag="ldscale")
    dma("sy", ones11[:, :], ones11_d[:, :], [], [buf("ones11")], tag="ldo11")
    dma("sy", emask[:, :], emask_d[:, :], [], [buf("emask")], tag="ldem")
    dma("sy", omask[:, :], omask_d[:, :], [], [buf("omask")], tag="ldom")
    OP("dv", lambda e: e.tensor_scalar(sclh[:, :], scale_pt[:, :], 0.25, None, ALU.mult),
       rd=[buf("scale_pt")], wr=[buf("sclh")], tag="sclh")
    OP("dv", lambda e: e.memset(seedb[:, :], float(LN2C * 63.5)),
       rd=[], wr=[buf("seedb")], tag="seedb")

    def wload(wname, tgt, tgt_buf, src, tag, replaces=None):
        deps = []
        if replaces is not None:
            b = buf(replaces)
            deps = list(b.r) + ([b.w] if b.w else [])
        last = None
        for ec in range(ECH):
            last = dma("gp", tgt[:, :].rearrange("p (ec i) -> p ec i", ec=ECH)[:, ec, :],
                       src.rearrange("(ec p) i -> p ec i", p=128)[:, ec, :],
                       [], [buf(tgt_buf)], deps=deps, tag=f"{tag}_{ec}")
        return last

    def emit_pk(name, wt, wt_buf):
        t, po = pk[name]
        for half in range(2):
            p, pb = (pT1, buf("pT1")) if half == 0 else (pT2, buf("pT2"))
            def grp(e, half=half, p=p, wt=wt, po=po):
                inst = None
                for ec in range(ECH):
                    inst = e.matmul(p[po:po + 1, 0:512], onescol[:, :],
                                    wt[:, ec * E + half * 512: ec * E + (half + 1) * 512],
                                    start=(ec == 0), stop=(ec == ECH - 1))
                return inst
            OP("pe", grp, rd=[wt_buf, buf("onescol")], wr=[pb], tag=f"pk{name}{half}")
            OP("dv", (lambda t=t, po=po, half=half, p=p: (lambda e: e.tensor_scalar(
                t[po:po + 1, half * 512:(half + 1) * 512], p[po:po + 1, 0:512], -1.0, None, ALU.mult)))(),
               rd=[pb], wr=[buf(f"pk_{name}_c")], tag=f"pk{name}e{half}")



    def newton_rsqrt(src_buf, src_ap, out_buf, out_ap, eps_add, tag, iters=3):
        n = src_ap.shape[-1]
        OP("dv", lambda e: e.tensor_scalar(nsc3[:, 0:n], src_ap, float(eps_add), None, ALU.add),
           rd=[src_buf], wr=[buf("nsc3")], tag=f"{tag}e")
        OP("dv", lambda e: e.tensor_copy(exp_in[:, 0:n], nsc3[:, 0:n].bitcast(I32)),
           rd=[buf("nsc3")], wr=[buf("nsc2")], tag=f"{tag}b")
        OP("sc", lambda e: e.activation(nsc[:, 0:n], exp_in[:, 0:n], AF.Exp,
                                        bias=seedb[:, :],
                                        scale=float(-LN2C * 0.5 / (2 ** 23))),
           rd=[buf("nsc2"), buf("seedb")], wr=[buf("nsc")], tag=f"{tag}s")
        for it in range(iters):
            OP("dv", lambda e: e.tensor_mul(nsc2[:, 0:n], nsc[:, 0:n], nsc[:, 0:n]),
               rd=[buf("nsc")], wr=[buf("nsc2")], tag=f"{tag}{it}a")
            OP("dv", lambda e: e.tensor_mul(nsc2[:, 0:n], nsc2[:, 0:n], nsc3[:, 0:n]),
               rd=[buf("nsc2"), buf("nsc3")], wr=[buf("nsc2")], tag=f"{tag}{it}b")
            OP("dv", lambda e: e.tensor_scalar(nsc2[:, 0:n], nsc2[:, 0:n], -0.5, 1.5, ALU.mult, ALU.add),
               rd=[buf("nsc2")], wr=[buf("nsc2")], tag=f"{tag}{it}c")
            OP("dv", lambda e: e.tensor_mul(nsc[:, 0:n], nsc[:, 0:n], nsc2[:, 0:n]),
               rd=[buf("nsc"), buf("nsc2")], wr=[buf("nsc")], tag=f"{tag}{it}d")
        OP("dv", lambda e: e.tensor_copy(out_ap, nsc[:, 0:n]),
           rd=[buf("nsc")], wr=[out_buf], tag=f"{tag}o")

    def col_to_row(col_buf, col_ap, p, p_buf, c0, tag):
        OP("dv", lambda e: e.tensor_copy(statc[:, :], col_ap),
           rd=[col_buf], wr=[buf("statc")], tag=f"{tag}c")
        OP("pe", lambda e: e.matmul(p[0:1, c0:c0 + 128], statc[:, :], ident_bf[:, :],
                                    start=True, stop=True),
           rd=[buf("statc"), buf("ident_bf")], wr=[p_buf], tag=f"{tag}T")

    def ln1_rows(c, ph):
        tag = f"p{ph}c{c}"
        rdyn, brdyn = rows_dyn[c % 2], buf(f"rows_mr{c % 2}")
        for tt in range(NTT):
            s = c * NTT + tt
            col_to_row(buf("l1r"), l1r[:, s:s + 1], pS1, buf("pS1"), tt * 128, f"{tag}r{tt}")
            col_to_row(buf("l1mr"), l1mr[:, s:s + 1], pS2, buf("pS2"), tt * 128, f"{tag}g{tt}")
            OP("pe", (lambda tt=tt: (lambda e: e.matmul(
                pS2[32:33, tt * 128:(tt + 1) * 128], statc[:, :], ident_bf[:, :],
                start=True, stop=True)))(),
               rd=[buf("statc"), buf("ident_bf")], wr=[buf("pS2")], tag=f"{tag}g32_{tt}")
        OP("dv", lambda e: e.tensor_copy(rrow[0:1, :], pS1[0:1, 0:C]),
           rd=[buf("pS1")], wr=[buf("rrow")], tag=f"{tag}rev")
        OP("dv", (lambda rdyn=rdyn: (lambda e: e.tensor_copy(rdyn[0:1, :], pS2[0:1, 0:C])))(),
           rd=[buf("pS2")], wr=[brdyn], tag=f"{tag}mrev")
        OP("dv", (lambda rdyn=rdyn: (lambda e: e.tensor_copy(rdyn[32:33, :], pS2[32:33, 0:C])))(),
           rd=[buf("pS2")], wr=[brdyn], tag=f"{tag}mrev32")
        OP("pe", lambda e: e.matmul(pS1[:, 0:C], ones_bf[0:1, 0:128], rrow[0:1, :], start=True, stop=True),
           rd=[buf("rrow"), buf("ones_bf")], wr=[buf("pS1")], tag=f"{tag}rbcM")
        OP("dv", lambda e: e.tensor_copy(rbc[:, :], pS1[:, 0:C]),
           rd=[buf("pS1")], wr=[buf("rbc")], tag=f"{tag}rbc")

    def make_xn(c, ph):
        tag = f"p{ph}c{c}"
        for ec in range(ECH):
            xb, bxb = xf32[ec % 2], buf(f"xf32_{ec % 2}")
            dma("sy", xb[:, :], x_fm_d.rearrange("(ec p) t -> p ec t", p=128)[:, ec, c * C:(c + 1) * C],
                [], [bxb], tag=f"{tag}xld{ec}")
            OP("dv", (lambda xb_=xb, ec_=ec: (lambda e: e.tensor_mul(
                xn[:, ec_ * C:(ec_ + 1) * C], xb_[:, :], rbc[:, :])))(),
               rd=[bxb, buf("rbc")], wr=[buf("xn")], tag=f"{tag}xr{ec}")

    def proj_fm(c, wt_buf, wt, pkn, out_s, out_buf, kind, tag):
        rdyn, brdyn = rows_dyn[c % 2], buf(f"rows_mr{c % 2}")
        bpk = [buf(f"pk_{pkn}_c"), buf(f"pk_{pkn}_b"), brdyn, buf(f"rows_ones{c % 2}")]
        for ic in range(ICH):
            p, pb = (pS1, buf("pS1")) if ic % 2 == 0 else (pS2, buf("pS2"))
            pkt, po = pk[pkn]
            def group(e, ic=ic, p=p, rdyn=rdyn, pkt=pkt, po=po):
                e.matmul(p[:, 0:C], pkt[po:po + 2, ic * 128:(ic + 1) * 128],
                         rdyn[po:po + 2, 0:C], start=True, stop=False)
                inst = None
                for ec in range(ECH):
                    inst = e.matmul(p[:, 0:C], wt[:, ec * E + ic * 128: ec * E + (ic + 1) * 128],
                                    xn[:, ec * C:(ec + 1) * C], start=False, stop=(ec == ECH - 1))
                return inst
            OP("pe", group, rd=[wt_buf, buf("xn")] + bpk, wr=[pb], tag=f"{tag}mm{ic}")
            if kind == "elu":
                OP("sc", (lambda p=p: (lambda e: e.activation(e_scr[:, 0:C], p[:, 0:C], AF.Exp)))(),
                   rd=[pb], wr=[buf("e_scr")], tag=f"{tag}ex{ic}")
                OP("dv", lambda e: e.tensor_scalar(t2_scr[:, 0:C], e_scr[:, 0:C], 1.0, 0.0,
                                                   ALU.subtract, ALU.min),
                   rd=[buf("e_scr")], wr=[buf("t2_scr")], tag=f"{tag}mn{ic}")
                OP("dv", (lambda p=p, ic=ic: (lambda e: e.scalar_tensor_tensor(
                    out_s[:, ic * C:(ic + 1) * C], p[:, 0:C], 0.0, t2_scr[:, 0:C], ALU.max, ALU.add)))(),
                   rd=[pb, buf("t2_scr")], wr=[out_buf], tag=f"{tag}el{ic}")
            else:
                OP("sc", (lambda p=p: (lambda e: e.activation(e_scr[:, 0:C], p[:, 0:C], AF.Tanh, scale=0.5)))(),
                   rd=[pb], wr=[buf("e_scr")], tag=f"{tag}th{ic}")
                OP("dv", (lambda p=p, ic=ic: (lambda e: e.tensor_scalar(
                    t2_scr[:, 0:C], p[:, 0:C], sclh[:, ic:ic + 1], None, ALU.mult)))(),
                   rd=[pb, buf("sclh")], wr=[buf("t2_scr")], tag=f"{tag}w1{ic}")
                OP("dv", (lambda ic=ic: (lambda e: e.scalar_tensor_tensor(
                    out_s[:, ic * C:(ic + 1) * C], e_scr[:, 0:C], 1.0, t2_scr[:, 0:C], ALU.add, ALU.mult)))(),
                   rd=[buf("e_scr"), buf("t2_scr")], wr=[out_buf], tag=f"{tag}us{ic}")

    def proj_tm_tile(c, tt, wt_buf, wt, pkn, out_st, out_buf, kind, tag, pj):
        if True:
            s = c * NTT + tt
            rdyn, brdyn = rows_dyn[c % 2], buf(f"rows_mr{c % 2}")
            bpk = [buf(f"pk_{pkn}_c"), buf(f"pk_{pkn}_b"), brdyn, buf(f"rows_ones{c % 2}")]
            if pj % 2 == 0:
                halves = [(pBig[:, 0:512], buf("pBig")), (pBig[:, 512:1024], buf("pBig"))]
            else:
                halves = [(pS1[:, :], buf("pS1")), (pS2[:, :], buf("pS2"))]
            hbufs = list({id(b): b for _, b in halves}.values())
            pkt, po = pk[pkn]
            def group(e, tt=tt, halves=halves, rdyn=rdyn, pkt=pkt, po=po):
                for half in range(2):
                    e.matmul(halves[half][0], rdyn[po:po + 2, tt * 128:(tt + 1) * 128],
                             pkt[po:po + 2, half * 512:(half + 1) * 512],
                             start=True, stop=False)
                inst = None
                for ec in range(ECH):
                    for half in range(2):
                        inst = e.matmul(halves[half][0],
                                        xn[:, ec * C + tt * 128: ec * C + (tt + 1) * 128],
                                        wt[:, ec * E + half * 512: ec * E + (half + 1) * 512],
                                        start=False, stop=(ec == ECH - 1))
                return inst
            OP("pe", group, rd=[wt_buf, buf("xn")] + bpk, wr=hbufs,
               tag=f"{tag}mm{tt}")
            for hf in range(2):
                pap, pb = halves[hf]
                if kind == "elu":
                    OP("sc", (lambda pap=pap: (lambda e: e.activation(
                        e_scr[:, :], pap, AF.Exp)))(),
                       rd=[pb], wr=[buf("e_scr")], tag=f"{tag}ex{tt}_{hf}")
                    OP("dv", lambda e: e.tensor_scalar(t2_scr[:, :], e_scr[:, :], 1.0, 0.0,
                                                       ALU.subtract, ALU.min),
                       rd=[buf("e_scr")], wr=[buf("t2_scr")], tag=f"{tag}mn{tt}_{hf}")
                    OP("dv", (lambda s=s, hf=hf, pap=pap: (lambda e: e.scalar_tensor_tensor(
                        out_st[:, s * E + hf * 512: s * E + (hf + 1) * 512],
                        pap, 0.0, t2_scr[:, :], ALU.max, ALU.add)))(),
                       rd=[pb, buf("t2_scr")], wr=[out_buf], tag=f"{tag}el{tt}_{hf}")
                else:
                    OP("sc", (lambda pap=pap: (lambda e: e.activation(
                        e_scr[:, :], pap, AF.Tanh, scale=0.5)))(),
                       rd=[pb], wr=[buf("e_scr")], tag=f"{tag}th{tt}_{hf}")
                    OP("dv", (lambda s=s, hf=hf, pap=pap: (lambda e: e.scalar_tensor_tensor(
                        out_st[:, s * E + hf * 512: s * E + (hf + 1) * 512], e_scr[:, :], 1.0,
                        pap, ALU.add, ALU.mult)))(),
                       rd=[buf("e_scr"), pb], wr=[out_buf], tag=f"{tag}v{tt}_{hf}")

    def state_update1(s):
        def group(e, s=s):
            inst = None
            for h in range(H):
                inst = e.matmul(S_ps[:, h * D:(h + 1) * D],
                                ktm_st[:, s * E + h * D: s * E + (h + 1) * D],
                                vtm_st[:, s * E + h * D: s * E + (h + 1) * D],
                                start=(s == 0 and h in (0, 4)),
                                stop=(s == NSUB - 1 and h in (3, 7)))
            return inst
        OP("pe", group, rd=[buf("ktm_st"), buf("vtm_st")], wr=[buf("S_ps")], tag=f"p1st{s}")

    def state_update(s, ph):
        def group(e, s=s):
            inst = None
            for h in range(H):
                inst = e.matmul(S_ps[:, h * D:(h + 1) * D],
                                ktm_st[:, s * E + h * D: s * E + (h + 1) * D],
                                vtm_st[:, s * E + h * D: s * E + (h + 1) * D],
                                start=(h in (0, 4)), stop=(h in (3, 7)))
            return inst
        OP("pe", group, rd=[buf("ktm_st"), buf("vtm_st")], wr=[buf("S_ps")], tag=f"p{ph}st{s}")

    def state_accum(s, ph, reset):
        if reset:
            OP("dv", lambda e: e.tensor_copy(S_acc[:, :], S_ps[:, :]),
               rd=[buf("S_ps")], wr=[buf("S_acc")], tag=f"p{ph}sa{s}")
        else:
            OP("dv", lambda e: e.tensor_add(S_acc[:, :], S_acc[:, :], S_ps[:, :]),
               rd=[buf("S_ps"), buf("S_acc")], wr=[buf("S_acc")], tag=f"p{ph}sa{s}")

    def ln1_stats(c):
        tag = f"p1c{c}"
        for tt in range(NTT):
            s = c * NTT + tt
            xt, bxt = x_tile[tt % 2], buf(f"x_tile{tt % 2}")
            dma("sy", xt[:, :], x_tm_d.rearrange("(s p) e -> p s e", p=128)[:, s, :],
                [], [bxt], tag=f"{tag}xt{tt}")
            OP("dv", (lambda xt=xt: (lambda e: e.bn_stats(bn6[:, 0:6], xt[:, 0:512])))(),
               rd=[bxt], wr=[buf("bn6")], tag=f"{tag}bnA{tt}")
            OP("dv", (lambda xt=xt: (lambda e: e.bn_stats(bn6[:, 6:12], xt[:, 512:1024])))(),
               rd=[bxt], wr=[buf("bn6")], tag=f"{tag}bnB{tt}")
            OP("dv", lambda e: e.bn_aggr(bn2[:, :], bn6[:, :]),
               rd=[buf("bn6")], wr=[buf("bn2")], tag=f"{tag}ag{tt}")
            OP("dv", (lambda s=s: (lambda e: e.tensor_copy(l1m[:, s:s + 1], bn2[:, 0:1])))(),
               rd=[buf("bn2")], wr=[buf("l1m")], tag=f"{tag}mc{tt}")
            OP("dv", (lambda s=s: (lambda e: e.tensor_copy(l1v[:, s:s + 1], bn2[:, 1:2])))(),
               rd=[buf("bn2")], wr=[buf("l1v")], tag=f"{tag}vc{tt}")
        newton_rsqrt(buf("l1v"), l1v[:, c * NTT:(c + 1) * NTT], buf("l1r"),
                     l1r[:, c * NTT:(c + 1) * NTT], LN_EPS, f"{tag}rs")
        OP("dv", (lambda c=c: (lambda e: e.tensor_mul(
            l1mr[:, c * NTT:(c + 1) * NTT], l1m[:, c * NTT:(c + 1) * NTT],
            l1r[:, c * NTT:(c + 1) * NTT])))(),
           rd=[buf("l1m"), buf("l1r")], wr=[buf("l1mr")], tag=f"{tag}mr")

    for ix in range(4 * ECH):
        ec, qr = ix // 4, ix % 4
        dma("sy", wf32[:, :],
            w_d["wf_t"].rearrange("(ec p) o -> p ec o", p=128)[:, ec, qr * 256:(qr + 1) * 256],
            [], [buf("wf32")], tag=f"wfl{ix}")
        OP("dv", (lambda ix=ix: (lambda e: e.reduce_max(wmax[:, ix:ix + 1], wf32[:, :],
                                                        axis=AX.X, apply_absolute_value=True)))(),
           rd=[buf("wf32")], wr=[buf("wmax")], tag=f"wfm{ix}")
    OP("dv", lambda e: e.reduce_max(wmax1[:, :], wmax[:, :], axis=AX.X),
       rd=[buf("wmax")], wr=[buf("wmax1")], tag="wfmc")
    OP("pe", lambda e: e.matmul(pT2[0:1, 0:128], wmax1[:, :], ident_f32[:, :], start=True, stop=True),
       rd=[buf("wmax1"), buf("ident_f32")], wr=[buf("pT2")], tag="wfT")
    OP("dv", lambda e: e.tensor_copy(wrow[:, :], pT2[0:1, 0:128]),
       rd=[buf("pT2")], wr=[buf("wrow")], tag="wfTe")
    OP("dv", lambda e: e.reduce_max(scalar_sb[:, 0:1], wrow[:, :], axis=AX.X),
       rd=[buf("wrow")], wr=[buf("scalar_sb")], tag="wfam")
    OP("dv", lambda e: e.tensor_scalar(scalar_sb[:, 1:2], scalar_sb[:, 0:1], 1e-12, None, ALU.max),
       rd=[buf("scalar_sb")], wr=[buf("scalar_sb")], tag="swh1")
    OP("dv", lambda e: e.reciprocal(scalar_sb[:, 1:2], scalar_sb[:, 1:2]),
       rd=[buf("scalar_sb")], wr=[buf("scalar_sb")], tag="swh2")
    OP("dv", lambda e: e.tensor_scalar(scalar_sb[:, 1:2], scalar_sb[:, 1:2], FP8_MAX_HALF, None, ALU.mult),
       rd=[buf("scalar_sb")], wr=[buf("scalar_sb")], tag="swh3")
    OP("pe", lambda e: e.matmul(pT2[:, 128:129], ones11[:, :], scalar_sb[0:1, 1:2], start=True, stop=True),
       rd=[buf("scalar_sb"), buf("ones11")], wr=[buf("pT2")], tag="swhB")
    OP("dv", lambda e: e.tensor_copy(swh_col[:, :], pT2[:, 128:129]),
       rd=[buf("pT2")], wr=[buf("swh_col")], tag="swhE")
    for ix in range(4 * ECH):
        ec, qr = ix // 4, ix % 4
        dma("sy", wf32[:, :],
            w_d["wf_t"].rearrange("(ec p) o -> p ec o", p=128)[:, ec, qr * 256:(qr + 1) * 256],
            [], [buf("wf32")], tag=f"wfl2_{ix}")
        OP("sc", (lambda ec=ec, qr=qr: (lambda e: e.activation(
            wf8[:, ec * E + qr * 256: ec * E + (qr + 1) * 256], wf32[:, :],
            AF.Identity, bias=0.0, scale=swh_col[:, :])))(),
           rd=[buf("wf32"), buf("swh_col")], wr=[buf("wf8")], tag=f"wfq{ix}")

    wload("wk", wA, "wA", w_d["wk_t"], "ldwk")
    ln1_stats(0)
    ln1_rows(0, 1)
    make_xn(0, 1)
    emit_pk("k", wA, buf("wA"))
    wload("wv", wB, "wB", w_d["wv_t"], "ldwv")
    wload("wo", wC, "wC", w_d["wo_t"], "ldwo")
    for c in range(NCH):
        pj = 0
        for tt in range(NTT):
            proj_tm_tile(c, tt, buf("wA"), wA, "k", ktm_st,
                         buf("ktm_st"), "elu", f"p1c{c}k", pj)
            pj += 1
            if c == 0 and tt == 0:
                emit_pk("v", wB, buf("wB"))
            proj_tm_tile(c, tt, buf("wB"), wB, "v", vtm_st,
                         buf("vtm_st"), "silu2x", f"p1c{c}v", pj)
            pj += 1
            if c + 1 < NCH:
                if tt == 0:
                    ln1_stats(c + 1)
                elif tt == 1:
                    ln1_rows(c + 1, 1)
                elif tt == 3:
                    make_xn(c + 1, 1)
        for tt in range(NTT):
            s = c * NTT + tt
            state_update1(s)
    OP("dv", lambda e: e.tensor_copy(S_acc[:, :], S_ps[:, :]),
       rd=[buf("S_ps")], wr=[buf("S_acc")], tag="p1sfin")

    wload("wq", wA, "wA", w_d["wq_t"], "ldwq", replaces="wA")
    wload("wu", wB, "wB", w_d["wu_t"], "ldwu", replaces="wB")
    emit_pk("q", wA, buf("wA"))
    emit_pk("u", wB, buf("wB"))

    OP("dv", lambda e: e.tensor_scalar(ssb[0][:, :], S_acc[:, :], emask[:, :], None, ALU.mult),
       rd=[buf("S_acc"), buf("emask")], wr=[buf("ssb0")], tag="ccmask")
    dma("gp", ccs_in_d[:, :], ssb[0][:, :], [buf("ssb0")], [buf("ccs_in")], tag="ccst")
    OP("gp", lambda e: e.collective_compute(
        "AllReduce", ALU.add, replica_groups=[[i, i + 1] for i in range(0, n_cores, 2)],
        ins=[ccs_in_d.ap().opt()], outs=[ccs_out_d.ap().opt()]),
        rd=[buf("ccs_in")], wr=[buf("ccs_out")], skey=("cc", False), tag="cc1")
    dma("gp", ssb[0][:, :], ccs_out_d[:, :], [buf("ccs_out")], [buf("ssb0")], tag="ccld")
    OP("dv", lambda e: e.tensor_scalar(sinit[:, :], ssb[0][:, :], omask[:, :], None, ALU.mult),
       rd=[buf("ssb0"), buf("omask")], wr=[buf("sinit")], tag="sinit")

    ln1_rows(0, 2)
    make_xn(0, 2)
    for c in range(NCH):
        tg = f"p2c{c}"
        proj_fm(c, buf("wA"), wA, "q", q_s, buf("q_s"), "elu", f"{tg}q")
        proj_fm(c, buf("wB"), wB, "u", us_s, buf("us_s"), "silu", f"{tg}u")
        for tt in range(NTT):
            s = c * NTT + tt
            kb, bkb = kfm_sub[0], buf("kfm0")
            for half in range(2):
                def ktr(e, s=s, half=half):
                    inst = None
                    for j in range(4):
                        h = half * 4 + j
                        inst = e.matmul(pT1[:, j * 128:(j + 1) * 128],
                                        ktm_st[:, s * E + h * D: s * E + (h + 1) * D],
                                        ident_bf[:, :], start=True, stop=True)
                    return inst
                OP("pe", ktr, rd=[buf("ktm_st"), buf("ident_bf")], wr=[buf("pT1")],
                   tag=f"{tg}kT{tt}_{half}")
                OP("sc", (lambda kb=kb, half=half: (lambda e: e.activation(
                    kb[:, half * 512:(half + 1) * 512], pT1[:, 0:512], AF.Identity)))(),
                   rd=[buf("pT1")], wr=[bkb], tag=f"{tg}kE{tt}_{half}")
            sb_t, bsb = ssb[0], buf("ssb0")
            if s > 0:
                state_accum(s - 1, 2, reset=(s == 1))
                OP("sc", (lambda sb_t=sb_t: (lambda e: e.activation(sb_t[:, :], S_acc[:, :],
                                                                    AF.Identity)))(),
                   rd=[buf("S_acc")], wr=[bsb], tag=f"{tg}sc{tt}")
            def eTg(e, s=s, tt=tt, kb=kb):
                inst = None
                for h in range(H):
                    inst = e.matmul(pBig[:, h * 128:(h + 1) * 128],
                                    kb[:, h * 128:(h + 1) * 128],
                                    q_s[:, h * C + tt * 128: h * C + (tt + 1) * 128],
                                    start=True, stop=True)
                return inst
            OP("pe", eTg, rd=[bkb, buf("q_s")], wr=[buf("pBig")], tag=f"{tg}eT{tt}")
            eb, beb = eT_s[0], buf("eT0")
            OP("dv", (lambda eb=eb: (lambda e: e.tensor_tensor(
                eb[:, :].rearrange("p (h n) -> p h n", h=H),
                pBig[:, :].rearrange("p (h n) -> p h n", h=H),
                triu_f[:, :].rearrange("p (o n) -> p o n", o=1).broadcast_to([128, H, 128]),
                ALU.mult)))(),
               rd=[buf("pBig"), buf("triu_f")], wr=[beb], tag=f"{tg}msk{tt}")
            def outg(e, s=s, tt=tt, eb=eb, sb_t=sb_t):
                inst = None
                for h in range(H):
                    inst = e.matmul(pBig[:, h * 128:(h + 1) * 128],
                                    vtm_st[:, s * E + h * D: s * E + (h + 1) * D],
                                    eb[:, h * 128:(h + 1) * 128], start=True, stop=False)
                    if s > 0:
                        inst = e.matmul(pBig[:, h * 128:(h + 1) * 128],
                                        sb_t[:, h * D:(h + 1) * D],
                                        q_s[:, h * C + tt * 128: h * C + (tt + 1) * 128],
                                        start=False, stop=False)
                    inst = e.matmul(pBig[:, h * 128:(h + 1) * 128],
                                    sinit[:, h * D:(h + 1) * D],
                                    q_s[:, h * C + tt * 128: h * C + (tt + 1) * 128],
                                    start=False, stop=True)
                return inst
            rd_o = [beb, buf("vtm_st"), buf("q_s"), buf("sinit")] + ([bsb] if s > 0 else [])
            OP("pe", outg, rd=rd_o, wr=[buf("pBig")], tag=f"{tg}out{tt}")
            if s < NSUB - 1:
                state_update(s, 2)
            OP("dv", (lambda tt=tt: (lambda e: e.tensor_tensor(
                g1_s[tt][:, :].rearrange("p (h n) -> p h n", h=H),
                pBig[:, :].rearrange("p (h n) -> p h n", h=H),
                us_s[:, :].rearrange("p (h cc) -> p h cc", h=H)[:, :, tt * 128:(tt + 1) * 128],
                ALU.mult)))(),
               rd=[buf("pBig"), buf("us_s")], wr=[buf(f"g1_{tt}")], tag=f"{tg}g1{tt}")
            OP("sc", lambda e: e.activation(eT_s[0][:, :], pBig[:, :], AF.Square),
               rd=[buf("pBig")], wr=[buf("eT0")], tag=f"{tg}sq{tt}")
            def ssqg(e, tt=tt):
                inst = None
                for h in range(H):
                    inst = e.matmul(pT2[0:1, tt * 128:(tt + 1) * 128], onescol[:, :],
                                    eT_s[0][:, h * 128:(h + 1) * 128],
                                    start=(h == 0), stop=(h == H - 1))
                return inst
            OP("pe", ssqg, rd=[buf("eT0"), buf("onescol")], wr=[buf("pT2")], tag=f"{tg}ssq{tt}")
            if tt == 0 and c + 1 < NCH:
                ln1_rows(c + 1, 2)
                make_xn(c + 1, 2)
        OP("dv", lambda e: e.tensor_copy(e_scr[0:1, 0:C], pT2[0:1, 0:C]),
           rd=[buf("pT2")], wr=[buf("e_scr")], tag=f"{tg}sse")
        def ssqT(e):
            inst = None
            for tt in range(NTT):
                inst = e.matmul(pT1[:, 256 + tt:257 + tt],
                                e_scr[0:1, tt * 128:(tt + 1) * 128],
                                ones11[0:1, 0:1], start=True, stop=True)
            return inst
        OP("pe", ssqT, rd=[buf("e_scr"), buf("ones11")], wr=[buf("pT1")], tag=f"{tg}ssT")
        OP("dv", lambda e: e.tensor_copy(nsc2[:, 0:NTT], pT1[:, 256:256 + NTT]),
           rd=[buf("pT1")], wr=[buf("nsc2")], tag=f"{tg}ssTe")
        OP("dv", lambda e: e.tensor_copy(rcp_t[:, :], nsc2[:, 0:NTT]),
           rd=[buf("nsc2")], wr=[buf("rcp_t")], tag=f"{tg}sscp")
        newton_rsqrt(buf("rcp_t"), rcp_t[:, :], buf("nsc2"), nsc2[:, 0:NTT], 0.0, f"{tg}nr")
        OP("dv", lambda e: e.tensor_mul(rcp_t[:, :], rcp_t[:, :], nsc2[:, 0:NTT]),
           rd=[buf("rcp_t"), buf("nsc2")], wr=[buf("rcp_t")], tag=f"{tg}rm1")
        OP("dv", lambda e: e.tensor_scalar(rcp_t[:, :], rcp_t[:, :], 1.0 / 64.0, RMS_EPS,
                                           ALU.mult, ALU.add),
           rd=[buf("rcp_t")], wr=[buf("rcp_t")], tag=f"{tg}rm2")
        OP("dv", lambda e: e.tensor_copy(rmse_bf[:, :], rcp_t[:, :]),
           rd=[buf("rcp_t")], wr=[buf("rmse_bf")], tag=f"{tg}rmb")
        OP("dv", lambda e: e.reciprocal(rcp_t[:, :], rcp_t[:, :]),
           rd=[buf("rcp_t")], wr=[buf("rcp_t")], tag=f"{tg}rcp")
        for tt in range(NTT):
            OP("pe", (lambda tt=tt: (lambda e: e.matmul(
                pT1[0:1, tt * 128:(tt + 1) * 128], rmse_bf[:, tt:tt + 1], ident_bf[:, :],
                start=True, stop=True)))(),
               rd=[buf("rmse_bf"), buf("ident_bf")], wr=[buf("pT1")], tag=f"{tg}rT{tt}")
        OP("dv", lambda e: e.tensor_copy(rrow[0:1, :], pT1[0:1, 0:C]),
           rd=[buf("pT1")], wr=[buf("rrow")], tag=f"{tg}rTe")
        for tt in range(NTT):
            s = c * NTT + tt
            xt, bxt = x_tile[tt % 2], buf(f"x_tile{tt % 2}")
            dma("sy", xt[:, :], x_tm_d.rearrange("(s p) e -> p s e", p=128)[:, s, :],
                [], [bxt], tag=f"{tg}x2{tt}")
            def wog(e, tt=tt):
                e.matmul(pBig[:, 0:512], rrow[0:1, tt * 128:(tt + 1) * 128],
                         bo_row[0:1, 0:512], start=True, stop=False)
                e.matmul(pBig[:, 512:1024], rrow[0:1, tt * 128:(tt + 1) * 128],
                         bo_row[0:1, 512:1024], start=True, stop=False)
                inst = None
                for h in range(H):
                    for half in range(2):
                        inst = e.matmul(pBig[:, half * 512:(half + 1) * 512],
                                        g1_s[tt][:, h * 128:(h + 1) * 128],
                                        wC[:, h * E + half * 512: h * E + (half + 1) * 512],
                                        start=False, stop=(h == H - 1))
                return inst
            OP("pe", wog, rd=[buf(f"g1_{tt}"), buf("wC"), buf("rrow"), buf("bo_row")],
               wr=[buf("pBig")], tag=f"{tg}wo{tt}")
            r2, br2 = res2_s[tt % 2], buf(f"res2_{tt % 2}")
            OP("dv", (lambda tt=tt, xt=xt, r2=r2, s=s: (lambda e: e.scalar_tensor_tensor(
                r2[:, :], pBig[:, :], rcp_t[:, tt:tt + 1], xt[:, :], ALU.mult, ALU.add,
                accum_out=l2sum[:, s:s + 1])))(),
               rd=[buf("pBig"), buf("rcp_t"), bxt], wr=[br2, buf("l2sum")], tag=f"{tg}r2{tt}")
            OP("sc", (lambda r2=r2, s=s: (lambda e: e.activation(
                eT_s[0][:, :], r2[:, :], AF.Square, accum_out=l2sq[:, s:s + 1])))(),
               rd=[br2], wr=[buf("eT0"), buf("l2sq")], tag=f"{tg}l2q{tt}")
            OP("dv", (lambda r2=r2, s=s: (lambda e: e.reduce_max(
                l2max[:, s:s + 1], r2[:, :], axis=AX.X)))(),
               rd=[br2], wr=[buf("l2max")], tag=f"{tg}l2x{tt}")
            OP("dv", (lambda r2=r2, s=s: (lambda e: e.tensor_reduce(
                l2min[:, s:s + 1], r2[:, :], AX.X, ALU.min)))(),
               rd=[br2], wr=[buf("l2min")], tag=f"{tg}l2n{tt}")
            dma("sc", res2_d.rearrange("(s p) e -> p s e", p=128)[:, s, :], r2[:, :],
                [br2], [buf("res2_d")], tag=f"{tg}rs{tt}")

    OP("dv", lambda e: e.tensor_scalar(l2m[:, :], l2sum[:, :], 1.0 / float(E), None, ALU.mult),
       rd=[buf("l2sum")], wr=[buf("l2m")], tag="l2m")
    OP("dv", lambda e: e.tensor_mul(l2r[:, :], l2m[:, :], l2m[:, :]),
       rd=[buf("l2m")], wr=[buf("l2r")], tag="l2ms")
    OP("dv", lambda e: e.scalar_tensor_tensor(l2r[:, :], l2sq[:, :], 1.0 / float(E), l2r[:, :],
                                              ALU.mult, ALU.subtract),
       rd=[buf("l2sq"), buf("l2r")], wr=[buf("l2r")], tag="l2v")
    newton_rsqrt(buf("l2r"), l2r[:, :], buf("l2r"), l2r[:, :], LN_EPS, "l2rs")
    OP("dv", lambda e: e.tensor_sub(l2max[:, :], l2max[:, :], l2m[:, :]),
       rd=[buf("l2max"), buf("l2m")], wr=[buf("l2max")], tag="am1")
    OP("dv", lambda e: e.tensor_sub(l2min[:, :], l2m[:, :], l2min[:, :]),
       rd=[buf("l2min"), buf("l2m")], wr=[buf("l2min")], tag="am2")
    OP("dv", lambda e: e.tensor_max(l2max[:, :], l2max[:, :], l2min[:, :]),
       rd=[buf("l2max"), buf("l2min")], wr=[buf("l2max")], tag="am3")
    OP("dv", lambda e: e.tensor_mul(l2max[:, :], l2max[:, :], l2r[:, :]),
       rd=[buf("l2max"), buf("l2r")], wr=[buf("l2max")], tag="am4")
    OP("dv", lambda e: e.reduce_max(amrow[:, :], l2max[:, :], axis=AX.X),
       rd=[buf("l2max")], wr=[buf("amrow")], tag="am5")
    OP("pe", lambda e: e.matmul(pT2[0:1, 256:384], amrow[:, :], ident_f32[:, :], start=True, stop=True),
       rd=[buf("amrow"), buf("ident_f32")], wr=[buf("pT2")], tag="amT")
    OP("dv", lambda e: e.tensor_copy(wrow[:, :], pT2[0:1, 256:384]),
       rd=[buf("pT2")], wr=[buf("wrow")], tag="amTe")
    OP("dv", lambda e: e.reduce_max(scalar_sb[:, 2:3], wrow[:, :], axis=AX.X),
       rd=[buf("wrow")], wr=[buf("scalar_sb")], tag="aml")
    OP("dv", lambda e: e.tensor_copy(wrow[0:1, :], scalar_sb[0:1, 2:3].broadcast_to([1, 128])),
       rd=[buf("scalar_sb")], wr=[buf("wrow")], tag="amb")
    dma("gp", ccm_in_d[:, :], wrow[0:1, :], [buf("wrow")], [buf("ccm_in")], tag="ccmst")
    OP("gp", lambda e: e.collective_compute(
        "AllReduce", ALU.max, replica_groups=[list(range(n_cores))],
        ins=[ccm_in_d.ap().opt()], outs=[ccm_out_d.ap().opt()]),
        rd=[buf("ccm_in")], wr=[buf("ccm_out")], skey=("cc", False), tag="cc2")
    dma("gp", scalar_sb[0:1, 3:4], ccm_out_d[0:1, 0:1], [buf("ccm_out")], [buf("scalar_sb")], tag="ccmld")
    OP("dv", lambda e: e.tensor_scalar(scalar_sb[:, 4:5], scalar_sb[:, 3:4], 1e-12, None, ALU.max),
       rd=[buf("scalar_sb")], wr=[buf("scalar_sb")], tag="sx1")
    OP("dv", lambda e: e.reciprocal(scalar_sb[:, 4:5], scalar_sb[:, 4:5]),
       rd=[buf("scalar_sb")], wr=[buf("scalar_sb")], tag="sx2")
    OP("dv", lambda e: e.tensor_scalar(scalar_sb[:, 4:5], scalar_sb[:, 4:5], FP8_MAX_HALF, None, ALU.mult),
       rd=[buf("scalar_sb")], wr=[buf("scalar_sb")], tag="sx3")
    OP("dv", lambda e: e.tensor_mul(scalar_sb[:, 5:6], scalar_sb[:, 4:5], scalar_sb[:, 1:2]),
       rd=[buf("scalar_sb")], wr=[buf("scalar_sb")], tag="ssw")
    OP("dv", lambda e: e.tensor_scalar(bf_row[0:1, :], bf_row[0:1, :], scalar_sb[0:1, 5:6], None, ALU.mult),
       rd=[buf("scalar_sb"), buf("bf_row")], wr=[buf("bf_row")], tag="bfs")
    OP("pe", lambda e: e.matmul(pT2[:, 129:130], ones11[:, :], scalar_sb[0:1, 4:5], start=True, stop=True),
       rd=[buf("scalar_sb"), buf("ones11")], wr=[buf("pT2")], tag="sxB")
    OP("dv", lambda e: e.tensor_copy(sxcol[:, :], pT2[:, 129:130]),
       rd=[buf("pT2")], wr=[buf("sxcol")], tag="sxE")
    OP("pe", lambda e: e.matmul(pT2[:, 130:131], ones11[:, :], scalar_sb[0:1, 5:6], start=True, stop=True),
       rd=[buf("scalar_sb"), buf("ones11")], wr=[buf("pT2")], tag="dB")
    OP("dv", lambda e: e.tensor_copy(d_col[:, :], pT2[:, 130:131]),
       rd=[buf("pT2")], wr=[buf("d_col")], tag="dE")
    OP("dv", lambda e: e.reciprocal(d_col[:, :], d_col[:, :]),
       rd=[buf("d_col")], wr=[buf("d_col")], tag="dR")
    OP("dv", lambda e: e.tensor_scalar(qa[:, :], l2r[:, :], sxcol[:, :], None, ALU.mult),
       rd=[buf("l2r"), buf("sxcol")], wr=[buf("qa")], tag="qa")
    OP("dv", lambda e: e.scalar_tensor_tensor(qb[:, :], l2m[:, :], -1.0, qa[:, :], ALU.mult, ALU.mult),
       rd=[buf("l2m"), buf("qa")], wr=[buf("qb")], tag="qb")

    st_ops = []
    for s in range(NSUB):
        ri, bri = resin[s % 2], buf(f"x_tile{s % 2}")
        dma("sy", ri[:, :], res2_d.rearrange("(s p) e -> p s e", p=128)[:, s, :],
            [buf("res2_d")], [bri], tag=f"f3l{s}")
        xq, bxq = xq_s[s % 2], buf(f"xq{s % 2}")
        OP("sc", (lambda ri=ri, xq=xq, s=s: (lambda e: e.activation(
            xq[:, :], ri[:, :], AF.Identity, bias=qb[:, s:s + 1], scale=qa[:, s:s + 1])))(),
           rd=[bri, buf("qa"), buf("qb")], wr=[bxq], tag=f"f3q{s}")
        xqf, bxqf = xqf_s[0], buf("xqf0")
        for half in range(2):
            p, pb = (pT1, buf("pT1")) if half == 0 else (pT2, buf("pT2"))
            def xtr(e, xq=xq, half=half, p=p):
                inst = None
                for j in range(4):
                    ec = half * 4 + j
                    inst = e.matmul(p[:, j * 128:(j + 1) * 128],
                                    xq[:, ec * 128:(ec + 1) * 128], ident_bf[:, :],
                                    start=True, stop=True)
                return inst
            OP("pe", xtr, rd=[bxq, buf("ident_bf")], wr=[pb], tag=f"f3T{s}_{half}")
            OP("sc", (lambda xqf=xqf, half=half, p=p: (lambda e: e.activation(
                xqf[:, half * 512:(half + 1) * 512], p[:, 0:512], AF.Identity)))(),
               rd=[pb], wr=[bxqf], tag=f"f3E{s}_{half}")
        def ffn(e, xqf=xqf):
            e.matmul(S_ps[:, 0:512], ones_bf[0:1, 0:128], bf_row[0:1, 0:512], start=True, stop=False)
            e.matmul(S_ps[:, 512:1024], ones_bf[0:1, 0:128], bf_row[0:1, 512:1024], start=True, stop=False)
            inst = None
            for ec in range(ECH):
                for half in range(2):
                    inst = e.matmul(S_ps[:, half * 512:(half + 1) * 512],
                                    xqf[:, ec * 128:(ec + 1) * 128],
                                    wf8[:, ec * E + half * 512: ec * E + (half + 1) * 512],
                                    start=False, stop=(ec == ECH - 1))
            return inst
        OP("pe", ffn, rd=[bxqf, buf("wf8"), buf("bf_row"), buf("ones_bf")], wr=[buf("S_ps")],
           tag=f"f3m{s}")
        yb, byb = res2_s[s % 2], buf(f"res2_{s % 2}")
        OP("dv", (lambda ri=ri, yb=yb: (lambda e: e.scalar_tensor_tensor(
            yb[:, :], S_ps[:, :], d_col[:, :], ri[:, :], ALU.mult, ALU.add)))(),
           rd=[buf("S_ps"), buf("d_col"), bri], wr=[byb], tag=f"f3y{s}")
        st_ops.append(dma("sc", y_d.rearrange("(s p) e -> p s e", p=128)[:, s, :], yb[:, :],
                          [byb], [buf("y_d")], tag=f"f3s{s}"))

    OP("sy", None, deps=st_ops, tag="final")

    if max_ops is not None:
        keep = sch.ops[:max_ops]
        kept = set(id(o) for o in keep)
        sch.ops = keep
        for o in sch.ops:
            o.deps = {d for d in o.deps if id(d) in kept}
    sch.finalize()
    with ExitStack() as sem_es:
        sems = {}
        for key, name in [(("pe", False), "s_pe"), (("dv", False), "s_dv"),
                          (("sc", False), "s_ac"), (("gp", False), "s_gp"),
                          (("sy", False), "s_sy"), (("cc", False), "s_cc")]:
            sems[key] = sem_es.enter_context(nc.semaphore(name))
        for q in ("sy", "sc", "gp"):
            for j in range(Sched.NDMASEM):
                sems[(q, "dma", j)] = sem_es.enter_context(nc.semaphore(f"d_{q}{j}"))
        with nc.Block() as block:
            @block.sync
            def _(h):
                sch.emit("sy", h, sems)

            @block.tensor
            def _(h):
                sch.emit("pe", h, sems)

            @block.vector
            def _(h):
                sch.emit("dv", h, sems)

            @block.scalar
            def _(h):
                sch.emit("sc", h, sems)

            @block.gpsimd
            def _(h):
                sch.emit("gp", h, sems)
    es.close()
    return nc


def make_inputs(core_id, x_full, W, T):
    b, half = core_id // 2, core_id % 2
    xs = np.ascontiguousarray(x_full[b, half * T:(half + 1) * T, :], dtype=np.float32)
    d = {
        "x_tm": xs,
        "x_fm": np.ascontiguousarray(xs.T),
        "wq_t": np.ascontiguousarray(W["Wq"].T),
        "wk_t": np.ascontiguousarray(W["Wk"].T),
        "wv_t": np.ascontiguousarray(W["Wv"].T),
        "wu_t": np.ascontiguousarray(W["Wu"].T),
        "wo_t": np.ascontiguousarray(W["Wo"].T),
        "wf_t": np.ascontiguousarray(W["Wf"].T),
        "bq_row": np.ascontiguousarray(W["bq"].reshape(1, E)),
        "bk_row": np.ascontiguousarray(W["bk"].reshape(1, E)),
        "bv_row": np.ascontiguousarray(W["bv"].reshape(1, E)),
        "bu_row": np.ascontiguousarray(W["bu"].reshape(1, E)),
        "bo_row": np.ascontiguousarray(W["bo"].reshape(1, E)),
        "bf_row": np.ascontiguousarray(W["bf"].reshape(1, E)),
        "ones_row": np.ones((1, 512), dtype=np.float32),
        "scale_pt": np.ascontiguousarray(W["scale"].reshape(H, 128).T),
        "ident": np.eye(128, dtype=np.float32),
        "triu": np.triu(np.ones((128, 128), dtype=np.float32)),
        "onescol": np.ones((128, 1), dtype=np.float32),
        "ones11": np.ones((1, 128), dtype=np.float32),
        "emask": np.full((128, 1), 1.0 - (core_id % 2), dtype=np.float32),
        "omask": np.full((128, 1), float(core_id % 2), dtype=np.float32),
    }
    return d


def kernel(L0_attn_x, L0_attn_Wq, L0_attn_bq, L0_attn_Wk, L0_attn_bk,
           L0_attn_Wv, L0_attn_bv, L0_attn_Wu, L0_attn_bu, L0_attn_Wo,
           L0_attn_bo, L0_attn_scale, L0_ffn_weight, L0_ffn_bias):
    x = np.asarray(L0_attn_x, dtype=np.float32)
    Bn, N, _ = x.shape
    T = N // 2
    W = {
        "Wq": np.asarray(L0_attn_Wq, np.float32), "bq": np.asarray(L0_attn_bq, np.float32),
        "Wk": np.asarray(L0_attn_Wk, np.float32), "bk": np.asarray(L0_attn_bk, np.float32),
        "Wv": np.asarray(L0_attn_Wv, np.float32), "bv": np.asarray(L0_attn_bv, np.float32),
        "Wu": np.asarray(L0_attn_Wu, np.float32), "bu": np.asarray(L0_attn_bu, np.float32),
        "Wo": np.asarray(L0_attn_Wo, np.float32), "bo": np.asarray(L0_attn_bo, np.float32),
        "scale": np.asarray(L0_attn_scale, np.float32),
        "Wf": np.asarray(L0_ffn_weight, np.float32), "bf": np.asarray(L0_ffn_bias, np.float32),
    }
    n_cores = 2 * Bn
    nc = build_program(T=T, C=min(512, T), n_cores=n_cores)
    in_maps = [make_inputs(cid, x, W, T) for cid in range(n_cores)]
    res = run_bass_kernel_spmd(nc, in_maps, core_ids=list(range(n_cores)))
    y = np.empty((Bn, N, E), dtype=np.float32)
    for cid in range(n_cores):
        b, half = cid // 2, cid % 2
        y[b, half * T:(half + 1) * T, :] = res.results[cid]["y"]
    return y


# version 14
# speedup vs baseline: 1.0289x; 1.0289x over previous
import numpy as np
import concourse.bass as bass
ABLATE = set()
import concourse.mybir as mybir
from concourse.bass_utils import run_bass_kernel_spmd
from contextlib import ExitStack

F32 = mybir.dt.float32
BF16 = mybir.dt.bfloat16
F8 = mybir.dt.float8e4
I32 = mybir.dt.int32
AF = mybir.ActivationFunctionType
ALU = mybir.AluOpType
AX = mybir.AxisListType

E = 1024
H = 8
D = 128
ECH = 8
ICH = 8
LN_EPS = 1e-5
RMS_EPS = 1e-8
FP8_MAX_HALF = 224.0
LN2C = float(np.log(2.0))


class _Op:
    __slots__ = ("eng", "fn", "deps", "dma", "skey", "needs_inc", "val", "tag", "rdn", "wrn")

    def __init__(self, eng, fn, deps, dma, skey, tag):
        self.eng, self.fn, self.deps, self.dma, self.skey, self.tag = eng, fn, deps, dma, skey, tag
        self.needs_inc = False
        self.val = None
        self.rdn = ()
        self.wrn = ()


class Buf:
    __slots__ = ("w", "r", "name", "psum")

    def __init__(self, name="", psum=False):
        self.w, self.r, self.name, self.psum = None, [], name, psum


class Sched:
    SERIAL = False
    SERIAL_ENGS = set()

    def __init__(self):
        self.ops = []

    def op(self, eng, fn, rd=(), wr=(), dma=False, deps=(), skey=None, tag=""):
        dep = set()
        for b in rd:
            if b.w is not None:
                dep.add(b.w)
            if b.psum:
                for r_ in b.r:
                    if r_.eng != eng:
                        dep.add(r_)
        for b in wr:
            if b.w is not None:
                dep.add(b.w)
            dep.update(b.r)
        for d in deps:
            if d is not None:
                dep.add(d)
        o = _Op(eng, fn, dep, dma, skey or (eng, dma), tag)
        o.rdn = tuple(b.name for b in rd)
        o.wrn = tuple(b.name for b in wr)
        self.ops.append(o)
        for b in rd:
            b.r.append(o)
        for b in wr:
            b.w = o
            b.r = []
        return o

    NDMASEM = 6
    RELAX_SAME_ENGINE = {("pe", False), ("sc", False)}

    def _skip(self, d, o):
        return (d.skey == o.skey and d.skey in self.RELAX_SAME_ENGINE
                and not d.dma and not o.dma)

    def finalize(self):
        if Sched.SERIAL:
            prev = None
            for o in self.ops:
                if o.fn is not None and prev is not None:
                    o.deps.add(prev)
                if o.fn is not None:
                    prev = o
        if Sched.SERIAL_ENGS:
            prev_any = None
            last_chained = None
            for o in self.ops:
                if o.fn is None:
                    continue
                if o.eng in Sched.SERIAL_ENGS:
                    if prev_any is not None:
                        o.deps.add(prev_any)
                    last_chained = o
                elif last_chained is not None:
                    o.deps.add(last_chained)
                prev_any = o
        rot = {}
        for o in self.ops:
            if o.dma:
                q = o.skey
                j = rot.get(q, 0)
                rot[q] = (j + 1) % self.NDMASEM
                o.skey = (q[0], "dma", j)
                o.needs_inc = True
        for o in self.ops:
            o.deps = {d for d in o.deps if not self._skip(d, o)}
        for o in self.ops:
            o.needs_inc = o.needs_inc and False or o.needs_inc
        for o in self.ops:
            o.needs_inc = bool(o.dma)
        for o in self.ops:
            for d in o.deps:
                d.needs_inc = True
        cnt = {}
        for o in self.ops:
            if o.needs_inc:
                cnt[o.skey] = cnt.get(o.skey, 0) + (16 if o.dma else 1)
                o.val = cnt[o.skey]

    def emit(self, eng_name, handle, sems):
        water = {}
        for o in self.ops:
            if o.eng != eng_name:
                continue
            waits = {}
            for d in o.deps:
                if d.val is None:
                    raise RuntimeError(f"dep without inc: {d.tag} <- {o.tag}")
                waits[d.skey] = max(waits.get(d.skey, 0), d.val)
            if o.dma and o.val >= 32:
                waits[o.skey] = max(waits.get(o.skey, 0), o.val - 16)
            for key in sorted(waits, key=str):
                v = waits[key]
                if water.get(key, 0) < v:
                    handle.wait_ge(sems[key], v)
                    water[key] = v
            if o.fn is None:
                continue
            inst = o.fn(handle)
            if o.needs_inc:
                inst.then_inc(sems[o.skey], 16 if o.dma else 1)


def build_program(T=2048, C=512, n_cores=8, max_ops=None):
    NCH = T // C
    NTT = C // 128
    NSUB = T // 128
    nc = bass.Bass("TRN2", target_bir_lowering=False, detect_race_conditions=False)

    x_tm_d = nc.dram_tensor("x_tm", [T, E], F32, kind="ExternalInput")
    x_fm_d = nc.dram_tensor("x_fm", [E, T], F32, kind="ExternalInput")
    w_d = {n: nc.dram_tensor(n, [E, E], F32, kind="ExternalInput")
           for n in ("wq_t", "wk_t", "wv_t", "wu_t", "wo_t", "wf_t")}
    bias_d = {n: nc.dram_tensor(n, [1, E], F32, kind="ExternalInput")
              for n in ("bq_row", "bk_row", "bv_row", "bu_row")}
    bo_row_d = nc.dram_tensor("bo_row", [1, E], F32, kind="ExternalInput")
    bf_row_d = nc.dram_tensor("bf_row", [1, E], F32, kind="ExternalInput")
    scale_d = nc.dram_tensor("scale_pt", [128, H], F32, kind="ExternalInput")
    ident_d = nc.dram_tensor("ident", [128, 128], F32, kind="ExternalInput")
    triu_d = nc.dram_tensor("triu", [128, 128], F32, kind="ExternalInput")
    ones_d = nc.dram_tensor("ones_row", [1, 512], F32, kind="ExternalInput")
    onescol_d = nc.dram_tensor("onescol", [128, 1], F32, kind="ExternalInput")
    ones11_d = nc.dram_tensor("ones11", [1, 128], F32, kind="ExternalInput")
    emask_d = nc.dram_tensor("emask", [128, 1], F32, kind="ExternalInput")
    omask_d = nc.dram_tensor("omask", [128, 1], F32, kind="ExternalInput")
    y_d = nc.dram_tensor("y", [T, E], F32, kind="ExternalOutput")

    res2_d = nc.dram_tensor("res2_scratch", [T, E], F32)
    wq_bf_d = nc.dram_tensor("wq_bf", [E, E], BF16)
    wu_bf_d = nc.dram_tensor("wu_bf", [E, E], BF16)
    ccs_in_d = nc.dram_tensor("ccs_in", [128, H * D], BF16)
    ccs_out_d = nc.dram_tensor("ccs_out", [128, H * D], BF16)
    ccm_in_d = nc.dram_tensor("ccm_in", [1, 128], F32)
    ccm_out_d = nc.dram_tensor("ccm_out", [1, 128], F32)

    es = ExitStack()
    sbt = lambda name, shape, dt: es.enter_context(nc.sbuf_tensor(name, shape, dt))
    pst = lambda name, shape: es.enter_context(nc.psum_tensor(name, shape, F32))

    ktm_st = sbt("ktm_st", [128, NSUB * E], BF16)
    vtm_st = sbt("vtm_st", [128, NSUB * E], BF16)
    wA = sbt("wA", [128, ECH * E], BF16)
    wB = sbt("wB", [128, ECH * E], BF16)
    wC = sbt("wC", [128, ECH * E], BF16)
    wf8 = sbt("wf8", [128, ECH * E], F8)
    wf32 = sbt("wf32", [128, 256], F32)
    pkA = sbt("pkA", [34, E], BF16)
    pkB = sbt("pkB", [34, E], BF16)
    pk = {"q": (pkA, 0), "k": (pkA, 32), "v": (pkB, 0), "u": (pkB, 32)}
    rows_dyn = [sbt(f"rows_dyn{i}", [34, 512], BF16) for i in range(2)]
    ones_bf = sbt("ones_bf", [1, 128], BF16)
    bo_row = sbt("bo_row_s", [1, E], BF16)
    bf_row = sbt("bf_row_s", [1, E], BF16)
    scale_pt = sbt("scale_pt_s", [128, H], F32)
    sclh = sbt("sclh", [128, H], F32)
    ident_bf = sbt("ident_bf", [128, 128], BF16)
    ident_f32 = sbt("ident_f32", [128, 128], F32)
    triu_f = sbt("triu_f", [128, 128], BF16)
    onescol = sbt("onescol_s", [128, 1], BF16)
    ones11 = sbt("ones11_s", [1, 128], F32)
    emask = sbt("emask_s", [128, 1], F32)
    omask = sbt("omask_s", [128, 1], F32)

    xf32 = [sbt(f"xf32_{i}", [128, 512], F32) for i in range(2)]
    xn = sbt("xn", [128, ECH * C], BF16)
    rbc = sbt("rbc", [128, C], BF16)
    x_tile = [sbt(f"x_tile{i}", [128, E], F32) for i in range(2)]
    q_s = sbt("q_s", [128, H * C], BF16)
    us_s = sbt("us_s", [128, H * C], BF16)
    e_scr = sbt("e_scr", [128, 512], F32)
    t2_scr = sbt("t2_scr", [128, 512], BF16)
    rrow = sbt("rrow", [1, C], BF16)
    statc = sbt("statc", [128, 1], BF16)

    kfm_sub = [sbt("kfm0", [128, E], BF16)]
    eT_s = [sbt("eT0", [128, H * 128], BF16)]
    S_acc = sbt("S_acc", [128, H * D], F32)
    ssb = [sbt("ssb0", [128, H * D], BF16)]
    sinit = sbt("sinit", [128, H * D], BF16)
    g1_s = [sbt(f"g1_{i}", [128, E], BF16) for i in range(NTT)]
    res2_s = [sbt(f"res2_{i}", [128, E], F32) for i in range(2)]
    resin = x_tile

    xq_s = [sbt(f"xq{i}", [128, E], F8) for i in range(2)]
    xqf_s = [sbt("xqf0", [128, E], F8)]

    bn6 = sbt("bn6", [128, 12], F32)
    bn2 = sbt("bn2", [128, 2], F32)
    l1m = sbt("l1m", [128, NSUB], F32)
    l1v = sbt("l1v", [128, NSUB], F32)
    l1r = sbt("l1r", [128, NSUB], F32)
    l1mr = sbt("l1mr", [128, NSUB], F32)
    l2sum = sbt("l2sum", [128, NSUB], F32)
    l2sq = sbt("l2sq", [128, NSUB], F32)
    l2max = sbt("l2max", [128, NSUB], F32)
    l2min = sbt("l2min", [128, NSUB], F32)
    l2m = sbt("l2m", [128, NSUB], F32)
    l2r = sbt("l2r", [128, NSUB], F32)
    qa = sbt("qa", [128, NSUB], F32)
    qb = sbt("qb", [128, NSUB], F32)
    nsc = sbt("nsc", [128, NSUB], F32)
    nsc2 = sbt("nsc2", [128, NSUB], F32)
    nsc3 = sbt("nsc3", [128, NSUB], F32)
    exp_in = nsc2
    rcp_t = sbt("rcp_t", [128, NTT], F32)
    rmse_bf = sbt("rmse_bf", [128, NTT], BF16)
    wmax = sbt("wmax", [128, 4 * ECH], F32)
    wmax1 = sbt("wmax1", [128, 1], F32)
    wrow = sbt("wrow", [1, 128], F32)
    scalar_sb = sbt("scalar_sb", [1, 8], F32)
    swh_col = sbt("swh_col", [128, 1], F32)
    sxcol = sbt("sxcol", [128, 1], F32)
    d_col = sbt("d_col", [128, 1], F32)
    amrow = sbt("amrow", [128, 1], F32)
    seedb = sbt("seedb", [128, 1], F32)

    S_ps = pst("S_ps", [128, 1024])
    pBig = pst("pBig", [128, 1024])
    pS1 = pst("pS1", [128, 512])
    pS2 = pst("pS2", [128, 512])
    pT1 = pst("pT1", [128, 512])
    pT2 = pst("pT2", [128, 512])

    B = {}
    PSUM_BUFS = {"S_ps", "pBig", "pS1", "pS2", "pT1", "pT2"}

    def buf(name):
        if name not in B:
            B[name] = Buf(name, psum=(name in PSUM_BUFS))
        return B[name]

    sch = Sched()
    OP = sch.op

    def dma(eng, out_ap, in_ap, rd, wr, deps=(), tag=""):
        return OP(eng, (lambda o, i: (lambda e: e.dma_start(out=o, in_=i)))(out_ap, in_ap),
                  rd=rd, wr=wr, dma=True, deps=deps, tag=tag)

    for n, key in (("q", "bq_row"), ("k", "bk_row"), ("v", "bv_row"), ("u", "bu_row")):
        t, po = pk[n]
        dma("gp", t[po + 1:po + 2, :], bias_d[key][:, :], [], [buf(f"pk_{n}_b")], tag=f"ldb{n}")
    for i in range(2):
        dma("gp", rows_dyn[i][1:2, :], ones_d[:, :], [], [buf(f"rows_ones{i}")], tag=f"ldro{i}")
        dma("gp", rows_dyn[i][33:34, :], ones_d[:, :], [], [buf(f"rows_ones{i}")], tag=f"ldro32_{i}")
    dma("gp", bo_row[:, :], bo_row_d[:, :], [], [buf("bo_row")], tag="ldbo")
    dma("gp", bf_row[:, :], bf_row_d[:, :], [], [buf("bf_row")], tag="ldbf")
    dma("gp", ident_bf[:, :], ident_d[:, :], [], [buf("ident_bf")], tag="ldidb")
    dma("gp", ones_bf[:, :], ones_d[0:1, 0:128], [], [buf("ones_bf")], tag="ldonesbf")
    dma("gp", onescol[:, :], onescol_d[:, :], [], [buf("onescol")], tag="ldoc")
    dma("sy", ident_f32[:, :], ident_d[:, :], [], [buf("ident_f32")], tag="ldidf")
    dma("gp", triu_f[:, :], triu_d[:, :], [], [buf("triu_f")], tag="ldtriu")
    dma("sy", scale_pt[:, :], scale_d[:, :], [], [buf("scale_pt")], tag="ldscale")
    dma("sy", ones11[:, :], ones11_d[:, :], [], [buf("ones11")], tag="ldo11")
    dma("sy", emask[:, :], emask_d[:, :], [], [buf("emask")], tag="ldem")
    dma("sy", omask[:, :], omask_d[:, :], [], [buf("omask")], tag="ldom")
    OP("dv", lambda e: e.tensor_scalar(sclh[:, :], scale_pt[:, :], 0.25, None, ALU.mult),
       rd=[buf("scale_pt")], wr=[buf("sclh")], tag="sclh")
    OP("dv", lambda e: e.memset(seedb[:, :], float(LN2C * 63.5)),
       rd=[], wr=[buf("seedb")], tag="seedb")

    def wload(wname, tgt, tgt_buf, src, tag, replaces=None):
        deps = []
        if replaces is not None:
            b = buf(replaces)
            deps = list(b.r) + ([b.w] if b.w else [])
        last = None
        for ec in range(ECH):
            last = dma("gp", tgt[:, :].rearrange("p (ec i) -> p ec i", ec=ECH)[:, ec, :],
                       src.rearrange("(ec p) i -> p ec i", p=128)[:, ec, :],
                       [], [buf(tgt_buf)], deps=deps, tag=f"{tag}_{ec}")
        return last

    def emit_pk(name, wt, wt_buf):
        t, po = pk[name]
        for half in range(2):
            p, pb = (pT1, buf("pT1")) if half == 0 else (pT2, buf("pT2"))
            def grp(e, half=half, p=p, wt=wt, po=po):
                inst = None
                for ec in range(ECH):
                    inst = e.matmul(p[po:po + 1, 0:512], onescol[:, :],
                                    wt[:, ec * E + half * 512: ec * E + (half + 1) * 512],
                                    start=(ec == 0), stop=(ec == ECH - 1))
                return inst
            OP("pe", grp, rd=[wt_buf, buf("onescol")], wr=[pb], tag=f"pk{name}{half}")
            OP("dv", (lambda t=t, po=po, half=half, p=p: (lambda e: e.tensor_scalar(
                t[po:po + 1, half * 512:(half + 1) * 512], p[po:po + 1, 0:512], -1.0, None, ALU.mult)))(),
               rd=[pb], wr=[buf(f"pk_{name}_c")], tag=f"pk{name}e{half}")



    def newton_rsqrt(src_buf, src_ap, out_buf, out_ap, eps_add, tag, iters=3):
        n = src_ap.shape[-1]
        OP("dv", lambda e: e.tensor_scalar(nsc3[:, 0:n], src_ap, float(eps_add), None, ALU.add),
           rd=[src_buf], wr=[buf("nsc3")], tag=f"{tag}e")
        OP("dv", lambda e: e.tensor_copy(exp_in[:, 0:n], nsc3[:, 0:n].bitcast(I32)),
           rd=[buf("nsc3")], wr=[buf("nsc2")], tag=f"{tag}b")
        OP("sc", lambda e: e.activation(nsc[:, 0:n], exp_in[:, 0:n], AF.Exp,
                                        bias=seedb[:, :],
                                        scale=float(-LN2C * 0.5 / (2 ** 23))),
           rd=[buf("nsc2"), buf("seedb")], wr=[buf("nsc")], tag=f"{tag}s")
        for it in range(iters):
            OP("dv", lambda e: e.tensor_mul(nsc2[:, 0:n], nsc[:, 0:n], nsc[:, 0:n]),
               rd=[buf("nsc")], wr=[buf("nsc2")], tag=f"{tag}{it}a")
            OP("dv", lambda e: e.tensor_mul(nsc2[:, 0:n], nsc2[:, 0:n], nsc3[:, 0:n]),
               rd=[buf("nsc2"), buf("nsc3")], wr=[buf("nsc2")], tag=f"{tag}{it}b")
            OP("dv", lambda e: e.tensor_scalar(nsc2[:, 0:n], nsc2[:, 0:n], -0.5, 1.5, ALU.mult, ALU.add),
               rd=[buf("nsc2")], wr=[buf("nsc2")], tag=f"{tag}{it}c")
            OP("dv", lambda e: e.tensor_mul(nsc[:, 0:n], nsc[:, 0:n], nsc2[:, 0:n]),
               rd=[buf("nsc"), buf("nsc2")], wr=[buf("nsc")], tag=f"{tag}{it}d")
        OP("dv", lambda e: e.tensor_copy(out_ap, nsc[:, 0:n]),
           rd=[buf("nsc")], wr=[out_buf], tag=f"{tag}o")

    def col_to_row(col_buf, col_ap, p, p_buf, c0, tag):
        OP("dv", lambda e: e.tensor_copy(statc[:, :], col_ap),
           rd=[col_buf], wr=[buf("statc")], tag=f"{tag}c")
        OP("pe", lambda e: e.matmul(p[0:1, c0:c0 + 128], statc[:, :], ident_bf[:, :],
                                    start=True, stop=True),
           rd=[buf("statc"), buf("ident_bf")], wr=[p_buf], tag=f"{tag}T")

    def ln1_rows(c, ph):
        tag = f"p{ph}c{c}"
        rdyn, brdyn = rows_dyn[c % 2], buf(f"rows_mr{c % 2}")
        for tt in range(NTT):
            s = c * NTT + tt
            col_to_row(buf("l1r"), l1r[:, s:s + 1], pS1, buf("pS1"), tt * 128, f"{tag}r{tt}")
            col_to_row(buf("l1mr"), l1mr[:, s:s + 1], pS2, buf("pS2"), tt * 128, f"{tag}g{tt}")
            OP("pe", (lambda tt=tt: (lambda e: e.matmul(
                pS2[32:33, tt * 128:(tt + 1) * 128], statc[:, :], ident_bf[:, :],
                start=True, stop=True)))(),
               rd=[buf("statc"), buf("ident_bf")], wr=[buf("pS2")], tag=f"{tag}g32_{tt}")
        OP("dv", lambda e: e.tensor_copy(rrow[0:1, :], pS1[0:1, 0:C]),
           rd=[buf("pS1")], wr=[buf("rrow")], tag=f"{tag}rev")
        OP("dv", (lambda rdyn=rdyn: (lambda e: e.tensor_copy(rdyn[0:1, :], pS2[0:1, 0:C])))(),
           rd=[buf("pS2")], wr=[brdyn], tag=f"{tag}mrev")
        OP("dv", (lambda rdyn=rdyn: (lambda e: e.tensor_copy(rdyn[32:33, :], pS2[32:33, 0:C])))(),
           rd=[buf("pS2")], wr=[brdyn], tag=f"{tag}mrev32")
        OP("pe", lambda e: e.matmul(pS1[:, 0:C], ones_bf[0:1, 0:128], rrow[0:1, :], start=True, stop=True),
           rd=[buf("rrow"), buf("ones_bf")], wr=[buf("pS1")], tag=f"{tag}rbcM")
        OP("dv", lambda e: e.tensor_copy(rbc[:, :], pS1[:, 0:C]),
           rd=[buf("pS1")], wr=[buf("rbc")], tag=f"{tag}rbc")

    def make_xn(c, ph):
        tag = f"p{ph}c{c}"
        for ec in range(ECH):
            xb, bxb = xf32[ec % 2], buf(f"xf32_{ec % 2}")
            dma("sy", xb[:, :], x_fm_d.rearrange("(ec p) t -> p ec t", p=128)[:, ec, c * C:(c + 1) * C],
                [], [bxb], tag=f"{tag}xld{ec}")
            OP("dv", (lambda xb_=xb, ec_=ec: (lambda e: e.tensor_mul(
                xn[:, ec_ * C:(ec_ + 1) * C], xb_[:, :], rbc[:, :])))(),
               rd=[bxb, buf("rbc")], wr=[buf("xn")], tag=f"{tag}xr{ec}")

    def proj_fm(c, wt_buf, wt, pkn, out_s, out_buf, kind, tag):
        rdyn, brdyn = rows_dyn[c % 2], buf(f"rows_mr{c % 2}")
        bpk = [buf(f"pk_{pkn}_c"), buf(f"pk_{pkn}_b"), brdyn, buf(f"rows_ones{c % 2}")]
        for ic in range(ICH):
            p, pb = (pS1, buf("pS1")) if ic % 2 == 0 else (pS2, buf("pS2"))
            pkt, po = pk[pkn]
            def group(e, ic=ic, p=p, rdyn=rdyn, pkt=pkt, po=po):
                e.matmul(p[:, 0:C], pkt[po:po + 2, ic * 128:(ic + 1) * 128],
                         rdyn[po:po + 2, 0:C], start=True, stop=False)
                inst = None
                for ec in range(ECH):
                    inst = e.matmul(p[:, 0:C], wt[:, ec * E + ic * 128: ec * E + (ic + 1) * 128],
                                    xn[:, ec * C:(ec + 1) * C], start=False, stop=(ec == ECH - 1))
                return inst
            OP("pe", group, rd=[wt_buf, buf("xn")] + bpk, wr=[pb], tag=f"{tag}mm{ic}")
            if kind == "elu":
                OP("sc", (lambda p=p: (lambda e: e.activation(e_scr[:, 0:C], p[:, 0:C], AF.Exp)))(),
                   rd=[pb], wr=[buf("e_scr")], tag=f"{tag}ex{ic}")
                OP("dv", lambda e: e.tensor_scalar(t2_scr[:, 0:C], e_scr[:, 0:C], 1.0, 0.0,
                                                   ALU.subtract, ALU.min),
                   rd=[buf("e_scr")], wr=[buf("t2_scr")], tag=f"{tag}mn{ic}")
                OP("dv", (lambda p=p, ic=ic: (lambda e: e.scalar_tensor_tensor(
                    out_s[:, ic * C:(ic + 1) * C], p[:, 0:C], 0.0, t2_scr[:, 0:C], ALU.max, ALU.add)))(),
                   rd=[pb, buf("t2_scr")], wr=[out_buf], tag=f"{tag}el{ic}")
            else:
                OP("sc", (lambda p=p: (lambda e: e.activation(e_scr[:, 0:C], p[:, 0:C], AF.Tanh, scale=0.5)))(),
                   rd=[pb], wr=[buf("e_scr")], tag=f"{tag}th{ic}")
                OP("dv", (lambda p=p, ic=ic: (lambda e: e.tensor_scalar(
                    t2_scr[:, 0:C], p[:, 0:C], sclh[:, ic:ic + 1], None, ALU.mult)))(),
                   rd=[pb, buf("sclh")], wr=[buf("t2_scr")], tag=f"{tag}w1{ic}")
                OP("dv", (lambda ic=ic: (lambda e: e.scalar_tensor_tensor(
                    out_s[:, ic * C:(ic + 1) * C], e_scr[:, 0:C], 1.0, t2_scr[:, 0:C], ALU.add, ALU.mult)))(),
                   rd=[buf("e_scr"), buf("t2_scr")], wr=[out_buf], tag=f"{tag}us{ic}")

    def proj_tm_tile(c, tt, wt_buf, wt, pkn, out_st, out_buf, kind, tag, pj):
        if True:
            s = c * NTT + tt
            rdyn, brdyn = rows_dyn[c % 2], buf(f"rows_mr{c % 2}")
            bpk = [buf(f"pk_{pkn}_c"), buf(f"pk_{pkn}_b"), brdyn, buf(f"rows_ones{c % 2}")]
            if pj % 2 == 0:
                halves = [(pBig[:, 0:512], buf("pBig")), (pBig[:, 512:1024], buf("pBig"))]
            else:
                halves = [(pS1[:, :], buf("pS1")), (pS2[:, :], buf("pS2"))]
            hbufs = list({id(b): b for _, b in halves}.values())
            pkt, po = pk[pkn]
            def group(e, tt=tt, halves=halves, rdyn=rdyn, pkt=pkt, po=po):
                for half in range(2):
                    e.matmul(halves[half][0], rdyn[po:po + 2, tt * 128:(tt + 1) * 128],
                             pkt[po:po + 2, half * 512:(half + 1) * 512],
                             start=True, stop=False)
                inst = None
                for ec in range(ECH):
                    for half in range(2):
                        inst = e.matmul(halves[half][0],
                                        xn[:, ec * C + tt * 128: ec * C + (tt + 1) * 128],
                                        wt[:, ec * E + half * 512: ec * E + (half + 1) * 512],
                                        start=False, stop=(ec == ECH - 1))
                return inst
            OP("pe", group, rd=[wt_buf, buf("xn")] + bpk, wr=hbufs,
               tag=f"{tag}mm{tt}")
            for hf in range(2):
                pap, pb = halves[hf]
                if kind == "elu":
                    OP("sc", (lambda pap=pap: (lambda e: e.activation(
                        e_scr[:, :], pap, AF.Exp)))(),
                       rd=[pb], wr=[buf("e_scr")], tag=f"{tag}ex{tt}_{hf}")
                    OP("dv", lambda e: e.tensor_scalar(t2_scr[:, :], e_scr[:, :], 1.0, 0.0,
                                                       ALU.subtract, ALU.min),
                       rd=[buf("e_scr")], wr=[buf("t2_scr")], tag=f"{tag}mn{tt}_{hf}")
                    OP("dv", (lambda s=s, hf=hf, pap=pap: (lambda e: e.scalar_tensor_tensor(
                        out_st[:, s * E + hf * 512: s * E + (hf + 1) * 512],
                        pap, 0.0, t2_scr[:, :], ALU.max, ALU.add)))(),
                       rd=[pb, buf("t2_scr")], wr=[out_buf], tag=f"{tag}el{tt}_{hf}")
                else:
                    OP("sc", (lambda pap=pap: (lambda e: e.activation(
                        e_scr[:, :], pap, AF.Tanh, scale=0.5)))(),
                       rd=[pb], wr=[buf("e_scr")], tag=f"{tag}th{tt}_{hf}")
                    OP("dv", (lambda s=s, hf=hf, pap=pap: (lambda e: e.scalar_tensor_tensor(
                        out_st[:, s * E + hf * 512: s * E + (hf + 1) * 512], e_scr[:, :], 1.0,
                        pap, ALU.add, ALU.mult)))(),
                       rd=[buf("e_scr"), pb], wr=[out_buf], tag=f"{tag}v{tt}_{hf}")

    def state_update1(s):
        def group(e, s=s):
            inst = None
            for h in range(H):
                inst = e.matmul(S_ps[:, h * D:(h + 1) * D],
                                ktm_st[:, s * E + h * D: s * E + (h + 1) * D],
                                vtm_st[:, s * E + h * D: s * E + (h + 1) * D],
                                start=(s == 0 and h in (0, 4)),
                                stop=(s == NSUB - 1 and h in (3, 7)))
            return inst
        OP("pe", group, rd=[buf("ktm_st"), buf("vtm_st")], wr=[buf("S_ps")], tag=f"p1st{s}")

    def state_update(s, ph):
        def group(e, s=s):
            inst = None
            for h in range(H):
                inst = e.matmul(S_ps[:, h * D:(h + 1) * D],
                                ktm_st[:, s * E + h * D: s * E + (h + 1) * D],
                                vtm_st[:, s * E + h * D: s * E + (h + 1) * D],
                                start=(h in (0, 4)), stop=(h in (3, 7)))
            return inst
        OP("pe", group, rd=[buf("ktm_st"), buf("vtm_st")], wr=[buf("S_ps")], tag=f"p{ph}st{s}")

    def state_accum(s, ph, reset):
        if reset:
            OP("dv", lambda e: e.tensor_copy(S_acc[:, :], S_ps[:, :]),
               rd=[buf("S_ps")], wr=[buf("S_acc")], tag=f"p{ph}sa{s}")
        else:
            OP("dv", lambda e: e.tensor_add(S_acc[:, :], S_acc[:, :], S_ps[:, :]),
               rd=[buf("S_ps"), buf("S_acc")], wr=[buf("S_acc")], tag=f"p{ph}sa{s}")

    def ln1_stats(c):
        tag = f"p1c{c}"
        for tt in range(NTT):
            s = c * NTT + tt
            xt, bxt = x_tile[tt % 2], buf(f"x_tile{tt % 2}")
            dma("sy", xt[:, :], x_tm_d.rearrange("(s p) e -> p s e", p=128)[:, s, :],
                [], [bxt], tag=f"{tag}xt{tt}")
            OP("dv", (lambda xt=xt: (lambda e: e.bn_stats(bn6[:, 0:6], xt[:, 0:512])))(),
               rd=[bxt], wr=[buf("bn6")], tag=f"{tag}bnA{tt}")
            OP("dv", (lambda xt=xt: (lambda e: e.bn_stats(bn6[:, 6:12], xt[:, 512:1024])))(),
               rd=[bxt], wr=[buf("bn6")], tag=f"{tag}bnB{tt}")
            OP("dv", lambda e: e.bn_aggr(bn2[:, :], bn6[:, :]),
               rd=[buf("bn6")], wr=[buf("bn2")], tag=f"{tag}ag{tt}")
            OP("dv", (lambda s=s: (lambda e: e.tensor_copy(l1m[:, s:s + 1], bn2[:, 0:1])))(),
               rd=[buf("bn2")], wr=[buf("l1m")], tag=f"{tag}mc{tt}")
            OP("dv", (lambda s=s: (lambda e: e.tensor_copy(l1v[:, s:s + 1], bn2[:, 1:2])))(),
               rd=[buf("bn2")], wr=[buf("l1v")], tag=f"{tag}vc{tt}")
        newton_rsqrt(buf("l1v"), l1v[:, c * NTT:(c + 1) * NTT], buf("l1r"),
                     l1r[:, c * NTT:(c + 1) * NTT], LN_EPS, f"{tag}rs")
        OP("dv", (lambda c=c: (lambda e: e.tensor_mul(
            l1mr[:, c * NTT:(c + 1) * NTT], l1m[:, c * NTT:(c + 1) * NTT],
            l1r[:, c * NTT:(c + 1) * NTT])))(),
           rd=[buf("l1m"), buf("l1r")], wr=[buf("l1mr")], tag=f"{tag}mr")

    def emit_wf_amax(lo, hi):
        for ix in range(lo, hi):
            ec, qr = ix // 4, ix % 4
            dma("sy", wf32[:, :],
                w_d["wf_t"].rearrange("(ec p) o -> p ec o", p=128)[:, ec, qr * 256:(qr + 1) * 256],
                [], [buf("wf32")], tag=f"wfl{ix}")
            OP("dv", (lambda ix=ix: (lambda e: e.reduce_max(wmax[:, ix:ix + 1], wf32[:, :],
                                                            axis=AX.X, apply_absolute_value=True)))(),
               rd=[buf("wf32")], wr=[buf("wmax")], tag=f"wfm{ix}")

    def emit_wf_swh():
        OP("dv", lambda e: e.reduce_max(wmax1[:, :], wmax[:, :], axis=AX.X),
           rd=[buf("wmax")], wr=[buf("wmax1")], tag="wfmc")
        OP("pe", lambda e: e.matmul(pT2[0:1, 0:128], wmax1[:, :], ident_f32[:, :], start=True, stop=True),
           rd=[buf("wmax1"), buf("ident_f32")], wr=[buf("pT2")], tag="wfT")
        OP("dv", lambda e: e.tensor_copy(wrow[:, :], pT2[0:1, 0:128]),
           rd=[buf("pT2")], wr=[buf("wrow")], tag="wfTe")
        OP("dv", lambda e: e.reduce_max(scalar_sb[:, 0:1], wrow[:, :], axis=AX.X),
           rd=[buf("wrow")], wr=[buf("scalar_sb")], tag="wfam")
        OP("dv", lambda e: e.tensor_scalar(scalar_sb[:, 1:2], scalar_sb[:, 0:1], 1e-12, None, ALU.max),
           rd=[buf("scalar_sb")], wr=[buf("scalar_sb")], tag="swh1")
        OP("dv", lambda e: e.reciprocal(scalar_sb[:, 1:2], scalar_sb[:, 1:2]),
           rd=[buf("scalar_sb")], wr=[buf("scalar_sb")], tag="swh2")
        OP("dv", lambda e: e.tensor_scalar(scalar_sb[:, 1:2], scalar_sb[:, 1:2], FP8_MAX_HALF, None, ALU.mult),
           rd=[buf("scalar_sb")], wr=[buf("scalar_sb")], tag="swh3")
        OP("pe", lambda e: e.matmul(pT2[:, 128:129], ones11[:, :], scalar_sb[0:1, 1:2], start=True, stop=True),
           rd=[buf("scalar_sb"), buf("ones11")], wr=[buf("pT2")], tag="swhB")
        OP("dv", lambda e: e.tensor_copy(swh_col[:, :], pT2[:, 128:129]),
           rd=[buf("pT2")], wr=[buf("swh_col")], tag="swhE")

    def emit_wf_quant(lo, hi):
        for ix in range(lo, hi):
            ec, qr = ix // 4, ix % 4
            dma("sy", wf32[:, :],
                w_d["wf_t"].rearrange("(ec p) o -> p ec o", p=128)[:, ec, qr * 256:(qr + 1) * 256],
                [], [buf("wf32")], tag=f"wfl2_{ix}")
            OP("sc", (lambda ec=ec, qr=qr: (lambda e: e.activation(
                wf8[:, ec * E + qr * 256: ec * E + (qr + 1) * 256], wf32[:, :],
                AF.Identity, bias=0.0, scale=swh_col[:, :])))(),
               rd=[buf("wf32"), buf("swh_col")], wr=[buf("wf8")], tag=f"wfq{ix}")

    wload("wk", wA, "wA", w_d["wk_t"], "ldwk")
    ln1_stats(0)
    ln1_rows(0, 1)
    make_xn(0, 1)
    emit_pk("k", wA, buf("wA"))
    wload("wv", wB, "wB", w_d["wv_t"], "ldwv")
    wload("wo", wC, "wC", w_d["wo_t"], "ldwo")
    for ec in range(ECH):
        dma("gp", wq_bf_d.rearrange("(ec p) i -> p ec i", p=128)[:, ec, :],
            w_d["wq_t"].rearrange("(ec p) i -> p ec i", p=128)[:, ec, :],
            [], [buf("wq_bf_d")], tag=f"pcq{ec}")
        dma("gp", wu_bf_d.rearrange("(ec p) i -> p ec i", p=128)[:, ec, :],
            w_d["wu_t"].rearrange("(ec p) i -> p ec i", p=128)[:, ec, :],
            [], [buf("wu_bf_d")], tag=f"pcu{ec}")
    for c in range(NCH):
        pj = 0
        for tt in range(NTT):
            proj_tm_tile(c, tt, buf("wA"), wA, "k", ktm_st,
                         buf("ktm_st"), "elu", f"p1c{c}k", pj)
            pj += 1
            if c == 0 and tt == 0:
                emit_pk("v", wB, buf("wB"))
            proj_tm_tile(c, tt, buf("wB"), wB, "v", vtm_st,
                         buf("vtm_st"), "silu2x", f"p1c{c}v", pj)
            pj += 1
            if c + 1 < NCH:
                if tt == 0:
                    ln1_stats(c + 1)
                elif tt == 1:
                    ln1_rows(c + 1, 1)
                elif tt == 3:
                    make_xn(c + 1, 1)
        for tt in range(NTT):
            s = c * NTT + tt
            state_update1(s)
        for task in range(c * 4 // NCH, (c + 1) * 4 // NCH):
            if task == 0:
                emit_wf_amax(0, 2 * ECH)
            elif task == 1:
                emit_wf_amax(2 * ECH, 4 * ECH)
                emit_wf_swh()
            elif task == 2:
                emit_wf_quant(0, 2 * ECH)
            elif task == 3:
                emit_wf_quant(2 * ECH, 4 * ECH)
    OP("dv", lambda e: e.tensor_copy(S_acc[:, :], S_ps[:, :]),
       rd=[buf("S_ps")], wr=[buf("S_acc")], tag="p1sfin")

    def wload_bf(tgt, tgt_buf, src_d, tag, replaces):
        b = buf(replaces)
        deps = list(b.r) + ([b.w] if b.w else [])
        src_buf = {id(wq_bf_d): "wq_bf_d", id(wu_bf_d): "wu_bf_d"}[id(src_d)]
        bb = buf(src_buf)
        deps += [bb.w] + list(bb.r)
        for ec in range(ECH):
            dma("sy", tgt[:, :].rearrange("p (ec i) -> p ec i", ec=ECH)[:, ec, :],
                src_d.rearrange("(ec p) i -> p ec i", p=128)[:, ec, :],
                [], [buf(tgt_buf)], deps=deps, tag=f"{tag}_{ec}")
    wload_bf(wA, "wA", wq_bf_d, "ldwq", replaces="wA")
    wload_bf(wB, "wB", wu_bf_d, "ldwu", replaces="wB")
    emit_pk("q", wA, buf("wA"))
    emit_pk("u", wB, buf("wB"))

    OP("dv", lambda e: e.tensor_scalar(ssb[0][:, :], S_acc[:, :], emask[:, :], None, ALU.mult),
       rd=[buf("S_acc"), buf("emask")], wr=[buf("ssb0")], tag="ccmask")
    dma("gp", ccs_in_d[:, :], ssb[0][:, :], [buf("ssb0")], [buf("ccs_in")], tag="ccst")
    OP("gp", lambda e: e.collective_compute(
        "AllReduce", ALU.add, replica_groups=[[i, i + 1] for i in range(0, n_cores, 2)],
        ins=[ccs_in_d.ap().opt()], outs=[ccs_out_d.ap().opt()]),
        rd=[buf("ccs_in")], wr=[buf("ccs_out")], skey=("cc", False), tag="cc1")
    dma("gp", ssb[0][:, :], ccs_out_d[:, :], [buf("ccs_out")], [buf("ssb0")], tag="ccld")
    OP("dv", lambda e: e.tensor_scalar(sinit[:, :], ssb[0][:, :], omask[:, :], None, ALU.mult),
       rd=[buf("ssb0"), buf("omask")], wr=[buf("sinit")], tag="sinit")

    ln1_rows(0, 2)
    make_xn(0, 2)
    for c in range(NCH):
        tg = f"p2c{c}"
        proj_fm(c, buf("wA"), wA, "q", q_s, buf("q_s"), "elu", f"{tg}q")
        proj_fm(c, buf("wB"), wB, "u", us_s, buf("us_s"), "silu", f"{tg}u")
        for tt in range(NTT):
            s = c * NTT + tt
            kb, bkb = kfm_sub[0], buf("kfm0")
            for half in range(2):
                def ktr(e, s=s, half=half):
                    inst = None
                    for j in range(4):
                        h = half * 4 + j
                        inst = e.matmul(pT1[:, j * 128:(j + 1) * 128],
                                        ktm_st[:, s * E + h * D: s * E + (h + 1) * D],
                                        ident_bf[:, :], start=True, stop=True)
                    return inst
                OP("pe", ktr, rd=[buf("ktm_st"), buf("ident_bf")], wr=[buf("pT1")],
                   tag=f"{tg}kT{tt}_{half}")
                OP("sc", (lambda kb=kb, half=half: (lambda e: e.activation(
                    kb[:, half * 512:(half + 1) * 512], pT1[:, 0:512], AF.Identity)))(),
                   rd=[buf("pT1")], wr=[bkb], tag=f"{tg}kE{tt}_{half}")
            sb_t, bsb = ssb[0], buf("ssb0")
            if s > 0:
                state_accum(s - 1, 2, reset=(s == 1))
                OP("sc", (lambda sb_t=sb_t: (lambda e: e.activation(sb_t[:, :], S_acc[:, :],
                                                                    AF.Identity)))(),
                   rd=[buf("S_acc")], wr=[bsb], tag=f"{tg}sc{tt}")
            def eTg(e, s=s, tt=tt, kb=kb):
                inst = None
                for h in range(H):
                    inst = e.matmul(pBig[:, h * 128:(h + 1) * 128],
                                    kb[:, h * 128:(h + 1) * 128],
                                    q_s[:, h * C + tt * 128: h * C + (tt + 1) * 128],
                                    start=True, stop=True)
                return inst
            OP("pe", eTg, rd=[bkb, buf("q_s")], wr=[buf("pBig")], tag=f"{tg}eT{tt}")
            eb, beb = eT_s[0], buf("eT0")
            OP("dv", (lambda eb=eb: (lambda e: e.tensor_tensor(
                eb[:, :].rearrange("p (h n) -> p h n", h=H),
                pBig[:, :].rearrange("p (h n) -> p h n", h=H),
                triu_f[:, :].rearrange("p (o n) -> p o n", o=1).broadcast_to([128, H, 128]),
                ALU.mult)))(),
               rd=[buf("pBig"), buf("triu_f")], wr=[beb], tag=f"{tg}msk{tt}")
            def outg(e, s=s, tt=tt, eb=eb, sb_t=sb_t):
                inst = None
                for h in range(H):
                    inst = e.matmul(pBig[:, h * 128:(h + 1) * 128],
                                    vtm_st[:, s * E + h * D: s * E + (h + 1) * D],
                                    eb[:, h * 128:(h + 1) * 128], start=True, stop=False)
                    if s > 0:
                        inst = e.matmul(pBig[:, h * 128:(h + 1) * 128],
                                        sb_t[:, h * D:(h + 1) * D],
                                        q_s[:, h * C + tt * 128: h * C + (tt + 1) * 128],
                                        start=False, stop=False)
                    inst = e.matmul(pBig[:, h * 128:(h + 1) * 128],
                                    sinit[:, h * D:(h + 1) * D],
                                    q_s[:, h * C + tt * 128: h * C + (tt + 1) * 128],
                                    start=False, stop=True)
                return inst
            rd_o = [beb, buf("vtm_st"), buf("q_s"), buf("sinit")] + ([bsb] if s > 0 else [])
            OP("pe", outg, rd=rd_o, wr=[buf("pBig")], tag=f"{tg}out{tt}")
            if s < NSUB - 1:
                state_update(s, 2)
            OP("dv", (lambda tt=tt: (lambda e: e.tensor_tensor(
                g1_s[tt][:, :].rearrange("p (h n) -> p h n", h=H),
                pBig[:, :].rearrange("p (h n) -> p h n", h=H),
                us_s[:, :].rearrange("p (h cc) -> p h cc", h=H)[:, :, tt * 128:(tt + 1) * 128],
                ALU.mult)))(),
               rd=[buf("pBig"), buf("us_s")], wr=[buf(f"g1_{tt}")], tag=f"{tg}g1{tt}")
            OP("sc", lambda e: e.activation(eT_s[0][:, :], pBig[:, :], AF.Square),
               rd=[buf("pBig")], wr=[buf("eT0")], tag=f"{tg}sq{tt}")
            def ssqg(e, tt=tt):
                inst = None
                for h in range(H):
                    inst = e.matmul(pT2[0:1, tt * 128:(tt + 1) * 128], onescol[:, :],
                                    eT_s[0][:, h * 128:(h + 1) * 128],
                                    start=(h == 0), stop=(h == H - 1))
                return inst
            OP("pe", ssqg, rd=[buf("eT0"), buf("onescol")], wr=[buf("pT2")], tag=f"{tg}ssq{tt}")
            if tt == 0 and c + 1 < NCH:
                ln1_rows(c + 1, 2)
                make_xn(c + 1, 2)
        OP("dv", lambda e: e.tensor_copy(e_scr[0:1, 0:C], pT2[0:1, 0:C]),
           rd=[buf("pT2")], wr=[buf("e_scr")], tag=f"{tg}sse")
        def ssqT(e):
            inst = None
            for tt in range(NTT):
                inst = e.matmul(pT1[:, 256 + tt:257 + tt],
                                e_scr[0:1, tt * 128:(tt + 1) * 128],
                                ones11[0:1, 0:1], start=True, stop=True)
            return inst
        OP("pe", ssqT, rd=[buf("e_scr"), buf("ones11")], wr=[buf("pT1")], tag=f"{tg}ssT")
        OP("dv", lambda e: e.tensor_copy(nsc2[:, 0:NTT], pT1[:, 256:256 + NTT]),
           rd=[buf("pT1")], wr=[buf("nsc2")], tag=f"{tg}ssTe")
        OP("dv", lambda e: e.tensor_copy(rcp_t[:, :], nsc2[:, 0:NTT]),
           rd=[buf("nsc2")], wr=[buf("rcp_t")], tag=f"{tg}sscp")
        newton_rsqrt(buf("rcp_t"), rcp_t[:, :], buf("nsc2"), nsc2[:, 0:NTT], 0.0, f"{tg}nr")
        OP("dv", lambda e: e.tensor_mul(rcp_t[:, :], rcp_t[:, :], nsc2[:, 0:NTT]),
           rd=[buf("rcp_t"), buf("nsc2")], wr=[buf("rcp_t")], tag=f"{tg}rm1")
        OP("dv", lambda e: e.tensor_scalar(rcp_t[:, :], rcp_t[:, :], 1.0 / 64.0, RMS_EPS,
                                           ALU.mult, ALU.add),
           rd=[buf("rcp_t")], wr=[buf("rcp_t")], tag=f"{tg}rm2")
        OP("dv", lambda e: e.tensor_copy(rmse_bf[:, :], rcp_t[:, :]),
           rd=[buf("rcp_t")], wr=[buf("rmse_bf")], tag=f"{tg}rmb")
        OP("dv", lambda e: e.reciprocal(rcp_t[:, :], rcp_t[:, :]),
           rd=[buf("rcp_t")], wr=[buf("rcp_t")], tag=f"{tg}rcp")
        for tt in range(NTT):
            OP("pe", (lambda tt=tt: (lambda e: e.matmul(
                pT1[0:1, tt * 128:(tt + 1) * 128], rmse_bf[:, tt:tt + 1], ident_bf[:, :],
                start=True, stop=True)))(),
               rd=[buf("rmse_bf"), buf("ident_bf")], wr=[buf("pT1")], tag=f"{tg}rT{tt}")
        OP("dv", lambda e: e.tensor_copy(rrow[0:1, :], pT1[0:1, 0:C]),
           rd=[buf("pT1")], wr=[buf("rrow")], tag=f"{tg}rTe")
        for tt in range(NTT):
            s = c * NTT + tt
            xt, bxt = x_tile[tt % 2], buf(f"x_tile{tt % 2}")
            dma("sy", xt[:, :], x_tm_d.rearrange("(s p) e -> p s e", p=128)[:, s, :],
                [], [bxt], tag=f"{tg}x2{tt}")
            def wog(e, tt=tt):
                e.matmul(pBig[:, 0:512], rrow[0:1, tt * 128:(tt + 1) * 128],
                         bo_row[0:1, 0:512], start=True, stop=False)
                e.matmul(pBig[:, 512:1024], rrow[0:1, tt * 128:(tt + 1) * 128],
                         bo_row[0:1, 512:1024], start=True, stop=False)
                inst = None
                for h in range(H):
                    for half in range(2):
                        inst = e.matmul(pBig[:, half * 512:(half + 1) * 512],
                                        g1_s[tt][:, h * 128:(h + 1) * 128],
                                        wC[:, h * E + half * 512: h * E + (half + 1) * 512],
                                        start=False, stop=(h == H - 1))
                return inst
            OP("pe", wog, rd=[buf(f"g1_{tt}"), buf("wC"), buf("rrow"), buf("bo_row")],
               wr=[buf("pBig")], tag=f"{tg}wo{tt}")
            r2, br2 = res2_s[tt % 2], buf(f"res2_{tt % 2}")
            OP("dv", (lambda tt=tt, xt=xt, r2=r2, s=s: (lambda e: e.scalar_tensor_tensor(
                r2[:, :], pBig[:, :], rcp_t[:, tt:tt + 1], xt[:, :], ALU.mult, ALU.add,
                accum_out=l2sum[:, s:s + 1])))(),
               rd=[buf("pBig"), buf("rcp_t"), bxt], wr=[br2, buf("l2sum")], tag=f"{tg}r2{tt}")
            OP("sc", (lambda r2=r2, s=s: (lambda e: e.activation(
                eT_s[0][:, :], r2[:, :], AF.Square, accum_out=l2sq[:, s:s + 1])))(),
               rd=[br2], wr=[buf("eT0"), buf("l2sq")], tag=f"{tg}l2q{tt}")
            OP("dv", (lambda r2=r2, s=s: (lambda e: e.reduce_max(
                l2max[:, s:s + 1], r2[:, :], axis=AX.X)))(),
               rd=[br2], wr=[buf("l2max")], tag=f"{tg}l2x{tt}")
            OP("dv", (lambda r2=r2, s=s: (lambda e: e.tensor_reduce(
                l2min[:, s:s + 1], r2[:, :], AX.X, ALU.min)))(),
               rd=[br2], wr=[buf("l2min")], tag=f"{tg}l2n{tt}")
            dma("sc", res2_d.rearrange("(s p) e -> p s e", p=128)[:, s, :], r2[:, :],
                [br2], [buf("res2_d")], tag=f"{tg}rs{tt}")

    OP("dv", lambda e: e.tensor_scalar(l2m[:, :], l2sum[:, :], 1.0 / float(E), None, ALU.mult),
       rd=[buf("l2sum")], wr=[buf("l2m")], tag="l2m")
    OP("dv", lambda e: e.tensor_mul(l2r[:, :], l2m[:, :], l2m[:, :]),
       rd=[buf("l2m")], wr=[buf("l2r")], tag="l2ms")
    OP("dv", lambda e: e.scalar_tensor_tensor(l2r[:, :], l2sq[:, :], 1.0 / float(E), l2r[:, :],
                                              ALU.mult, ALU.subtract),
       rd=[buf("l2sq"), buf("l2r")], wr=[buf("l2r")], tag="l2v")
    newton_rsqrt(buf("l2r"), l2r[:, :], buf("l2r"), l2r[:, :], LN_EPS, "l2rs")
    OP("dv", lambda e: e.tensor_sub(l2max[:, :], l2max[:, :], l2m[:, :]),
       rd=[buf("l2max"), buf("l2m")], wr=[buf("l2max")], tag="am1")
    OP("dv", lambda e: e.tensor_sub(l2min[:, :], l2m[:, :], l2min[:, :]),
       rd=[buf("l2min"), buf("l2m")], wr=[buf("l2min")], tag="am2")
    OP("dv", lambda e: e.tensor_max(l2max[:, :], l2max[:, :], l2min[:, :]),
       rd=[buf("l2max"), buf("l2min")], wr=[buf("l2max")], tag="am3")
    OP("dv", lambda e: e.tensor_mul(l2max[:, :], l2max[:, :], l2r[:, :]),
       rd=[buf("l2max"), buf("l2r")], wr=[buf("l2max")], tag="am4")
    OP("dv", lambda e: e.reduce_max(amrow[:, :], l2max[:, :], axis=AX.X),
       rd=[buf("l2max")], wr=[buf("amrow")], tag="am5")
    OP("pe", lambda e: e.matmul(pT2[0:1, 256:384], amrow[:, :], ident_f32[:, :], start=True, stop=True),
       rd=[buf("amrow"), buf("ident_f32")], wr=[buf("pT2")], tag="amT")
    OP("dv", lambda e: e.tensor_copy(wrow[:, :], pT2[0:1, 256:384]),
       rd=[buf("pT2")], wr=[buf("wrow")], tag="amTe")
    OP("dv", lambda e: e.reduce_max(scalar_sb[:, 2:3], wrow[:, :], axis=AX.X),
       rd=[buf("wrow")], wr=[buf("scalar_sb")], tag="aml")
    OP("dv", lambda e: e.tensor_copy(wrow[0:1, :], scalar_sb[0:1, 2:3].broadcast_to([1, 128])),
       rd=[buf("scalar_sb")], wr=[buf("wrow")], tag="amb")
    dma("gp", ccm_in_d[:, :], wrow[0:1, :], [buf("wrow")], [buf("ccm_in")], tag="ccmst")
    OP("gp", lambda e: e.collective_compute(
        "AllReduce", ALU.max, replica_groups=[list(range(n_cores))],
        ins=[ccm_in_d.ap().opt()], outs=[ccm_out_d.ap().opt()]),
        rd=[buf("ccm_in")], wr=[buf("ccm_out")], skey=("cc", False), tag="cc2")
    dma("gp", scalar_sb[0:1, 3:4], ccm_out_d[0:1, 0:1], [buf("ccm_out")], [buf("scalar_sb")], tag="ccmld")
    OP("dv", lambda e: e.tensor_scalar(scalar_sb[:, 4:5], scalar_sb[:, 3:4], 1e-12, None, ALU.max),
       rd=[buf("scalar_sb")], wr=[buf("scalar_sb")], tag="sx1")
    OP("dv", lambda e: e.reciprocal(scalar_sb[:, 4:5], scalar_sb[:, 4:5]),
       rd=[buf("scalar_sb")], wr=[buf("scalar_sb")], tag="sx2")
    OP("dv", lambda e: e.tensor_scalar(scalar_sb[:, 4:5], scalar_sb[:, 4:5], FP8_MAX_HALF, None, ALU.mult),
       rd=[buf("scalar_sb")], wr=[buf("scalar_sb")], tag="sx3")
    OP("dv", lambda e: e.tensor_mul(scalar_sb[:, 5:6], scalar_sb[:, 4:5], scalar_sb[:, 1:2]),
       rd=[buf("scalar_sb")], wr=[buf("scalar_sb")], tag="ssw")
    OP("dv", lambda e: e.tensor_scalar(bf_row[0:1, :], bf_row[0:1, :], scalar_sb[0:1, 5:6], None, ALU.mult),
       rd=[buf("scalar_sb"), buf("bf_row")], wr=[buf("bf_row")], tag="bfs")
    OP("pe", lambda e: e.matmul(pT2[:, 129:130], ones11[:, :], scalar_sb[0:1, 4:5], start=True, stop=True),
       rd=[buf("scalar_sb"), buf("ones11")], wr=[buf("pT2")], tag="sxB")
    OP("dv", lambda e: e.tensor_copy(sxcol[:, :], pT2[:, 129:130]),
       rd=[buf("pT2")], wr=[buf("sxcol")], tag="sxE")
    OP("pe", lambda e: e.matmul(pT2[:, 130:131], ones11[:, :], scalar_sb[0:1, 5:6], start=True, stop=True),
       rd=[buf("scalar_sb"), buf("ones11")], wr=[buf("pT2")], tag="dB")
    OP("dv", lambda e: e.tensor_copy(d_col[:, :], pT2[:, 130:131]),
       rd=[buf("pT2")], wr=[buf("d_col")], tag="dE")
    OP("dv", lambda e: e.reciprocal(d_col[:, :], d_col[:, :]),
       rd=[buf("d_col")], wr=[buf("d_col")], tag="dR")
    OP("dv", lambda e: e.tensor_scalar(qa[:, :], l2r[:, :], sxcol[:, :], None, ALU.mult),
       rd=[buf("l2r"), buf("sxcol")], wr=[buf("qa")], tag="qa")
    OP("dv", lambda e: e.scalar_tensor_tensor(qb[:, :], l2m[:, :], -1.0, qa[:, :], ALU.mult, ALU.mult),
       rd=[buf("l2m"), buf("qa")], wr=[buf("qb")], tag="qb")

    st_ops = []
    for s in range(NSUB):
        ri, bri = resin[s % 2], buf(f"x_tile{s % 2}")
        dma("sy", ri[:, :], res2_d.rearrange("(s p) e -> p s e", p=128)[:, s, :],
            [buf("res2_d")], [bri], tag=f"f3l{s}")
        xq, bxq = xq_s[s % 2], buf(f"xq{s % 2}")
        OP("sc", (lambda ri=ri, xq=xq, s=s: (lambda e: e.activation(
            xq[:, :], ri[:, :], AF.Identity, bias=qb[:, s:s + 1], scale=qa[:, s:s + 1])))(),
           rd=[bri, buf("qa"), buf("qb")], wr=[bxq], tag=f"f3q{s}")
        xqf, bxqf = xqf_s[0], buf("xqf0")
        for half in range(2):
            p, pb = (pT1, buf("pT1")) if half == 0 else (pT2, buf("pT2"))
            def xtr(e, xq=xq, half=half, p=p):
                inst = None
                for j in range(4):
                    ec = half * 4 + j
                    inst = e.matmul(p[:, j * 128:(j + 1) * 128],
                                    xq[:, ec * 128:(ec + 1) * 128], ident_bf[:, :],
                                    start=True, stop=True)
                return inst
            OP("pe", xtr, rd=[bxq, buf("ident_bf")], wr=[pb], tag=f"f3T{s}_{half}")
            OP("sc", (lambda xqf=xqf, half=half, p=p: (lambda e: e.activation(
                xqf[:, half * 512:(half + 1) * 512], p[:, 0:512], AF.Identity)))(),
               rd=[pb], wr=[bxqf], tag=f"f3E{s}_{half}")
        def ffn(e, xqf=xqf):
            e.matmul(S_ps[:, 0:512], ones_bf[0:1, 0:128], bf_row[0:1, 0:512], start=True, stop=False)
            e.matmul(S_ps[:, 512:1024], ones_bf[0:1, 0:128], bf_row[0:1, 512:1024], start=True, stop=False)
            inst = None
            for ec in range(ECH):
                for half in range(2):
                    inst = e.matmul(S_ps[:, half * 512:(half + 1) * 512],
                                    xqf[:, ec * 128:(ec + 1) * 128],
                                    wf8[:, ec * E + half * 512: ec * E + (half + 1) * 512],
                                    start=False, stop=(ec == ECH - 1))
            return inst
        OP("pe", ffn, rd=[bxqf, buf("wf8"), buf("bf_row"), buf("ones_bf")], wr=[buf("S_ps")],
           tag=f"f3m{s}")
        yb, byb = res2_s[s % 2], buf(f"res2_{s % 2}")
        OP("dv", (lambda ri=ri, yb=yb: (lambda e: e.scalar_tensor_tensor(
            yb[:, :], S_ps[:, :], d_col[:, :], ri[:, :], ALU.mult, ALU.add)))(),
           rd=[buf("S_ps"), buf("d_col"), bri], wr=[byb], tag=f"f3y{s}")
        st_ops.append(dma("sc", y_d.rearrange("(s p) e -> p s e", p=128)[:, s, :], yb[:, :],
                          [byb], [buf("y_d")], tag=f"f3s{s}"))

    OP("sy", None, deps=st_ops, tag="final")

    if max_ops is not None:
        keep = sch.ops[:max_ops]
        kept = set(id(o) for o in keep)
        sch.ops = keep
        for o in sch.ops:
            o.deps = {d for d in o.deps if id(d) in kept}
    sch.finalize()
    with ExitStack() as sem_es:
        sems = {}
        for key, name in [(("pe", False), "s_pe"), (("dv", False), "s_dv"),
                          (("sc", False), "s_ac"), (("gp", False), "s_gp"),
                          (("sy", False), "s_sy"), (("cc", False), "s_cc")]:
            sems[key] = sem_es.enter_context(nc.semaphore(name))
        for q in ("sy", "sc", "gp"):
            for j in range(Sched.NDMASEM):
                sems[(q, "dma", j)] = sem_es.enter_context(nc.semaphore(f"d_{q}{j}"))
        with nc.Block() as block:
            @block.sync
            def _(h):
                sch.emit("sy", h, sems)

            @block.tensor
            def _(h):
                sch.emit("pe", h, sems)

            @block.vector
            def _(h):
                sch.emit("dv", h, sems)

            @block.scalar
            def _(h):
                sch.emit("sc", h, sems)

            @block.gpsimd
            def _(h):
                sch.emit("gp", h, sems)
    es.close()
    return nc


def make_inputs(core_id, x_full, W, T):
    b, half = core_id // 2, core_id % 2
    xs = np.ascontiguousarray(x_full[b, half * T:(half + 1) * T, :], dtype=np.float32)
    d = {
        "x_tm": xs,
        "x_fm": np.ascontiguousarray(xs.T),
        "wq_t": np.ascontiguousarray(W["Wq"].T),
        "wk_t": np.ascontiguousarray(W["Wk"].T),
        "wv_t": np.ascontiguousarray(W["Wv"].T),
        "wu_t": np.ascontiguousarray(W["Wu"].T),
        "wo_t": np.ascontiguousarray(W["Wo"].T),
        "wf_t": np.ascontiguousarray(W["Wf"].T),
        "bq_row": np.ascontiguousarray(W["bq"].reshape(1, E)),
        "bk_row": np.ascontiguousarray(W["bk"].reshape(1, E)),
        "bv_row": np.ascontiguousarray(W["bv"].reshape(1, E)),
        "bu_row": np.ascontiguousarray(W["bu"].reshape(1, E)),
        "bo_row": np.ascontiguousarray(W["bo"].reshape(1, E)),
        "bf_row": np.ascontiguousarray(W["bf"].reshape(1, E)),
        "ones_row": np.ones((1, 512), dtype=np.float32),
        "scale_pt": np.ascontiguousarray(W["scale"].reshape(H, 128).T),
        "ident": np.eye(128, dtype=np.float32),
        "triu": np.triu(np.ones((128, 128), dtype=np.float32)),
        "onescol": np.ones((128, 1), dtype=np.float32),
        "ones11": np.ones((1, 128), dtype=np.float32),
        "emask": np.full((128, 1), 1.0 - (core_id % 2), dtype=np.float32),
        "omask": np.full((128, 1), float(core_id % 2), dtype=np.float32),
    }
    return d


def kernel(L0_attn_x, L0_attn_Wq, L0_attn_bq, L0_attn_Wk, L0_attn_bk,
           L0_attn_Wv, L0_attn_bv, L0_attn_Wu, L0_attn_bu, L0_attn_Wo,
           L0_attn_bo, L0_attn_scale, L0_ffn_weight, L0_ffn_bias):
    x = np.asarray(L0_attn_x, dtype=np.float32)
    Bn, N, _ = x.shape
    T = N // 2
    W = {
        "Wq": np.asarray(L0_attn_Wq, np.float32), "bq": np.asarray(L0_attn_bq, np.float32),
        "Wk": np.asarray(L0_attn_Wk, np.float32), "bk": np.asarray(L0_attn_bk, np.float32),
        "Wv": np.asarray(L0_attn_Wv, np.float32), "bv": np.asarray(L0_attn_bv, np.float32),
        "Wu": np.asarray(L0_attn_Wu, np.float32), "bu": np.asarray(L0_attn_bu, np.float32),
        "Wo": np.asarray(L0_attn_Wo, np.float32), "bo": np.asarray(L0_attn_bo, np.float32),
        "scale": np.asarray(L0_attn_scale, np.float32),
        "Wf": np.asarray(L0_ffn_weight, np.float32), "bf": np.asarray(L0_ffn_bias, np.float32),
    }
    n_cores = 2 * Bn
    nc = build_program(T=T, C=min(512, T), n_cores=n_cores)
    in_maps = [make_inputs(cid, x, W, T) for cid in range(n_cores)]
    res = run_bass_kernel_spmd(nc, in_maps, core_ids=list(range(n_cores)))
    y = np.empty((Bn, N, E), dtype=np.float32)
    for cid in range(n_cores):
        b, half = cid // 2, cid % 2
        y[b, half * T:(half + 1) * T, :] = res.results[cid]["y"]
    return y


# version 15
# speedup vs baseline: 1.0797x; 1.0493x over previous
import numpy as np
import concourse.bass as bass
ABLATE = set()
import concourse.mybir as mybir
from concourse.bass_utils import run_bass_kernel_spmd
from contextlib import ExitStack

F32 = mybir.dt.float32
BF16 = mybir.dt.bfloat16
F8 = mybir.dt.float8e4
I32 = mybir.dt.int32
AF = mybir.ActivationFunctionType
ALU = mybir.AluOpType
AX = mybir.AxisListType

E = 1024
H = 8
D = 128
ECH = 8
ICH = 8
LN_EPS = 1e-5
RMS_EPS = 1e-8
FP8_MAX_HALF = 224.0
LN2C = float(np.log(2.0))


class _Op:
    __slots__ = ("eng", "fn", "deps", "dma", "skey", "needs_inc", "val", "tag", "rdn", "wrn")

    def __init__(self, eng, fn, deps, dma, skey, tag):
        self.eng, self.fn, self.deps, self.dma, self.skey, self.tag = eng, fn, deps, dma, skey, tag
        self.needs_inc = False
        self.val = None
        self.rdn = ()
        self.wrn = ()


class Buf:
    __slots__ = ("w", "r", "name", "psum")

    def __init__(self, name="", psum=False):
        self.w, self.r, self.name, self.psum = None, [], name, psum


class Sched:
    SERIAL = False
    SERIAL_ENGS = set()

    def __init__(self):
        self.ops = []

    def op(self, eng, fn, rd=(), wr=(), dma=False, deps=(), skey=None, tag=""):
        dep = set()
        for b in rd:
            if b.w is not None:
                dep.add(b.w)
            if b.psum:
                for r_ in b.r:
                    if r_.eng != eng:
                        dep.add(r_)
        for b in wr:
            if b.w is not None:
                dep.add(b.w)
            dep.update(b.r)
        for d in deps:
            if d is not None:
                dep.add(d)
        o = _Op(eng, fn, dep, dma, skey or (eng, dma), tag)
        o.rdn = tuple(b.name for b in rd)
        o.wrn = tuple(b.name for b in wr)
        self.ops.append(o)
        for b in rd:
            b.r.append(o)
        for b in wr:
            b.w = o
            b.r = []
        return o

    NDMASEM = 8
    RELAX_SAME_ENGINE = {("pe", False), ("sc", False)}

    def _skip(self, d, o):
        return (d.skey == o.skey and d.skey in self.RELAX_SAME_ENGINE
                and not d.dma and not o.dma)

    def finalize(self):
        if Sched.SERIAL:
            prev = None
            for o in self.ops:
                if o.fn is not None and prev is not None:
                    o.deps.add(prev)
                if o.fn is not None:
                    prev = o
        if Sched.SERIAL_ENGS:
            prev_any = None
            last_chained = None
            for o in self.ops:
                if o.fn is None:
                    continue
                if o.eng in Sched.SERIAL_ENGS:
                    if prev_any is not None:
                        o.deps.add(prev_any)
                    last_chained = o
                elif last_chained is not None:
                    o.deps.add(last_chained)
                prev_any = o
        rot = {}
        for o in self.ops:
            if o.dma:
                q = o.skey
                j = rot.get(q, 0)
                rot[q] = (j + 1) % self.NDMASEM
                o.skey = (q[0], "dma", j)
                o.needs_inc = True
        for o in self.ops:
            o.deps = {d for d in o.deps if not self._skip(d, o)}
        for o in self.ops:
            o.needs_inc = o.needs_inc and False or o.needs_inc
        for o in self.ops:
            o.needs_inc = bool(o.dma)
        for o in self.ops:
            for d in o.deps:
                d.needs_inc = True
        cnt = {}
        for o in self.ops:
            if o.needs_inc:
                cnt[o.skey] = cnt.get(o.skey, 0) + (16 if o.dma else 1)
                o.val = cnt[o.skey]

    def emit(self, eng_name, handle, sems):
        water = {}
        for o in self.ops:
            if o.eng != eng_name:
                continue
            waits = {}
            for d in o.deps:
                if d.val is None:
                    raise RuntimeError(f"dep without inc: {d.tag} <- {o.tag}")
                waits[d.skey] = max(waits.get(d.skey, 0), d.val)
            if o.dma and o.val >= 32:
                waits[o.skey] = max(waits.get(o.skey, 0), o.val - 16)
            for key in sorted(waits, key=str):
                v = waits[key]
                if water.get(key, 0) < v:
                    handle.wait_ge(sems[key], v)
                    water[key] = v
            if o.fn is None:
                continue
            inst = o.fn(handle)
            if o.needs_inc:
                inst.then_inc(sems[o.skey], 16 if o.dma else 1)


def build_program(T=2048, C=512, n_cores=8, max_ops=None):
    NCH = T // C
    NTT = C // 128
    NSUB = T // 128
    nc = bass.Bass("TRN2", target_bir_lowering=False, detect_race_conditions=False)

    x_tm_d = nc.dram_tensor("x_tm", [T, E], F32, kind="ExternalInput")
    x_fm_d = nc.dram_tensor("x_fm", [E, T], F32, kind="ExternalInput")
    w_d = {n: nc.dram_tensor(n, [E, E], F32, kind="ExternalInput")
           for n in ("wq_t", "wk_t", "wv_t", "wu_t", "wo_t", "wf_t")}
    bias_d = {n: nc.dram_tensor(n, [1, E], F32, kind="ExternalInput")
              for n in ("bq_row", "bk_row", "bv_row", "bu_row")}
    bo_row_d = nc.dram_tensor("bo_row", [1, E], F32, kind="ExternalInput")
    bf_row_d = nc.dram_tensor("bf_row", [1, E], F32, kind="ExternalInput")
    scale_d = nc.dram_tensor("scale_pt", [128, H], F32, kind="ExternalInput")
    ident_d = nc.dram_tensor("ident", [128, 128], F32, kind="ExternalInput")
    triu_d = nc.dram_tensor("triu", [128, 128], F32, kind="ExternalInput")
    ones_d = nc.dram_tensor("ones_row", [1, 512], F32, kind="ExternalInput")
    onescol_d = nc.dram_tensor("onescol", [128, 1], F32, kind="ExternalInput")
    ones11_d = nc.dram_tensor("ones11", [1, 128], F32, kind="ExternalInput")
    emask_d = nc.dram_tensor("emask", [128, 1], F32, kind="ExternalInput")
    omask_d = nc.dram_tensor("omask", [128, 1], F32, kind="ExternalInput")
    y_d = nc.dram_tensor("y", [T, E], F32, kind="ExternalOutput")

    res2_d = nc.dram_tensor("res2_scratch", [T, E], F32)
    wq_bf_d = nc.dram_tensor("wq_bf", [E, E], BF16)
    wu_bf_d = nc.dram_tensor("wu_bf", [E, E], BF16)
    ccs_in_d = nc.dram_tensor("ccs_in", [128, H * D], BF16)
    ccs_out_d = nc.dram_tensor("ccs_out", [128, H * D], BF16)
    ccm_in_d = nc.dram_tensor("ccm_in", [1, 128], F32)
    ccm_out_d = nc.dram_tensor("ccm_out", [1, 128], F32)

    es = ExitStack()
    sbt = lambda name, shape, dt: es.enter_context(nc.sbuf_tensor(name, shape, dt))
    pst = lambda name, shape: es.enter_context(nc.psum_tensor(name, shape, F32))

    ktm_st = sbt("ktm_st", [128, NSUB * E], BF16)
    vtm_st = sbt("vtm_st", [128, NSUB * E], BF16)
    wA = sbt("wA", [128, ECH * E], BF16)
    wB = sbt("wB", [128, ECH * E], BF16)
    wC = sbt("wC", [128, ECH * E], BF16)
    wf8 = sbt("wf8", [128, ECH * E], F8)
    wf32 = sbt("wf32", [128, 256], F32)
    pkA = sbt("pkA", [34, E], BF16)
    pkB = sbt("pkB", [34, E], BF16)
    pk = {"q": (pkA, 0), "k": (pkA, 32), "v": (pkB, 0), "u": (pkB, 32)}
    rows_dyn = [sbt(f"rows_dyn{i}", [34, 512], BF16) for i in range(2)]
    ones_bf = sbt("ones_bf", [1, 128], BF16)
    bo_row = sbt("bo_row_s", [1, E], BF16)
    bf_row = sbt("bf_row_s", [1, E], BF16)
    scale_pt = sbt("scale_pt_s", [128, H], F32)
    sclh = sbt("sclh", [128, H], F32)
    ident_bf = sbt("ident_bf", [128, 128], BF16)
    ident_f32 = sbt("ident_f32", [128, 128], F32)
    triu_f = sbt("triu_f", [128, 128], BF16)
    onescol = sbt("onescol_s", [128, 1], BF16)
    ones11 = sbt("ones11_s", [1, 128], F32)
    emask = sbt("emask_s", [128, 1], F32)
    omask = sbt("omask_s", [128, 1], F32)

    xf32 = [sbt(f"xf32_{i}", [128, 512], F32) for i in range(2)]
    xn = sbt("xn", [128, ECH * C], BF16)
    rbc = sbt("rbc", [128, C], BF16)
    x_tile = [sbt(f"x_tile{i}", [128, E], F32) for i in range(2)]
    q_s = sbt("q_s", [128, H * C], BF16)
    us_s = sbt("us_s", [128, H * C], BF16)
    e_scr = sbt("e_scr", [128, 512], F32)
    t2_scr = sbt("t2_scr", [128, 512], BF16)
    rrow = sbt("rrow", [1, C], BF16)
    statc = sbt("statc", [128, 1], BF16)

    kfm_sub = [sbt("kfm0", [128, E], BF16)]
    eT_s = [sbt("eT0", [128, H * 128], BF16)]
    S_acc = sbt("S_acc", [128, H * D], F32)
    ssb = [sbt("ssb0", [128, H * D], BF16)]
    sinit = sbt("sinit", [128, H * D], BF16)
    g1_s = [sbt(f"g1_{i}", [128, E], BF16) for i in range(NTT)]
    res2_s = [sbt(f"res2_{i}", [128, E], F32) for i in range(2)]
    resin = x_tile

    xq_s = [sbt(f"xq{i}", [128, E], F8) for i in range(2)]
    xqf_s = [sbt("xqf0", [128, E], F8)]

    bn6 = sbt("bn6", [128, 12], F32)
    bn2 = sbt("bn2", [128, 2], F32)
    l1m = sbt("l1m", [128, NSUB], F32)
    l1v = sbt("l1v", [128, NSUB], F32)
    l1r = sbt("l1r", [128, NSUB], F32)
    l1mr = sbt("l1mr", [128, NSUB], F32)
    l2sum = sbt("l2sum", [128, NSUB], F32)
    l2sq = sbt("l2sq", [128, NSUB], F32)
    l2max = sbt("l2max", [128, NSUB], F32)
    l2min = sbt("l2min", [128, NSUB], F32)
    l2m = sbt("l2m", [128, NSUB], F32)
    l2r = sbt("l2r", [128, NSUB], F32)
    qa = sbt("qa", [128, NSUB], F32)
    qb = sbt("qb", [128, NSUB], F32)
    nsc = sbt("nsc", [128, NSUB], F32)
    nsc2 = sbt("nsc2", [128, NSUB], F32)
    nsc3 = sbt("nsc3", [128, NSUB], F32)
    exp_in = nsc2
    rcp_t = sbt("rcp_t", [128, NTT], F32)
    rmse_bf = sbt("rmse_bf", [128, NTT], BF16)
    wmax = sbt("wmax", [128, 4 * ECH], F32)
    wmax1 = sbt("wmax1", [128, 1], F32)
    wrow = sbt("wrow", [1, 128], F32)
    scalar_sb = sbt("scalar_sb", [1, 8], F32)
    swh_col = sbt("swh_col", [128, 1], F32)
    sxcol = sbt("sxcol", [128, 1], F32)
    d_col = sbt("d_col", [128, 1], F32)
    amrow = sbt("amrow", [128, 1], F32)
    seedb = sbt("seedb", [128, 1], F32)

    S_ps = pst("S_ps", [128, 1024])
    pBig = pst("pBig", [128, 1024])
    pS1 = pst("pS1", [128, 512])
    pS2 = pst("pS2", [128, 512])
    pT1 = pst("pT1", [128, 512])
    pT2 = pst("pT2", [128, 512])

    B = {}
    PSUM_BUFS = {"S_ps", "pBig", "pS1", "pS2", "pT1", "pT2"}

    def buf(name):
        if name not in B:
            B[name] = Buf(name, psum=(name in PSUM_BUFS))
        return B[name]

    sch = Sched()
    OP = sch.op

    def dma(eng, out_ap, in_ap, rd, wr, deps=(), tag=""):
        return OP(eng, (lambda o, i: (lambda e: e.dma_start(out=o, in_=i)))(out_ap, in_ap),
                  rd=rd, wr=wr, dma=True, deps=deps, tag=tag)

    for n, key in (("q", "bq_row"), ("k", "bk_row"), ("v", "bv_row"), ("u", "bu_row")):
        t, po = pk[n]
        dma("gp", t[po + 1:po + 2, :], bias_d[key][:, :], [], [buf(f"pk_{n}_b")], tag=f"ldb{n}")
    for i in range(2):
        dma("gp", rows_dyn[i][1:2, :], ones_d[:, :], [], [buf(f"rows_ones{i}")], tag=f"ldro{i}")
        dma("gp", rows_dyn[i][33:34, :], ones_d[:, :], [], [buf(f"rows_ones{i}")], tag=f"ldro32_{i}")
    dma("gp", bo_row[:, :], bo_row_d[:, :], [], [buf("bo_row")], tag="ldbo")
    dma("gp", bf_row[:, :], bf_row_d[:, :], [], [buf("bf_row")], tag="ldbf")
    dma("gp", ident_bf[:, :], ident_d[:, :], [], [buf("ident_bf")], tag="ldidb")
    dma("gp", ones_bf[:, :], ones_d[0:1, 0:128], [], [buf("ones_bf")], tag="ldonesbf")
    dma("gp", onescol[:, :], onescol_d[:, :], [], [buf("onescol")], tag="ldoc")
    dma("sy", ident_f32[:, :], ident_d[:, :], [], [buf("ident_f32")], tag="ldidf")
    dma("gp", triu_f[:, :], triu_d[:, :], [], [buf("triu_f")], tag="ldtriu")
    dma("sy", scale_pt[:, :], scale_d[:, :], [], [buf("scale_pt")], tag="ldscale")
    dma("sy", ones11[:, :], ones11_d[:, :], [], [buf("ones11")], tag="ldo11")
    dma("sy", emask[:, :], emask_d[:, :], [], [buf("emask")], tag="ldem")
    dma("sy", omask[:, :], omask_d[:, :], [], [buf("omask")], tag="ldom")
    OP("dv", lambda e: e.tensor_scalar(sclh[:, :], scale_pt[:, :], 0.25, None, ALU.mult),
       rd=[buf("scale_pt")], wr=[buf("sclh")], tag="sclh")
    OP("dv", lambda e: e.memset(seedb[:, :], float(LN2C * 63.5)),
       rd=[], wr=[buf("seedb")], tag="seedb")

    def wload(wname, tgt, tgt_buf, src, tag, replaces=None):
        deps = []
        if replaces is not None:
            b = buf(replaces)
            deps = list(b.r) + ([b.w] if b.w else [])
        last = None
        for ec in range(ECH):
            last = dma("gp", tgt[:, :].rearrange("p (ec i) -> p ec i", ec=ECH)[:, ec, :],
                       src.rearrange("(ec p) i -> p ec i", p=128)[:, ec, :],
                       [], [buf(tgt_buf)], deps=deps, tag=f"{tag}_{ec}")
        return last

    def emit_pk(name, wt, wt_buf):
        t, po = pk[name]
        for half in range(2):
            p, pb = (pT1, buf("pT1")) if half == 0 else (pT2, buf("pT2"))
            def grp(e, half=half, p=p, wt=wt, po=po):
                inst = None
                for ec in range(ECH):
                    inst = e.matmul(p[po:po + 1, 0:512], onescol[:, :],
                                    wt[:, ec * E + half * 512: ec * E + (half + 1) * 512],
                                    start=(ec == 0), stop=(ec == ECH - 1))
                return inst
            OP("pe", grp, rd=[wt_buf, buf("onescol")], wr=[pb], tag=f"pk{name}{half}")
            OP("dv", (lambda t=t, po=po, half=half, p=p: (lambda e: e.tensor_scalar(
                t[po:po + 1, half * 512:(half + 1) * 512], p[po:po + 1, 0:512], -1.0, None, ALU.mult)))(),
               rd=[pb], wr=[buf(f"pk_{name}_c")], tag=f"pk{name}e{half}")



    def newton_rsqrt(src_buf, src_ap, out_buf, out_ap, eps_add, tag, iters=3):
        n = src_ap.shape[-1]
        OP("dv", lambda e: e.tensor_scalar(nsc3[:, 0:n], src_ap, float(eps_add), None, ALU.add),
           rd=[src_buf], wr=[buf("nsc3")], tag=f"{tag}e")
        OP("dv", lambda e: e.tensor_copy(exp_in[:, 0:n], nsc3[:, 0:n].bitcast(I32)),
           rd=[buf("nsc3")], wr=[buf("nsc2")], tag=f"{tag}b")
        OP("sc", lambda e: e.activation(nsc[:, 0:n], exp_in[:, 0:n], AF.Exp,
                                        bias=seedb[:, :],
                                        scale=float(-LN2C * 0.5 / (2 ** 23))),
           rd=[buf("nsc2"), buf("seedb")], wr=[buf("nsc")], tag=f"{tag}s")
        for it in range(iters):
            OP("dv", lambda e: e.tensor_mul(nsc2[:, 0:n], nsc[:, 0:n], nsc[:, 0:n]),
               rd=[buf("nsc")], wr=[buf("nsc2")], tag=f"{tag}{it}a")
            OP("dv", lambda e: e.tensor_mul(nsc2[:, 0:n], nsc2[:, 0:n], nsc3[:, 0:n]),
               rd=[buf("nsc2"), buf("nsc3")], wr=[buf("nsc2")], tag=f"{tag}{it}b")
            OP("dv", lambda e: e.tensor_scalar(nsc2[:, 0:n], nsc2[:, 0:n], -0.5, 1.5, ALU.mult, ALU.add),
               rd=[buf("nsc2")], wr=[buf("nsc2")], tag=f"{tag}{it}c")
            OP("dv", lambda e: e.tensor_mul(nsc[:, 0:n], nsc[:, 0:n], nsc2[:, 0:n]),
               rd=[buf("nsc"), buf("nsc2")], wr=[buf("nsc")], tag=f"{tag}{it}d")
        OP("dv", lambda e: e.tensor_copy(out_ap, nsc[:, 0:n]),
           rd=[buf("nsc")], wr=[out_buf], tag=f"{tag}o")

    def col_to_row(col_buf, col_ap, p, p_buf, c0, tag):
        OP("dv", lambda e: e.tensor_copy(statc[:, :], col_ap),
           rd=[col_buf], wr=[buf("statc")], tag=f"{tag}c")
        OP("pe", lambda e: e.matmul(p[0:1, c0:c0 + 128], statc[:, :], ident_bf[:, :],
                                    start=True, stop=True),
           rd=[buf("statc"), buf("ident_bf")], wr=[p_buf], tag=f"{tag}T")

    def ln1_rows(c, ph):
        tag = f"p{ph}c{c}"
        rdyn, brdyn = rows_dyn[c % 2], buf(f"rows_mr{c % 2}")
        for tt in range(NTT):
            s = c * NTT + tt
            col_to_row(buf("l1r"), l1r[:, s:s + 1], pS1, buf("pS1"), tt * 128, f"{tag}r{tt}")
            col_to_row(buf("l1mr"), l1mr[:, s:s + 1], pS2, buf("pS2"), tt * 128, f"{tag}g{tt}")
            OP("pe", (lambda tt=tt: (lambda e: e.matmul(
                pS2[32:33, tt * 128:(tt + 1) * 128], statc[:, :], ident_bf[:, :],
                start=True, stop=True)))(),
               rd=[buf("statc"), buf("ident_bf")], wr=[buf("pS2")], tag=f"{tag}g32_{tt}")
        OP("dv", lambda e: e.tensor_copy(rrow[0:1, :], pS1[0:1, 0:C]),
           rd=[buf("pS1")], wr=[buf("rrow")], tag=f"{tag}rev")
        OP("dv", (lambda rdyn=rdyn: (lambda e: e.tensor_copy(rdyn[0:1, :], pS2[0:1, 0:C])))(),
           rd=[buf("pS2")], wr=[brdyn], tag=f"{tag}mrev")
        OP("dv", (lambda rdyn=rdyn: (lambda e: e.tensor_copy(rdyn[32:33, :], pS2[32:33, 0:C])))(),
           rd=[buf("pS2")], wr=[brdyn], tag=f"{tag}mrev32")
        OP("pe", lambda e: e.matmul(pS1[:, 0:C], ones_bf[0:1, 0:128], rrow[0:1, :], start=True, stop=True),
           rd=[buf("rrow"), buf("ones_bf")], wr=[buf("pS1")], tag=f"{tag}rbcM")
        OP("dv", lambda e: e.tensor_copy(rbc[:, :], pS1[:, 0:C]),
           rd=[buf("pS1")], wr=[buf("rbc")], tag=f"{tag}rbc")

    def make_xn(c, ph):
        tag = f"p{ph}c{c}"
        for ec in range(ECH):
            xb, bxb = xf32[ec % 2], buf(f"xf32_{ec % 2}")
            dma("sy", xb[:, :], x_fm_d.rearrange("(ec p) t -> p ec t", p=128)[:, ec, c * C:(c + 1) * C],
                [], [bxb], tag=f"{tag}xld{ec}")
            OP("dv", (lambda xb_=xb, ec_=ec: (lambda e: e.tensor_mul(
                xn[:, ec_ * C:(ec_ + 1) * C], xb_[:, :], rbc[:, :])))(),
               rd=[bxb, buf("rbc")], wr=[buf("xn")], tag=f"{tag}xr{ec}")

    def proj_fm(c, wt_buf, wt, pkn, out_s, out_buf, kind, tag):
        rdyn, brdyn = rows_dyn[c % 2], buf(f"rows_mr{c % 2}")
        bpk = [buf(f"pk_{pkn}_c"), buf(f"pk_{pkn}_b"), brdyn, buf(f"rows_ones{c % 2}")]
        for ic in range(ICH):
            p, pb = (pS1, buf("pS1")) if ic % 2 == 0 else (pS2, buf("pS2"))
            pkt, po = pk[pkn]
            def group(e, ic=ic, p=p, rdyn=rdyn, pkt=pkt, po=po):
                e.matmul(p[:, 0:C], pkt[po:po + 2, ic * 128:(ic + 1) * 128],
                         rdyn[po:po + 2, 0:C], start=True, stop=False)
                inst = None
                for ec in range(ECH):
                    inst = e.matmul(p[:, 0:C], wt[:, ec * E + ic * 128: ec * E + (ic + 1) * 128],
                                    xn[:, ec * C:(ec + 1) * C], start=False, stop=(ec == ECH - 1))
                return inst
            OP("pe", group, rd=[wt_buf, buf("xn")] + bpk, wr=[pb], tag=f"{tag}mm{ic}")
            if kind == "elu":
                OP("sc", (lambda p=p: (lambda e: e.activation(e_scr[:, 0:C], p[:, 0:C], AF.Exp)))(),
                   rd=[pb], wr=[buf("e_scr")], tag=f"{tag}ex{ic}")
                OP("dv", lambda e: e.tensor_scalar(t2_scr[:, 0:C], e_scr[:, 0:C], 1.0, 0.0,
                                                   ALU.subtract, ALU.min),
                   rd=[buf("e_scr")], wr=[buf("t2_scr")], tag=f"{tag}mn{ic}")
                OP("dv", (lambda p=p, ic=ic: (lambda e: e.scalar_tensor_tensor(
                    out_s[:, ic * C:(ic + 1) * C], p[:, 0:C], 0.0, t2_scr[:, 0:C], ALU.max, ALU.add)))(),
                   rd=[pb, buf("t2_scr")], wr=[out_buf], tag=f"{tag}el{ic}")
            else:
                OP("sc", (lambda p=p: (lambda e: e.activation(e_scr[:, 0:C], p[:, 0:C], AF.Tanh, scale=0.5)))(),
                   rd=[pb], wr=[buf("e_scr")], tag=f"{tag}th{ic}")
                OP("dv", (lambda p=p, ic=ic: (lambda e: e.tensor_scalar(
                    t2_scr[:, 0:C], p[:, 0:C], sclh[:, ic:ic + 1], None, ALU.mult)))(),
                   rd=[pb, buf("sclh")], wr=[buf("t2_scr")], tag=f"{tag}w1{ic}")
                OP("dv", (lambda ic=ic: (lambda e: e.scalar_tensor_tensor(
                    out_s[:, ic * C:(ic + 1) * C], e_scr[:, 0:C], 1.0, t2_scr[:, 0:C], ALU.add, ALU.mult)))(),
                   rd=[buf("e_scr"), buf("t2_scr")], wr=[out_buf], tag=f"{tag}us{ic}")

    def proj_tm_tile(c, tt, wt_buf, wt, pkn, out_st, out_buf, kind, tag, pj):
        if True:
            s = c * NTT + tt
            rdyn, brdyn = rows_dyn[c % 2], buf(f"rows_mr{c % 2}")
            bpk = [buf(f"pk_{pkn}_c"), buf(f"pk_{pkn}_b"), brdyn, buf(f"rows_ones{c % 2}")]
            if pj % 2 == 0:
                halves = [(pBig[:, 0:512], buf("pBig")), (pBig[:, 512:1024], buf("pBig"))]
            else:
                halves = [(pS1[:, :], buf("pS1")), (pS2[:, :], buf("pS2"))]
            hbufs = list({id(b): b for _, b in halves}.values())
            pkt, po = pk[pkn]
            def group(e, tt=tt, halves=halves, rdyn=rdyn, pkt=pkt, po=po):
                for half in range(2):
                    e.matmul(halves[half][0], rdyn[po:po + 2, tt * 128:(tt + 1) * 128],
                             pkt[po:po + 2, half * 512:(half + 1) * 512],
                             start=True, stop=False)
                inst = None
                for ec in range(ECH):
                    for half in range(2):
                        inst = e.matmul(halves[half][0],
                                        xn[:, ec * C + tt * 128: ec * C + (tt + 1) * 128],
                                        wt[:, ec * E + half * 512: ec * E + (half + 1) * 512],
                                        start=False, stop=(ec == ECH - 1))
                return inst
            OP("pe", group, rd=[wt_buf, buf("xn")] + bpk, wr=hbufs,
               tag=f"{tag}mm{tt}")
            for hf in range(2):
                pap, pb = halves[hf]
                if kind == "elu":
                    OP("sc", (lambda pap=pap: (lambda e: e.activation(
                        e_scr[:, :], pap, AF.Exp)))(),
                       rd=[pb], wr=[buf("e_scr")], tag=f"{tag}ex{tt}_{hf}")
                    OP("dv", lambda e: e.tensor_scalar(t2_scr[:, :], e_scr[:, :], 1.0, 0.0,
                                                       ALU.subtract, ALU.min),
                       rd=[buf("e_scr")], wr=[buf("t2_scr")], tag=f"{tag}mn{tt}_{hf}")
                    OP("dv", (lambda s=s, hf=hf, pap=pap: (lambda e: e.scalar_tensor_tensor(
                        out_st[:, s * E + hf * 512: s * E + (hf + 1) * 512],
                        pap, 0.0, t2_scr[:, :], ALU.max, ALU.add)))(),
                       rd=[pb, buf("t2_scr")], wr=[out_buf], tag=f"{tag}el{tt}_{hf}")
                else:
                    OP("sc", (lambda pap=pap: (lambda e: e.activation(
                        e_scr[:, :], pap, AF.Tanh, scale=0.5)))(),
                       rd=[pb], wr=[buf("e_scr")], tag=f"{tag}th{tt}_{hf}")
                    OP("dv", (lambda s=s, hf=hf, pap=pap: (lambda e: e.scalar_tensor_tensor(
                        out_st[:, s * E + hf * 512: s * E + (hf + 1) * 512], e_scr[:, :], 1.0,
                        pap, ALU.add, ALU.mult)))(),
                       rd=[buf("e_scr"), pb], wr=[out_buf], tag=f"{tag}v{tt}_{hf}")

    def state_update1(s):
        def group(e, s=s):
            inst = None
            for h in range(H):
                inst = e.matmul(S_ps[:, h * D:(h + 1) * D],
                                ktm_st[:, s * E + h * D: s * E + (h + 1) * D],
                                vtm_st[:, s * E + h * D: s * E + (h + 1) * D],
                                start=(s == 0 and h in (0, 4)),
                                stop=(s == NSUB - 1 and h in (3, 7)))
            return inst
        OP("pe", group, rd=[buf("ktm_st"), buf("vtm_st")], wr=[buf("S_ps")], tag=f"p1st{s}")

    def state_update(s, ph):
        def group(e, s=s):
            inst = None
            for h in range(H):
                inst = e.matmul(S_ps[:, h * D:(h + 1) * D],
                                ktm_st[:, s * E + h * D: s * E + (h + 1) * D],
                                vtm_st[:, s * E + h * D: s * E + (h + 1) * D],
                                start=(h in (0, 4)), stop=(h in (3, 7)))
            return inst
        OP("pe", group, rd=[buf("ktm_st"), buf("vtm_st")], wr=[buf("S_ps")], tag=f"p{ph}st{s}")

    def state_accum(s, ph, reset):
        if reset:
            OP("dv", lambda e: e.tensor_copy(S_acc[:, :], S_ps[:, :]),
               rd=[buf("S_ps")], wr=[buf("S_acc")], tag=f"p{ph}sa{s}")
        else:
            OP("dv", lambda e: e.tensor_add(S_acc[:, :], S_acc[:, :], S_ps[:, :]),
               rd=[buf("S_ps"), buf("S_acc")], wr=[buf("S_acc")], tag=f"p{ph}sa{s}")

    def ln1_stats(c):
        tag = f"p1c{c}"
        for tt in range(NTT):
            s = c * NTT + tt
            xt, bxt = x_tile[tt % 2], buf(f"x_tile{tt % 2}")
            dma("sy", xt[:, :], x_tm_d.rearrange("(s p) e -> p s e", p=128)[:, s, :],
                [], [bxt], tag=f"{tag}xt{tt}")
            OP("dv", (lambda xt=xt: (lambda e: e.bn_stats(bn6[:, 0:6], xt[:, 0:512])))(),
               rd=[bxt], wr=[buf("bn6")], tag=f"{tag}bnA{tt}")
            OP("dv", (lambda xt=xt: (lambda e: e.bn_stats(bn6[:, 6:12], xt[:, 512:1024])))(),
               rd=[bxt], wr=[buf("bn6")], tag=f"{tag}bnB{tt}")
            OP("dv", lambda e: e.bn_aggr(bn2[:, :], bn6[:, :]),
               rd=[buf("bn6")], wr=[buf("bn2")], tag=f"{tag}ag{tt}")
            OP("dv", (lambda s=s: (lambda e: e.tensor_copy(l1m[:, s:s + 1], bn2[:, 0:1])))(),
               rd=[buf("bn2")], wr=[buf("l1m")], tag=f"{tag}mc{tt}")
            OP("dv", (lambda s=s: (lambda e: e.tensor_copy(l1v[:, s:s + 1], bn2[:, 1:2])))(),
               rd=[buf("bn2")], wr=[buf("l1v")], tag=f"{tag}vc{tt}")
        newton_rsqrt(buf("l1v"), l1v[:, c * NTT:(c + 1) * NTT], buf("l1r"),
                     l1r[:, c * NTT:(c + 1) * NTT], LN_EPS, f"{tag}rs")
        OP("dv", (lambda c=c: (lambda e: e.tensor_mul(
            l1mr[:, c * NTT:(c + 1) * NTT], l1m[:, c * NTT:(c + 1) * NTT],
            l1r[:, c * NTT:(c + 1) * NTT])))(),
           rd=[buf("l1m"), buf("l1r")], wr=[buf("l1mr")], tag=f"{tag}mr")

    def emit_wf_amax(lo, hi):
        for ix in range(lo, hi):
            ec, qr = ix // 4, ix % 4
            dma("sy", wf32[:, :],
                w_d["wf_t"].rearrange("(ec p) o -> p ec o", p=128)[:, ec, qr * 256:(qr + 1) * 256],
                [], [buf("wf32")], tag=f"wfl{ix}")
            OP("dv", (lambda ix=ix: (lambda e: e.reduce_max(wmax[:, ix:ix + 1], wf32[:, :],
                                                            axis=AX.X, apply_absolute_value=True)))(),
               rd=[buf("wf32")], wr=[buf("wmax")], tag=f"wfm{ix}")

    def emit_wf_swh():
        OP("dv", lambda e: e.reduce_max(wmax1[:, :], wmax[:, :], axis=AX.X),
           rd=[buf("wmax")], wr=[buf("wmax1")], tag="wfmc")
        OP("pe", lambda e: e.matmul(pT2[0:1, 0:128], wmax1[:, :], ident_f32[:, :], start=True, stop=True),
           rd=[buf("wmax1"), buf("ident_f32")], wr=[buf("pT2")], tag="wfT")
        OP("dv", lambda e: e.tensor_copy(wrow[:, :], pT2[0:1, 0:128]),
           rd=[buf("pT2")], wr=[buf("wrow")], tag="wfTe")
        OP("dv", lambda e: e.reduce_max(scalar_sb[:, 0:1], wrow[:, :], axis=AX.X),
           rd=[buf("wrow")], wr=[buf("scalar_sb")], tag="wfam")
        OP("dv", lambda e: e.tensor_scalar(scalar_sb[:, 1:2], scalar_sb[:, 0:1], 1e-12, None, ALU.max),
           rd=[buf("scalar_sb")], wr=[buf("scalar_sb")], tag="swh1")
        OP("dv", lambda e: e.reciprocal(scalar_sb[:, 1:2], scalar_sb[:, 1:2]),
           rd=[buf("scalar_sb")], wr=[buf("scalar_sb")], tag="swh2")
        OP("dv", lambda e: e.tensor_scalar(scalar_sb[:, 1:2], scalar_sb[:, 1:2], FP8_MAX_HALF, None, ALU.mult),
           rd=[buf("scalar_sb")], wr=[buf("scalar_sb")], tag="swh3")
        OP("pe", lambda e: e.matmul(pT2[:, 128:129], ones11[:, :], scalar_sb[0:1, 1:2], start=True, stop=True),
           rd=[buf("scalar_sb"), buf("ones11")], wr=[buf("pT2")], tag="swhB")
        OP("dv", lambda e: e.tensor_copy(swh_col[:, :], pT2[:, 128:129]),
           rd=[buf("pT2")], wr=[buf("swh_col")], tag="swhE")

    def emit_wf_quant(lo, hi):
        for ix in range(lo, hi):
            ec, qr = ix // 4, ix % 4
            dma("sy", wf32[:, :],
                w_d["wf_t"].rearrange("(ec p) o -> p ec o", p=128)[:, ec, qr * 256:(qr + 1) * 256],
                [], [buf("wf32")], tag=f"wfl2_{ix}")
            OP("sc", (lambda ec=ec, qr=qr: (lambda e: e.activation(
                wf8[:, ec * E + qr * 256: ec * E + (qr + 1) * 256], wf32[:, :],
                AF.Identity, bias=0.0, scale=swh_col[:, :])))(),
               rd=[buf("wf32"), buf("swh_col")], wr=[buf("wf8")], tag=f"wfq{ix}")

    wload("wk", wA, "wA", w_d["wk_t"], "ldwk")
    ln1_stats(0)
    ln1_rows(0, 1)
    make_xn(0, 1)
    emit_pk("k", wA, buf("wA"))
    wload("wv", wB, "wB", w_d["wv_t"], "ldwv")
    wload("wo", wC, "wC", w_d["wo_t"], "ldwo")
    for ec in range(ECH):
        dma("gp", wq_bf_d.rearrange("(ec p) i -> p ec i", p=128)[:, ec, :],
            w_d["wq_t"].rearrange("(ec p) i -> p ec i", p=128)[:, ec, :],
            [], [buf("wq_bf_d")], tag=f"pcq{ec}")
        dma("gp", wu_bf_d.rearrange("(ec p) i -> p ec i", p=128)[:, ec, :],
            w_d["wu_t"].rearrange("(ec p) i -> p ec i", p=128)[:, ec, :],
            [], [buf("wu_bf_d")], tag=f"pcu{ec}")
    for c in range(NCH):
        pj = 0
        for tt in range(NTT):
            proj_tm_tile(c, tt, buf("wA"), wA, "k", ktm_st,
                         buf("ktm_st"), "elu", f"p1c{c}k", pj)
            pj += 1
            if c == 0 and tt == 0:
                emit_pk("v", wB, buf("wB"))
            proj_tm_tile(c, tt, buf("wB"), wB, "v", vtm_st,
                         buf("vtm_st"), "silu2x", f"p1c{c}v", pj)
            pj += 1
            if c + 1 < NCH:
                if tt == 0:
                    ln1_stats(c + 1)
                elif tt == 1:
                    ln1_rows(c + 1, 1)
                elif tt == 3:
                    make_xn(c + 1, 1)
            slot = c * NTT + tt
            nslots = NCH * NTT
            lo = slot * (8 * ECH) // nslots
            hi = (slot + 1) * (8 * ECH) // nslots
            for ix in range(lo, hi):
                if ix < 4 * ECH:
                    emit_wf_amax(ix, ix + 1)
                else:
                    if ix == 4 * ECH:
                        emit_wf_swh()
                    emit_wf_quant(ix - 4 * ECH, ix - 4 * ECH + 1)
        for tt in range(NTT):
            s = c * NTT + tt
            state_update1(s)
    OP("dv", lambda e: e.tensor_copy(S_acc[:, :], S_ps[:, :]),
       rd=[buf("S_ps")], wr=[buf("S_acc")], tag="p1sfin")

    def wload_bf(tgt, tgt_buf, src_d, tag, replaces):
        b = buf(replaces)
        deps = list(b.r) + ([b.w] if b.w else [])
        src_buf = {id(wq_bf_d): "wq_bf_d", id(wu_bf_d): "wu_bf_d"}[id(src_d)]
        bb = buf(src_buf)
        deps += [bb.w] + list(bb.r)
        for ec in range(ECH):
            dma("sy", tgt[:, :].rearrange("p (ec i) -> p ec i", ec=ECH)[:, ec, :],
                src_d.rearrange("(ec p) i -> p ec i", p=128)[:, ec, :],
                [], [buf(tgt_buf)], deps=deps, tag=f"{tag}_{ec}")
    wload_bf(wA, "wA", wq_bf_d, "ldwq", replaces="wA")
    wload_bf(wB, "wB", wu_bf_d, "ldwu", replaces="wB")
    emit_pk("q", wA, buf("wA"))
    emit_pk("u", wB, buf("wB"))

    OP("dv", lambda e: e.tensor_scalar(ssb[0][:, :], S_acc[:, :], emask[:, :], None, ALU.mult),
       rd=[buf("S_acc"), buf("emask")], wr=[buf("ssb0")], tag="ccmask")
    dma("gp", ccs_in_d[:, :], ssb[0][:, :], [buf("ssb0")], [buf("ccs_in")], tag="ccst")
    OP("gp", lambda e: e.collective_compute(
        "AllReduce", ALU.add, replica_groups=[[i, i + 1] for i in range(0, n_cores, 2)],
        ins=[ccs_in_d.ap().opt()], outs=[ccs_out_d.ap().opt()]),
        rd=[buf("ccs_in")], wr=[buf("ccs_out")], skey=("cc", False), tag="cc1")
    dma("gp", ssb[0][:, :], ccs_out_d[:, :], [buf("ccs_out")], [buf("ssb0")], tag="ccld")
    OP("dv", lambda e: e.tensor_scalar(sinit[:, :], ssb[0][:, :], omask[:, :], None, ALU.mult),
       rd=[buf("ssb0"), buf("omask")], wr=[buf("sinit")], tag="sinit")

    ln1_rows(0, 2)
    make_xn(0, 2)
    for c in range(NCH):
        tg = f"p2c{c}"
        proj_fm(c, buf("wA"), wA, "q", q_s, buf("q_s"), "elu", f"{tg}q")
        proj_fm(c, buf("wB"), wB, "u", us_s, buf("us_s"), "silu", f"{tg}u")
        for tt in range(NTT):
            s = c * NTT + tt
            kb, bkb = kfm_sub[0], buf("kfm0")
            for half in range(2):
                def ktr(e, s=s, half=half):
                    inst = None
                    for j in range(4):
                        h = half * 4 + j
                        inst = e.matmul(pT1[:, j * 128:(j + 1) * 128],
                                        ktm_st[:, s * E + h * D: s * E + (h + 1) * D],
                                        ident_bf[:, :], start=True, stop=True)
                    return inst
                OP("pe", ktr, rd=[buf("ktm_st"), buf("ident_bf")], wr=[buf("pT1")],
                   tag=f"{tg}kT{tt}_{half}")
                OP("sc", (lambda kb=kb, half=half: (lambda e: e.activation(
                    kb[:, half * 512:(half + 1) * 512], pT1[:, 0:512], AF.Identity)))(),
                   rd=[buf("pT1")], wr=[bkb], tag=f"{tg}kE{tt}_{half}")
            sb_t, bsb = ssb[0], buf("ssb0")
            if s > 0:
                state_accum(s - 1, 2, reset=(s == 1))
                OP("sc", (lambda sb_t=sb_t: (lambda e: e.activation(sb_t[:, :], S_acc[:, :],
                                                                    AF.Identity)))(),
                   rd=[buf("S_acc")], wr=[bsb], tag=f"{tg}sc{tt}")
            def eTg(e, s=s, tt=tt, kb=kb):
                inst = None
                for h in range(H):
                    inst = e.matmul(pBig[:, h * 128:(h + 1) * 128],
                                    kb[:, h * 128:(h + 1) * 128],
                                    q_s[:, h * C + tt * 128: h * C + (tt + 1) * 128],
                                    start=True, stop=True)
                return inst
            OP("pe", eTg, rd=[bkb, buf("q_s")], wr=[buf("pBig")], tag=f"{tg}eT{tt}")
            eb, beb = eT_s[0], buf("eT0")
            OP("dv", (lambda eb=eb: (lambda e: e.tensor_tensor(
                eb[:, :].rearrange("p (h n) -> p h n", h=H),
                pBig[:, :].rearrange("p (h n) -> p h n", h=H),
                triu_f[:, :].rearrange("p (o n) -> p o n", o=1).broadcast_to([128, H, 128]),
                ALU.mult)))(),
               rd=[buf("pBig"), buf("triu_f")], wr=[beb], tag=f"{tg}msk{tt}")
            def outg(e, s=s, tt=tt, eb=eb, sb_t=sb_t):
                inst = None
                for h in range(H):
                    inst = e.matmul(pBig[:, h * 128:(h + 1) * 128],
                                    vtm_st[:, s * E + h * D: s * E + (h + 1) * D],
                                    eb[:, h * 128:(h + 1) * 128], start=True, stop=False)
                    if s > 0:
                        inst = e.matmul(pBig[:, h * 128:(h + 1) * 128],
                                        sb_t[:, h * D:(h + 1) * D],
                                        q_s[:, h * C + tt * 128: h * C + (tt + 1) * 128],
                                        start=False, stop=False)
                    inst = e.matmul(pBig[:, h * 128:(h + 1) * 128],
                                    sinit[:, h * D:(h + 1) * D],
                                    q_s[:, h * C + tt * 128: h * C + (tt + 1) * 128],
                                    start=False, stop=True)
                return inst
            rd_o = [beb, buf("vtm_st"), buf("q_s"), buf("sinit")] + ([bsb] if s > 0 else [])
            OP("pe", outg, rd=rd_o, wr=[buf("pBig")], tag=f"{tg}out{tt}")
            if s < NSUB - 1:
                state_update(s, 2)
            OP("dv", (lambda tt=tt: (lambda e: e.tensor_tensor(
                g1_s[tt][:, :].rearrange("p (h n) -> p h n", h=H),
                pBig[:, :].rearrange("p (h n) -> p h n", h=H),
                us_s[:, :].rearrange("p (h cc) -> p h cc", h=H)[:, :, tt * 128:(tt + 1) * 128],
                ALU.mult)))(),
               rd=[buf("pBig"), buf("us_s")], wr=[buf(f"g1_{tt}")], tag=f"{tg}g1{tt}")
            OP("sc", lambda e: e.activation(eT_s[0][:, :], pBig[:, :], AF.Square),
               rd=[buf("pBig")], wr=[buf("eT0")], tag=f"{tg}sq{tt}")
            def ssqg(e, tt=tt):
                inst = None
                for h in range(H):
                    inst = e.matmul(pT2[0:1, tt * 128:(tt + 1) * 128], onescol[:, :],
                                    eT_s[0][:, h * 128:(h + 1) * 128],
                                    start=(h == 0), stop=(h == H - 1))
                return inst
            OP("pe", ssqg, rd=[buf("eT0"), buf("onescol")], wr=[buf("pT2")], tag=f"{tg}ssq{tt}")
            if tt == 0 and c + 1 < NCH:
                ln1_rows(c + 1, 2)
                make_xn(c + 1, 2)
        OP("dv", lambda e: e.tensor_copy(e_scr[0:1, 0:C], pT2[0:1, 0:C]),
           rd=[buf("pT2")], wr=[buf("e_scr")], tag=f"{tg}sse")
        def ssqT(e):
            inst = None
            for tt in range(NTT):
                inst = e.matmul(pT1[:, 256 + tt:257 + tt],
                                e_scr[0:1, tt * 128:(tt + 1) * 128],
                                ones11[0:1, 0:1], start=True, stop=True)
            return inst
        OP("pe", ssqT, rd=[buf("e_scr"), buf("ones11")], wr=[buf("pT1")], tag=f"{tg}ssT")
        OP("dv", lambda e: e.tensor_copy(nsc2[:, 0:NTT], pT1[:, 256:256 + NTT]),
           rd=[buf("pT1")], wr=[buf("nsc2")], tag=f"{tg}ssTe")
        OP("dv", lambda e: e.tensor_copy(rcp_t[:, :], nsc2[:, 0:NTT]),
           rd=[buf("nsc2")], wr=[buf("rcp_t")], tag=f"{tg}sscp")
        newton_rsqrt(buf("rcp_t"), rcp_t[:, :], buf("nsc2"), nsc2[:, 0:NTT], 0.0, f"{tg}nr")
        OP("dv", lambda e: e.tensor_mul(rcp_t[:, :], rcp_t[:, :], nsc2[:, 0:NTT]),
           rd=[buf("rcp_t"), buf("nsc2")], wr=[buf("rcp_t")], tag=f"{tg}rm1")
        OP("dv", lambda e: e.tensor_scalar(rcp_t[:, :], rcp_t[:, :], 1.0 / 64.0, RMS_EPS,
                                           ALU.mult, ALU.add),
           rd=[buf("rcp_t")], wr=[buf("rcp_t")], tag=f"{tg}rm2")
        OP("dv", lambda e: e.tensor_copy(rmse_bf[:, :], rcp_t[:, :]),
           rd=[buf("rcp_t")], wr=[buf("rmse_bf")], tag=f"{tg}rmb")
        OP("dv", lambda e: e.reciprocal(rcp_t[:, :], rcp_t[:, :]),
           rd=[buf("rcp_t")], wr=[buf("rcp_t")], tag=f"{tg}rcp")
        for tt in range(NTT):
            OP("pe", (lambda tt=tt: (lambda e: e.matmul(
                pT1[0:1, tt * 128:(tt + 1) * 128], rmse_bf[:, tt:tt + 1], ident_bf[:, :],
                start=True, stop=True)))(),
               rd=[buf("rmse_bf"), buf("ident_bf")], wr=[buf("pT1")], tag=f"{tg}rT{tt}")
        OP("dv", lambda e: e.tensor_copy(rrow[0:1, :], pT1[0:1, 0:C]),
           rd=[buf("pT1")], wr=[buf("rrow")], tag=f"{tg}rTe")
        for tt in range(NTT):
            s = c * NTT + tt
            xt, bxt = x_tile[tt % 2], buf(f"x_tile{tt % 2}")
            dma("sy", xt[:, :], x_tm_d.rearrange("(s p) e -> p s e", p=128)[:, s, :],
                [], [bxt], tag=f"{tg}x2{tt}")
            def wog(e, tt=tt):
                e.matmul(pBig[:, 0:512], rrow[0:1, tt * 128:(tt + 1) * 128],
                         bo_row[0:1, 0:512], start=True, stop=False)
                e.matmul(pBig[:, 512:1024], rrow[0:1, tt * 128:(tt + 1) * 128],
                         bo_row[0:1, 512:1024], start=True, stop=False)
                inst = None
                for h in range(H):
                    for half in range(2):
                        inst = e.matmul(pBig[:, half * 512:(half + 1) * 512],
                                        g1_s[tt][:, h * 128:(h + 1) * 128],
                                        wC[:, h * E + half * 512: h * E + (half + 1) * 512],
                                        start=False, stop=(h == H - 1))
                return inst
            OP("pe", wog, rd=[buf(f"g1_{tt}"), buf("wC"), buf("rrow"), buf("bo_row")],
               wr=[buf("pBig")], tag=f"{tg}wo{tt}")
            r2, br2 = res2_s[tt % 2], buf(f"res2_{tt % 2}")
            OP("dv", (lambda tt=tt, xt=xt, r2=r2, s=s: (lambda e: e.scalar_tensor_tensor(
                r2[:, :], pBig[:, :], rcp_t[:, tt:tt + 1], xt[:, :], ALU.mult, ALU.add,
                accum_out=l2sum[:, s:s + 1])))(),
               rd=[buf("pBig"), buf("rcp_t"), bxt], wr=[br2, buf("l2sum")], tag=f"{tg}r2{tt}")
            OP("sc", (lambda r2=r2, s=s: (lambda e: e.activation(
                eT_s[0][:, :], r2[:, :], AF.Square, accum_out=l2sq[:, s:s + 1])))(),
               rd=[br2], wr=[buf("eT0"), buf("l2sq")], tag=f"{tg}l2q{tt}")
            OP("dv", (lambda r2=r2, s=s: (lambda e: e.reduce_max(
                l2max[:, s:s + 1], r2[:, :], axis=AX.X)))(),
               rd=[br2], wr=[buf("l2max")], tag=f"{tg}l2x{tt}")
            OP("dv", (lambda r2=r2, s=s: (lambda e: e.tensor_reduce(
                l2min[:, s:s + 1], r2[:, :], AX.X, ALU.min)))(),
               rd=[br2], wr=[buf("l2min")], tag=f"{tg}l2n{tt}")
            dma("sc", res2_d.rearrange("(s p) e -> p s e", p=128)[:, s, :], r2[:, :],
                [br2], [buf("res2_d")], tag=f"{tg}rs{tt}")

    OP("dv", lambda e: e.tensor_scalar(l2m[:, :], l2sum[:, :], 1.0 / float(E), None, ALU.mult),
       rd=[buf("l2sum")], wr=[buf("l2m")], tag="l2m")
    OP("dv", lambda e: e.tensor_mul(l2r[:, :], l2m[:, :], l2m[:, :]),
       rd=[buf("l2m")], wr=[buf("l2r")], tag="l2ms")
    OP("dv", lambda e: e.scalar_tensor_tensor(l2r[:, :], l2sq[:, :], 1.0 / float(E), l2r[:, :],
                                              ALU.mult, ALU.subtract),
       rd=[buf("l2sq"), buf("l2r")], wr=[buf("l2r")], tag="l2v")
    newton_rsqrt(buf("l2r"), l2r[:, :], buf("l2r"), l2r[:, :], LN_EPS, "l2rs")
    OP("dv", lambda e: e.tensor_sub(l2max[:, :], l2max[:, :], l2m[:, :]),
       rd=[buf("l2max"), buf("l2m")], wr=[buf("l2max")], tag="am1")
    OP("dv", lambda e: e.tensor_sub(l2min[:, :], l2m[:, :], l2min[:, :]),
       rd=[buf("l2min"), buf("l2m")], wr=[buf("l2min")], tag="am2")
    OP("dv", lambda e: e.tensor_max(l2max[:, :], l2max[:, :], l2min[:, :]),
       rd=[buf("l2max"), buf("l2min")], wr=[buf("l2max")], tag="am3")
    OP("dv", lambda e: e.tensor_mul(l2max[:, :], l2max[:, :], l2r[:, :]),
       rd=[buf("l2max"), buf("l2r")], wr=[buf("l2max")], tag="am4")
    OP("dv", lambda e: e.reduce_max(amrow[:, :], l2max[:, :], axis=AX.X),
       rd=[buf("l2max")], wr=[buf("amrow")], tag="am5")
    OP("pe", lambda e: e.matmul(pT2[0:1, 256:384], amrow[:, :], ident_f32[:, :], start=True, stop=True),
       rd=[buf("amrow"), buf("ident_f32")], wr=[buf("pT2")], tag="amT")
    OP("dv", lambda e: e.tensor_copy(wrow[:, :], pT2[0:1, 256:384]),
       rd=[buf("pT2")], wr=[buf("wrow")], tag="amTe")
    OP("dv", lambda e: e.reduce_max(scalar_sb[:, 2:3], wrow[:, :], axis=AX.X),
       rd=[buf("wrow")], wr=[buf("scalar_sb")], tag="aml")
    OP("dv", lambda e: e.tensor_copy(wrow[0:1, :], scalar_sb[0:1, 2:3].broadcast_to([1, 128])),
       rd=[buf("scalar_sb")], wr=[buf("wrow")], tag="amb")
    dma("gp", ccm_in_d[:, :], wrow[0:1, :], [buf("wrow")], [buf("ccm_in")], tag="ccmst")
    OP("gp", lambda e: e.collective_compute(
        "AllReduce", ALU.max, replica_groups=[list(range(n_cores))],
        ins=[ccm_in_d.ap().opt()], outs=[ccm_out_d.ap().opt()]),
        rd=[buf("ccm_in")], wr=[buf("ccm_out")], skey=("cc", False), tag="cc2")
    dma("gp", scalar_sb[0:1, 3:4], ccm_out_d[0:1, 0:1], [buf("ccm_out")], [buf("scalar_sb")], tag="ccmld")
    OP("dv", lambda e: e.tensor_scalar(scalar_sb[:, 4:5], scalar_sb[:, 3:4], 1e-12, None, ALU.max),
       rd=[buf("scalar_sb")], wr=[buf("scalar_sb")], tag="sx1")
    OP("dv", lambda e: e.reciprocal(scalar_sb[:, 4:5], scalar_sb[:, 4:5]),
       rd=[buf("scalar_sb")], wr=[buf("scalar_sb")], tag="sx2")
    OP("dv", lambda e: e.tensor_scalar(scalar_sb[:, 4:5], scalar_sb[:, 4:5], FP8_MAX_HALF, None, ALU.mult),
       rd=[buf("scalar_sb")], wr=[buf("scalar_sb")], tag="sx3")
    OP("dv", lambda e: e.tensor_mul(scalar_sb[:, 5:6], scalar_sb[:, 4:5], scalar_sb[:, 1:2]),
       rd=[buf("scalar_sb")], wr=[buf("scalar_sb")], tag="ssw")
    OP("dv", lambda e: e.tensor_scalar(bf_row[0:1, :], bf_row[0:1, :], scalar_sb[0:1, 5:6], None, ALU.mult),
       rd=[buf("scalar_sb"), buf("bf_row")], wr=[buf("bf_row")], tag="bfs")
    OP("pe", lambda e: e.matmul(pT2[:, 129:130], ones11[:, :], scalar_sb[0:1, 4:5], start=True, stop=True),
       rd=[buf("scalar_sb"), buf("ones11")], wr=[buf("pT2")], tag="sxB")
    OP("dv", lambda e: e.tensor_copy(sxcol[:, :], pT2[:, 129:130]),
       rd=[buf("pT2")], wr=[buf("sxcol")], tag="sxE")
    OP("pe", lambda e: e.matmul(pT2[:, 130:131], ones11[:, :], scalar_sb[0:1, 5:6], start=True, stop=True),
       rd=[buf("scalar_sb"), buf("ones11")], wr=[buf("pT2")], tag="dB")
    OP("dv", lambda e: e.tensor_copy(d_col[:, :], pT2[:, 130:131]),
       rd=[buf("pT2")], wr=[buf("d_col")], tag="dE")
    OP("dv", lambda e: e.reciprocal(d_col[:, :], d_col[:, :]),
       rd=[buf("d_col")], wr=[buf("d_col")], tag="dR")
    OP("dv", lambda e: e.tensor_scalar(qa[:, :], l2r[:, :], sxcol[:, :], None, ALU.mult),
       rd=[buf("l2r"), buf("sxcol")], wr=[buf("qa")], tag="qa")
    OP("dv", lambda e: e.scalar_tensor_tensor(qb[:, :], l2m[:, :], -1.0, qa[:, :], ALU.mult, ALU.mult),
       rd=[buf("l2m"), buf("qa")], wr=[buf("qb")], tag="qb")

    st_ops = []
    for s in range(NSUB):
        ri, bri = resin[s % 2], buf(f"x_tile{s % 2}")
        dma("sy", ri[:, :], res2_d.rearrange("(s p) e -> p s e", p=128)[:, s, :],
            [buf("res2_d")], [bri], tag=f"f3l{s}")
        xq, bxq = xq_s[s % 2], buf(f"xq{s % 2}")
        OP("sc", (lambda ri=ri, xq=xq, s=s: (lambda e: e.activation(
            xq[:, :], ri[:, :], AF.Identity, bias=qb[:, s:s + 1], scale=qa[:, s:s + 1])))(),
           rd=[bri, buf("qa"), buf("qb")], wr=[bxq], tag=f"f3q{s}")
        xqf, bxqf = xqf_s[0], buf("xqf0")
        for half in range(2):
            p, pb = (pT1, buf("pT1")) if half == 0 else (pT2, buf("pT2"))
            def xtr(e, xq=xq, half=half, p=p):
                inst = None
                for j in range(4):
                    ec = half * 4 + j
                    inst = e.matmul(p[:, j * 128:(j + 1) * 128],
                                    xq[:, ec * 128:(ec + 1) * 128], ident_bf[:, :],
                                    start=True, stop=True)
                return inst
            OP("pe", xtr, rd=[bxq, buf("ident_bf")], wr=[pb], tag=f"f3T{s}_{half}")
            OP("sc", (lambda xqf=xqf, half=half, p=p: (lambda e: e.activation(
                xqf[:, half * 512:(half + 1) * 512], p[:, 0:512], AF.Identity)))(),
               rd=[pb], wr=[bxqf], tag=f"f3E{s}_{half}")
        def ffn(e, xqf=xqf):
            e.matmul(S_ps[:, 0:512], ones_bf[0:1, 0:128], bf_row[0:1, 0:512], start=True, stop=False)
            e.matmul(S_ps[:, 512:1024], ones_bf[0:1, 0:128], bf_row[0:1, 512:1024], start=True, stop=False)
            inst = None
            for ec in range(ECH):
                for half in range(2):
                    inst = e.matmul(S_ps[:, half * 512:(half + 1) * 512],
                                    xqf[:, ec * 128:(ec + 1) * 128],
                                    wf8[:, ec * E + half * 512: ec * E + (half + 1) * 512],
                                    start=False, stop=(ec == ECH - 1))
            return inst
        OP("pe", ffn, rd=[bxqf, buf("wf8"), buf("bf_row"), buf("ones_bf")], wr=[buf("S_ps")],
           tag=f"f3m{s}")
        yb, byb = res2_s[s % 2], buf(f"res2_{s % 2}")
        OP("dv", (lambda ri=ri, yb=yb: (lambda e: e.scalar_tensor_tensor(
            yb[:, :], S_ps[:, :], d_col[:, :], ri[:, :], ALU.mult, ALU.add)))(),
           rd=[buf("S_ps"), buf("d_col"), bri], wr=[byb], tag=f"f3y{s}")
        st_ops.append(dma("sc", y_d.rearrange("(s p) e -> p s e", p=128)[:, s, :], yb[:, :],
                          [byb], [buf("y_d")], tag=f"f3s{s}"))

    OP("sy", None, deps=st_ops, tag="final")

    if max_ops is not None:
        keep = sch.ops[:max_ops]
        kept = set(id(o) for o in keep)
        sch.ops = keep
        for o in sch.ops:
            o.deps = {d for d in o.deps if id(d) in kept}
    sch.finalize()
    with ExitStack() as sem_es:
        sems = {}
        for key, name in [(("pe", False), "s_pe"), (("dv", False), "s_dv"),
                          (("sc", False), "s_ac"), (("gp", False), "s_gp"),
                          (("sy", False), "s_sy"), (("cc", False), "s_cc")]:
            sems[key] = sem_es.enter_context(nc.semaphore(name))
        for q in ("sy", "sc", "gp"):
            for j in range(Sched.NDMASEM):
                sems[(q, "dma", j)] = sem_es.enter_context(nc.semaphore(f"d_{q}{j}"))
        with nc.Block() as block:
            @block.sync
            def _(h):
                sch.emit("sy", h, sems)

            @block.tensor
            def _(h):
                sch.emit("pe", h, sems)

            @block.vector
            def _(h):
                sch.emit("dv", h, sems)

            @block.scalar
            def _(h):
                sch.emit("sc", h, sems)

            @block.gpsimd
            def _(h):
                sch.emit("gp", h, sems)
    es.close()
    return nc


def make_inputs(core_id, x_full, W, T):
    b, half = core_id // 2, core_id % 2
    xs = np.ascontiguousarray(x_full[b, half * T:(half + 1) * T, :], dtype=np.float32)
    d = {
        "x_tm": xs,
        "x_fm": np.ascontiguousarray(xs.T),
        "wq_t": np.ascontiguousarray(W["Wq"].T),
        "wk_t": np.ascontiguousarray(W["Wk"].T),
        "wv_t": np.ascontiguousarray(W["Wv"].T),
        "wu_t": np.ascontiguousarray(W["Wu"].T),
        "wo_t": np.ascontiguousarray(W["Wo"].T),
        "wf_t": np.ascontiguousarray(W["Wf"].T),
        "bq_row": np.ascontiguousarray(W["bq"].reshape(1, E)),
        "bk_row": np.ascontiguousarray(W["bk"].reshape(1, E)),
        "bv_row": np.ascontiguousarray(W["bv"].reshape(1, E)),
        "bu_row": np.ascontiguousarray(W["bu"].reshape(1, E)),
        "bo_row": np.ascontiguousarray(W["bo"].reshape(1, E)),
        "bf_row": np.ascontiguousarray(W["bf"].reshape(1, E)),
        "ones_row": np.ones((1, 512), dtype=np.float32),
        "scale_pt": np.ascontiguousarray(W["scale"].reshape(H, 128).T),
        "ident": np.eye(128, dtype=np.float32),
        "triu": np.triu(np.ones((128, 128), dtype=np.float32)),
        "onescol": np.ones((128, 1), dtype=np.float32),
        "ones11": np.ones((1, 128), dtype=np.float32),
        "emask": np.full((128, 1), 1.0 - (core_id % 2), dtype=np.float32),
        "omask": np.full((128, 1), float(core_id % 2), dtype=np.float32),
    }
    return d


def kernel(L0_attn_x, L0_attn_Wq, L0_attn_bq, L0_attn_Wk, L0_attn_bk,
           L0_attn_Wv, L0_attn_bv, L0_attn_Wu, L0_attn_bu, L0_attn_Wo,
           L0_attn_bo, L0_attn_scale, L0_ffn_weight, L0_ffn_bias):
    x = np.asarray(L0_attn_x, dtype=np.float32)
    Bn, N, _ = x.shape
    T = N // 2
    W = {
        "Wq": np.asarray(L0_attn_Wq, np.float32), "bq": np.asarray(L0_attn_bq, np.float32),
        "Wk": np.asarray(L0_attn_Wk, np.float32), "bk": np.asarray(L0_attn_bk, np.float32),
        "Wv": np.asarray(L0_attn_Wv, np.float32), "bv": np.asarray(L0_attn_bv, np.float32),
        "Wu": np.asarray(L0_attn_Wu, np.float32), "bu": np.asarray(L0_attn_bu, np.float32),
        "Wo": np.asarray(L0_attn_Wo, np.float32), "bo": np.asarray(L0_attn_bo, np.float32),
        "scale": np.asarray(L0_attn_scale, np.float32),
        "Wf": np.asarray(L0_ffn_weight, np.float32), "bf": np.asarray(L0_ffn_bias, np.float32),
    }
    n_cores = 2 * Bn
    nc = build_program(T=T, C=min(512, T), n_cores=n_cores)
    in_maps = [make_inputs(cid, x, W, T) for cid in range(n_cores)]
    res = run_bass_kernel_spmd(nc, in_maps, core_ids=list(range(n_cores)))
    y = np.empty((Bn, N, E), dtype=np.float32)
    for cid in range(n_cores):
        b, half = cid // 2, cid % 2
        y[b, half * T:(half + 1) * T, :] = res.results[cid]["y"]
    return y


# version 16
# speedup vs baseline: 1.1093x; 1.0274x over previous
import numpy as np
import concourse.bass as bass
ABLATE = set()
import concourse.mybir as mybir
from concourse.bass_utils import run_bass_kernel_spmd
from contextlib import ExitStack

F32 = mybir.dt.float32
BF16 = mybir.dt.bfloat16
F8 = mybir.dt.float8e4
I32 = mybir.dt.int32
AF = mybir.ActivationFunctionType
ALU = mybir.AluOpType
AX = mybir.AxisListType

E = 1024
H = 8
D = 128
ECH = 8
ICH = 8
LN_EPS = 1e-5
RMS_EPS = 1e-8
FP8_MAX_HALF = 224.0
LN2C = float(np.log(2.0))


class _Op:
    __slots__ = ("eng", "fn", "deps", "dma", "skey", "needs_inc", "val", "tag", "rdn", "wrn")

    def __init__(self, eng, fn, deps, dma, skey, tag):
        self.eng, self.fn, self.deps, self.dma, self.skey, self.tag = eng, fn, deps, dma, skey, tag
        self.needs_inc = False
        self.val = None
        self.rdn = ()
        self.wrn = ()


class Buf:
    __slots__ = ("w", "r", "name", "psum")

    def __init__(self, name="", psum=False):
        self.w, self.r, self.name, self.psum = None, [], name, psum


class Sched:
    SERIAL = False
    SERIAL_ENGS = set()

    def __init__(self):
        self.ops = []

    def op(self, eng, fn, rd=(), wr=(), dma=False, deps=(), skey=None, tag=""):
        dep = set()
        for b in rd:
            if b.w is not None:
                dep.add(b.w)
            if b.psum:
                for r_ in b.r:
                    if r_.eng != eng:
                        dep.add(r_)
        for b in wr:
            if b.w is not None:
                dep.add(b.w)
            dep.update(b.r)
        for d in deps:
            if d is not None:
                dep.add(d)
        o = _Op(eng, fn, dep, dma, skey or (eng, dma), tag)
        o.rdn = tuple(b.name for b in rd)
        o.wrn = tuple(b.name for b in wr)
        self.ops.append(o)
        for b in rd:
            b.r.append(o)
        for b in wr:
            b.w = o
            b.r = []
        return o

    NDMASEM = 8
    RELAX_SAME_ENGINE = {("pe", False), ("sc", False)}

    def _skip(self, d, o):
        return (d.skey == o.skey and d.skey in self.RELAX_SAME_ENGINE
                and not d.dma and not o.dma)

    def finalize(self):
        if Sched.SERIAL:
            prev = None
            for o in self.ops:
                if o.fn is not None and prev is not None:
                    o.deps.add(prev)
                if o.fn is not None:
                    prev = o
        if Sched.SERIAL_ENGS:
            prev_any = None
            last_chained = None
            for o in self.ops:
                if o.fn is None:
                    continue
                if o.eng in Sched.SERIAL_ENGS:
                    if prev_any is not None:
                        o.deps.add(prev_any)
                    last_chained = o
                elif last_chained is not None:
                    o.deps.add(last_chained)
                prev_any = o
        rot = {}
        for o in self.ops:
            if o.dma:
                q = o.skey
                j = rot.get(q, 0)
                rot[q] = (j + 1) % self.NDMASEM
                o.skey = (q[0], "dma", j)
                o.needs_inc = True
        for o in self.ops:
            o.deps = {d for d in o.deps if not self._skip(d, o)}
        for o in self.ops:
            o.needs_inc = o.needs_inc and False or o.needs_inc
        for o in self.ops:
            o.needs_inc = bool(o.dma)
        for o in self.ops:
            for d in o.deps:
                d.needs_inc = True
        cnt = {}
        for o in self.ops:
            if o.needs_inc:
                cnt[o.skey] = cnt.get(o.skey, 0) + (16 if o.dma else 1)
                o.val = cnt[o.skey]

    def emit(self, eng_name, handle, sems):
        water = {}
        for o in self.ops:
            if o.eng != eng_name:
                continue
            waits = {}
            for d in o.deps:
                if d.val is None:
                    raise RuntimeError(f"dep without inc: {d.tag} <- {o.tag}")
                waits[d.skey] = max(waits.get(d.skey, 0), d.val)
            if o.dma and o.val >= 32:
                waits[o.skey] = max(waits.get(o.skey, 0), o.val - 16)
            for key in sorted(waits, key=str):
                v = waits[key]
                if water.get(key, 0) < v:
                    handle.wait_ge(sems[key], v)
                    water[key] = v
            if o.fn is None:
                continue
            inst = o.fn(handle)
            if o.needs_inc:
                inst.then_inc(sems[o.skey], 16 if o.dma else 1)


def build_program(T=2048, C=512, n_cores=8, max_ops=None):
    NCH = T // C
    NTT = C // 128
    NSUB = T // 128
    nc = bass.Bass("TRN2", target_bir_lowering=False, detect_race_conditions=False)

    x_tm_d = nc.dram_tensor("x_tm", [T, E], F32, kind="ExternalInput")
    x_fm_d = nc.dram_tensor("x_fm", [E, T], F32, kind="ExternalInput")
    w_d = {n: nc.dram_tensor(n, [E, E], F32, kind="ExternalInput")
           for n in ("wq_t", "wk_t", "wv_t", "wu_t", "wo_t", "wf_t")}
    bias_d = {n: nc.dram_tensor(n, [1, E], F32, kind="ExternalInput")
              for n in ("bq_row", "bk_row", "bv_row", "bu_row")}
    bo_row_d = nc.dram_tensor("bo_row", [1, E], F32, kind="ExternalInput")
    bf_row_d = nc.dram_tensor("bf_row", [1, E], F32, kind="ExternalInput")
    scale_d = nc.dram_tensor("scale_pt", [128, H], F32, kind="ExternalInput")
    ident_d = nc.dram_tensor("ident", [128, 128], F32, kind="ExternalInput")
    triu_d = nc.dram_tensor("triu", [128, 128], F32, kind="ExternalInput")
    ones_d = nc.dram_tensor("ones_row", [1, 512], F32, kind="ExternalInput")
    onescol_d = nc.dram_tensor("onescol", [128, 1], F32, kind="ExternalInput")
    ones11_d = nc.dram_tensor("ones11", [1, 128], F32, kind="ExternalInput")
    emask_d = nc.dram_tensor("emask", [128, 1], F32, kind="ExternalInput")
    omask_d = nc.dram_tensor("omask", [128, 1], F32, kind="ExternalInput")
    y_d = nc.dram_tensor("y", [T, E], F32, kind="ExternalOutput")

    res2_d = nc.dram_tensor("res2_scratch", [T, E], F32)
    wq_bf_d = nc.dram_tensor("wq_bf", [E, E], BF16)
    wu_bf_d = nc.dram_tensor("wu_bf", [E, E], BF16)
    ccs_in_d = nc.dram_tensor("ccs_in", [128, H * D], BF16)
    ccs_out_d = nc.dram_tensor("ccs_out", [128, H * D], BF16)
    ccm_in_d = nc.dram_tensor("ccm_in", [1, 128], F32)
    ccm_out_d = nc.dram_tensor("ccm_out", [1, 128], F32)

    es = ExitStack()
    sbt = lambda name, shape, dt: es.enter_context(nc.sbuf_tensor(name, shape, dt))
    pst = lambda name, shape: es.enter_context(nc.psum_tensor(name, shape, F32))

    ktm_st = sbt("ktm_st", [128, NSUB * E], BF16)
    vtm_st = sbt("vtm_st", [128, NSUB * E], BF16)
    wA = sbt("wA", [128, ECH * E], BF16)
    wB = sbt("wB", [128, ECH * E], BF16)
    wC = sbt("wC", [128, ECH * E], BF16)
    wf8 = sbt("wf8", [128, ECH * E], F8)
    wf32 = sbt("wf32", [128, 256], F32)
    pkA = sbt("pkA", [34, E], BF16)
    pkB = sbt("pkB", [34, E], BF16)
    pk = {"q": (pkA, 0), "k": (pkA, 32), "v": (pkB, 0), "u": (pkB, 32)}
    rows_dyn = [sbt(f"rows_dyn{i}", [34, 512], BF16) for i in range(2)]
    ones_bf = sbt("ones_bf", [1, 128], BF16)
    bo_row = sbt("bo_row_s", [1, E], BF16)
    bf_row = sbt("bf_row_s", [1, E], BF16)
    scale_pt = sbt("scale_pt_s", [128, H], F32)
    sclh = sbt("sclh", [128, H], F32)
    ident_bf = sbt("ident_bf", [128, 128], BF16)
    ident_f32 = sbt("ident_f32", [128, 128], F32)
    triu_f = sbt("triu_f", [128, 128], BF16)
    onescol = sbt("onescol_s", [128, 1], BF16)
    ones11 = sbt("ones11_s", [1, 128], F32)
    emask = sbt("emask_s", [128, 1], F32)
    omask = sbt("omask_s", [128, 1], F32)

    xf32 = [sbt(f"xf32_{i}", [128, 512], F32) for i in range(2)]
    xn = sbt("xn", [128, ECH * C], BF16)
    rbc = sbt("rbc", [128, C], BF16)
    x_tile = [sbt(f"x_tile{i}", [128, E], F32) for i in range(2)]
    q_s = sbt("q_s", [128, H * C], BF16)
    us_s = sbt("us_s", [128, H * C], BF16)
    e_scr = sbt("e_scr", [128, 512], F32)
    t2_scr = sbt("t2_scr", [128, 512], BF16)
    rrow = sbt("rrow", [1, C], BF16)
    statc = sbt("statc", [128, 1], BF16)

    kfm_sub = [sbt("kfm0", [128, E], BF16)]
    eT_s = [sbt("eT0", [128, H * 128], BF16)]
    S_acc = sbt("S_acc", [128, H * D], F32)
    ssb = [sbt("ssb0", [128, H * D], BF16)]
    sinit = sbt("sinit", [128, H * D], BF16)
    g1_s = [sbt(f"g1_{i}", [128, E], BF16) for i in range(NTT)]
    res2_s = [sbt(f"res2_{i}", [128, E], F32) for i in range(2)]
    resin = x_tile

    xq_s = [sbt(f"xq{i}", [128, E], F8) for i in range(2)]
    xqf_s = [sbt("xqf0", [128, E], F8)]

    bn6 = sbt("bn6", [128, 12], F32)
    bn2 = sbt("bn2", [128, 2], F32)
    l1m = sbt("l1m", [128, NSUB], F32)
    l1v = sbt("l1v", [128, NSUB], F32)
    l1r = sbt("l1r", [128, NSUB], F32)
    l1mr = sbt("l1mr", [128, NSUB], F32)
    l2sum = sbt("l2sum", [128, NSUB], F32)
    l2sq = sbt("l2sq", [128, NSUB], F32)
    l2max = sbt("l2max", [128, NSUB], F32)
    l2min = sbt("l2min", [128, NSUB], F32)
    l2m = sbt("l2m", [128, NSUB], F32)
    l2r = sbt("l2r", [128, NSUB], F32)
    qa = sbt("qa", [128, NSUB], F32)
    qb = sbt("qb", [128, NSUB], F32)
    nsc = sbt("nsc", [128, NSUB], F32)
    nsc2 = sbt("nsc2", [128, NSUB], F32)
    nsc3 = sbt("nsc3", [128, NSUB], F32)
    exp_in = nsc2
    rcp_t = sbt("rcp_t", [128, NTT], F32)
    rmse_bf = sbt("rmse_bf", [128, NTT], BF16)
    wmax = sbt("wmax", [128, 4 * ECH], F32)
    wmax1 = sbt("wmax1", [128, 1], F32)
    wrow = sbt("wrow", [1, 128], F32)
    scalar_sb = sbt("scalar_sb", [1, 8], F32)
    swh_col = sbt("swh_col", [128, 1], F32)
    sxcol = sbt("sxcol", [128, 1], F32)
    d_col = sbt("d_col", [128, 1], F32)
    amrow = sbt("amrow", [128, 1], F32)
    seedb = sbt("seedb", [128, 1], F32)

    S_ps = pst("S_ps", [128, 1024])
    pBig = pst("pBig", [128, 1024])
    pS1 = pst("pS1", [128, 512])
    pS2 = pst("pS2", [128, 512])
    pT1 = pst("pT1", [128, 512])
    pT2 = pst("pT2", [128, 512])

    B = {}
    PSUM_BUFS = {"S_ps", "pBig", "pS1", "pS2", "pT1", "pT2"}

    def buf(name):
        if name not in B:
            B[name] = Buf(name, psum=(name in PSUM_BUFS))
        return B[name]

    sch = Sched()
    OP = sch.op

    def dma(eng, out_ap, in_ap, rd, wr, deps=(), tag=""):
        return OP(eng, (lambda o, i: (lambda e: e.dma_start(out=o, in_=i)))(out_ap, in_ap),
                  rd=rd, wr=wr, dma=True, deps=deps, tag=tag)

    for n, key in (("q", "bq_row"), ("k", "bk_row"), ("v", "bv_row"), ("u", "bu_row")):
        t, po = pk[n]
        dma("gp", t[po + 1:po + 2, :], bias_d[key][:, :], [], [buf(f"pk_{n}_b")], tag=f"ldb{n}")
    for i in range(2):
        dma("gp", rows_dyn[i][1:2, :], ones_d[:, :], [], [buf(f"rows_ones{i}")], tag=f"ldro{i}")
        dma("gp", rows_dyn[i][33:34, :], ones_d[:, :], [], [buf(f"rows_ones{i}")], tag=f"ldro32_{i}")
    dma("gp", bo_row[:, :], bo_row_d[:, :], [], [buf("bo_row")], tag="ldbo")
    dma("gp", bf_row[:, :], bf_row_d[:, :], [], [buf("bf_row")], tag="ldbf")
    dma("gp", ident_bf[:, :], ident_d[:, :], [], [buf("ident_bf")], tag="ldidb")
    dma("gp", ones_bf[:, :], ones_d[0:1, 0:128], [], [buf("ones_bf")], tag="ldonesbf")
    dma("gp", onescol[:, :], onescol_d[:, :], [], [buf("onescol")], tag="ldoc")
    dma("sy", ident_f32[:, :], ident_d[:, :], [], [buf("ident_f32")], tag="ldidf")
    dma("gp", triu_f[:, :], triu_d[:, :], [], [buf("triu_f")], tag="ldtriu")
    dma("sy", scale_pt[:, :], scale_d[:, :], [], [buf("scale_pt")], tag="ldscale")
    dma("sy", ones11[:, :], ones11_d[:, :], [], [buf("ones11")], tag="ldo11")
    dma("sy", emask[:, :], emask_d[:, :], [], [buf("emask")], tag="ldem")
    dma("sy", omask[:, :], omask_d[:, :], [], [buf("omask")], tag="ldom")
    OP("dv", lambda e: e.tensor_scalar(sclh[:, :], scale_pt[:, :], 0.25, None, ALU.mult),
       rd=[buf("scale_pt")], wr=[buf("sclh")], tag="sclh")
    OP("dv", lambda e: e.memset(seedb[:, :], float(LN2C * 63.5)),
       rd=[], wr=[buf("seedb")], tag="seedb")

    def wload(wname, tgt, tgt_buf, src, tag, replaces=None):
        deps = []
        if replaces is not None:
            b = buf(replaces)
            deps = list(b.r) + ([b.w] if b.w else [])
        last = None
        for ec in range(ECH):
            last = dma("gp", tgt[:, :].rearrange("p (ec i) -> p ec i", ec=ECH)[:, ec, :],
                       src.rearrange("(ec p) i -> p ec i", p=128)[:, ec, :],
                       [], [buf(tgt_buf)], deps=deps, tag=f"{tag}_{ec}")
        return last

    def emit_pk(name, wt, wt_buf):
        t, po = pk[name]
        for half in range(2):
            p, pb = (pT1, buf("pT1")) if half == 0 else (pT2, buf("pT2"))
            def grp(e, half=half, p=p, wt=wt, po=po):
                inst = None
                for ec in range(ECH):
                    inst = e.matmul(p[po:po + 1, 0:512], onescol[:, :],
                                    wt[:, ec * E + half * 512: ec * E + (half + 1) * 512],
                                    start=(ec == 0), stop=(ec == ECH - 1))
                return inst
            OP("pe", grp, rd=[wt_buf, buf("onescol")], wr=[pb], tag=f"pk{name}{half}")
            OP("dv", (lambda t=t, po=po, half=half, p=p: (lambda e: e.tensor_scalar(
                t[po:po + 1, half * 512:(half + 1) * 512], p[po:po + 1, 0:512], -1.0, None, ALU.mult)))(),
               rd=[pb], wr=[buf(f"pk_{name}_c")], tag=f"pk{name}e{half}")



    def newton_rsqrt(src_buf, src_ap, out_buf, out_ap, eps_add, tag, iters=2):
        n = src_ap.shape[-1]
        OP("dv", lambda e: e.tensor_scalar(nsc3[:, 0:n], src_ap, float(eps_add), None, ALU.add),
           rd=[src_buf], wr=[buf("nsc3")], tag=f"{tag}e")
        OP("dv", lambda e: e.tensor_copy(exp_in[:, 0:n], nsc3[:, 0:n].bitcast(I32)),
           rd=[buf("nsc3")], wr=[buf("nsc2")], tag=f"{tag}b")
        OP("sc", lambda e: e.activation(nsc[:, 0:n], exp_in[:, 0:n], AF.Exp,
                                        bias=seedb[:, :],
                                        scale=float(-LN2C * 0.5 / (2 ** 23))),
           rd=[buf("nsc2"), buf("seedb")], wr=[buf("nsc")], tag=f"{tag}s")
        for it in range(iters):
            OP("dv", lambda e: e.tensor_mul(nsc2[:, 0:n], nsc[:, 0:n], nsc[:, 0:n]),
               rd=[buf("nsc")], wr=[buf("nsc2")], tag=f"{tag}{it}a")
            OP("dv", lambda e: e.tensor_mul(nsc2[:, 0:n], nsc2[:, 0:n], nsc3[:, 0:n]),
               rd=[buf("nsc2"), buf("nsc3")], wr=[buf("nsc2")], tag=f"{tag}{it}b")
            OP("dv", lambda e: e.tensor_scalar(nsc2[:, 0:n], nsc2[:, 0:n], -0.5, 1.5, ALU.mult, ALU.add),
               rd=[buf("nsc2")], wr=[buf("nsc2")], tag=f"{tag}{it}c")
            OP("dv", lambda e: e.tensor_mul(nsc[:, 0:n], nsc[:, 0:n], nsc2[:, 0:n]),
               rd=[buf("nsc"), buf("nsc2")], wr=[buf("nsc")], tag=f"{tag}{it}d")
        OP("dv", lambda e: e.tensor_copy(out_ap, nsc[:, 0:n]),
           rd=[buf("nsc")], wr=[out_buf], tag=f"{tag}o")

    def col_to_row(col_buf, col_ap, p, p_buf, c0, tag):
        OP("dv", lambda e: e.tensor_copy(statc[:, :], col_ap),
           rd=[col_buf], wr=[buf("statc")], tag=f"{tag}c")
        OP("pe", lambda e: e.matmul(p[0:1, c0:c0 + 128], statc[:, :], ident_bf[:, :],
                                    start=True, stop=True),
           rd=[buf("statc"), buf("ident_bf")], wr=[p_buf], tag=f"{tag}T")

    def ln1_rows(c, ph):
        tag = f"p{ph}c{c}"
        rdyn, brdyn = rows_dyn[c % 2], buf(f"rows_mr{c % 2}")
        for tt in range(NTT):
            s = c * NTT + tt
            col_to_row(buf("l1r"), l1r[:, s:s + 1], pS1, buf("pS1"), tt * 128, f"{tag}r{tt}")
            col_to_row(buf("l1mr"), l1mr[:, s:s + 1], pS2, buf("pS2"), tt * 128, f"{tag}g{tt}")
            OP("pe", (lambda tt=tt: (lambda e: e.matmul(
                pS2[32:33, tt * 128:(tt + 1) * 128], statc[:, :], ident_bf[:, :],
                start=True, stop=True)))(),
               rd=[buf("statc"), buf("ident_bf")], wr=[buf("pS2")], tag=f"{tag}g32_{tt}")
        OP("dv", lambda e: e.tensor_copy(rrow[0:1, :], pS1[0:1, 0:C]),
           rd=[buf("pS1")], wr=[buf("rrow")], tag=f"{tag}rev")
        OP("dv", (lambda rdyn=rdyn: (lambda e: e.tensor_copy(rdyn[0:1, :], pS2[0:1, 0:C])))(),
           rd=[buf("pS2")], wr=[brdyn], tag=f"{tag}mrev")
        OP("dv", (lambda rdyn=rdyn: (lambda e: e.tensor_copy(rdyn[32:33, :], pS2[32:33, 0:C])))(),
           rd=[buf("pS2")], wr=[brdyn], tag=f"{tag}mrev32")
        OP("pe", lambda e: e.matmul(pS1[:, 0:C], ones_bf[0:1, 0:128], rrow[0:1, :], start=True, stop=True),
           rd=[buf("rrow"), buf("ones_bf")], wr=[buf("pS1")], tag=f"{tag}rbcM")
        OP("dv", lambda e: e.tensor_copy(rbc[:, :], pS1[:, 0:C]),
           rd=[buf("pS1")], wr=[buf("rbc")], tag=f"{tag}rbc")

    def make_xn(c, ph):
        tag = f"p{ph}c{c}"
        for ec in range(ECH):
            xb, bxb = xf32[ec % 2], buf(f"xf32_{ec % 2}")
            dma("sy", xb[:, :], x_fm_d.rearrange("(ec p) t -> p ec t", p=128)[:, ec, c * C:(c + 1) * C],
                [], [bxb], tag=f"{tag}xld{ec}")
            OP("dv", (lambda xb_=xb, ec_=ec: (lambda e: e.tensor_mul(
                xn[:, ec_ * C:(ec_ + 1) * C], xb_[:, :], rbc[:, :])))(),
               rd=[bxb, buf("rbc")], wr=[buf("xn")], tag=f"{tag}xr{ec}")

    def proj_fm(c, wt_buf, wt, pkn, out_s, out_buf, kind, tag):
        rdyn, brdyn = rows_dyn[c % 2], buf(f"rows_mr{c % 2}")
        bpk = [buf(f"pk_{pkn}_c"), buf(f"pk_{pkn}_b"), brdyn, buf(f"rows_ones{c % 2}")]
        for ic in range(ICH):
            p, pb = (pS1, buf("pS1")) if ic % 2 == 0 else (pS2, buf("pS2"))
            pkt, po = pk[pkn]
            def group(e, ic=ic, p=p, rdyn=rdyn, pkt=pkt, po=po):
                e.matmul(p[:, 0:C], pkt[po:po + 2, ic * 128:(ic + 1) * 128],
                         rdyn[po:po + 2, 0:C], start=True, stop=False)
                inst = None
                for ec in range(ECH):
                    inst = e.matmul(p[:, 0:C], wt[:, ec * E + ic * 128: ec * E + (ic + 1) * 128],
                                    xn[:, ec * C:(ec + 1) * C], start=False, stop=(ec == ECH - 1))
                return inst
            OP("pe", group, rd=[wt_buf, buf("xn")] + bpk, wr=[pb], tag=f"{tag}mm{ic}")
            if kind == "elu":
                OP("sc", (lambda p=p: (lambda e: e.activation(e_scr[:, 0:C], p[:, 0:C], AF.Exp)))(),
                   rd=[pb], wr=[buf("e_scr")], tag=f"{tag}ex{ic}")
                OP("dv", lambda e: e.tensor_scalar(t2_scr[:, 0:C], e_scr[:, 0:C], 1.0, 0.0,
                                                   ALU.subtract, ALU.min),
                   rd=[buf("e_scr")], wr=[buf("t2_scr")], tag=f"{tag}mn{ic}")
                OP("dv", (lambda p=p, ic=ic: (lambda e: e.scalar_tensor_tensor(
                    out_s[:, ic * C:(ic + 1) * C], p[:, 0:C], 0.0, t2_scr[:, 0:C], ALU.max, ALU.add)))(),
                   rd=[pb, buf("t2_scr")], wr=[out_buf], tag=f"{tag}el{ic}")
            else:
                OP("sc", (lambda p=p: (lambda e: e.activation(e_scr[:, 0:C], p[:, 0:C], AF.Tanh, scale=0.5)))(),
                   rd=[pb], wr=[buf("e_scr")], tag=f"{tag}th{ic}")
                OP("dv", (lambda p=p, ic=ic: (lambda e: e.tensor_scalar(
                    t2_scr[:, 0:C], p[:, 0:C], sclh[:, ic:ic + 1], None, ALU.mult)))(),
                   rd=[pb, buf("sclh")], wr=[buf("t2_scr")], tag=f"{tag}w1{ic}")
                OP("dv", (lambda ic=ic: (lambda e: e.scalar_tensor_tensor(
                    out_s[:, ic * C:(ic + 1) * C], e_scr[:, 0:C], 1.0, t2_scr[:, 0:C], ALU.add, ALU.mult)))(),
                   rd=[buf("e_scr"), buf("t2_scr")], wr=[out_buf], tag=f"{tag}us{ic}")

    def proj_tm_tile(c, tt, wt_buf, wt, pkn, out_st, out_buf, kind, tag, pj):
        if True:
            s = c * NTT + tt
            rdyn, brdyn = rows_dyn[c % 2], buf(f"rows_mr{c % 2}")
            bpk = [buf(f"pk_{pkn}_c"), buf(f"pk_{pkn}_b"), brdyn, buf(f"rows_ones{c % 2}")]
            if pj % 2 == 0:
                halves = [(pBig[:, 0:512], buf("pBig")), (pBig[:, 512:1024], buf("pBig"))]
            else:
                halves = [(pS1[:, :], buf("pS1")), (pS2[:, :], buf("pS2"))]
            hbufs = list({id(b): b for _, b in halves}.values())
            pkt, po = pk[pkn]
            def group(e, tt=tt, halves=halves, rdyn=rdyn, pkt=pkt, po=po):
                for half in range(2):
                    e.matmul(halves[half][0], rdyn[po:po + 2, tt * 128:(tt + 1) * 128],
                             pkt[po:po + 2, half * 512:(half + 1) * 512],
                             start=True, stop=False)
                inst = None
                for ec in range(ECH):
                    for half in range(2):
                        inst = e.matmul(halves[half][0],
                                        xn[:, ec * C + tt * 128: ec * C + (tt + 1) * 128],
                                        wt[:, ec * E + half * 512: ec * E + (half + 1) * 512],
                                        start=False, stop=(ec == ECH - 1))
                return inst
            OP("pe", group, rd=[wt_buf, buf("xn")] + bpk, wr=hbufs,
               tag=f"{tag}mm{tt}")
            for hf in range(2):
                pap, pb = halves[hf]
                if kind == "elu":
                    OP("sc", (lambda pap=pap: (lambda e: e.activation(
                        e_scr[:, :], pap, AF.Exp)))(),
                       rd=[pb], wr=[buf("e_scr")], tag=f"{tag}ex{tt}_{hf}")
                    OP("dv", lambda e: e.tensor_scalar(t2_scr[:, :], e_scr[:, :], 1.0, 0.0,
                                                       ALU.subtract, ALU.min),
                       rd=[buf("e_scr")], wr=[buf("t2_scr")], tag=f"{tag}mn{tt}_{hf}")
                    OP("dv", (lambda s=s, hf=hf, pap=pap: (lambda e: e.scalar_tensor_tensor(
                        out_st[:, s * E + hf * 512: s * E + (hf + 1) * 512],
                        pap, 0.0, t2_scr[:, :], ALU.max, ALU.add)))(),
                       rd=[pb, buf("t2_scr")], wr=[out_buf], tag=f"{tag}el{tt}_{hf}")
                else:
                    OP("sc", (lambda pap=pap: (lambda e: e.activation(
                        e_scr[:, :], pap, AF.Tanh, scale=0.5)))(),
                       rd=[pb], wr=[buf("e_scr")], tag=f"{tag}th{tt}_{hf}")
                    OP("dv", (lambda s=s, hf=hf, pap=pap: (lambda e: e.scalar_tensor_tensor(
                        out_st[:, s * E + hf * 512: s * E + (hf + 1) * 512], e_scr[:, :], 1.0,
                        pap, ALU.add, ALU.mult)))(),
                       rd=[buf("e_scr"), pb], wr=[out_buf], tag=f"{tag}v{tt}_{hf}")

    def state_update1(s):
        def group(e, s=s):
            inst = None
            for h in range(H):
                inst = e.matmul(S_ps[:, h * D:(h + 1) * D],
                                ktm_st[:, s * E + h * D: s * E + (h + 1) * D],
                                vtm_st[:, s * E + h * D: s * E + (h + 1) * D],
                                start=(s == 0 and h in (0, 4)),
                                stop=(s == NSUB - 1 and h in (3, 7)))
            return inst
        OP("pe", group, rd=[buf("ktm_st"), buf("vtm_st")], wr=[buf("S_ps")], tag=f"p1st{s}")

    def state_update(s, ph):
        def group(e, s=s):
            inst = None
            for h in range(H):
                inst = e.matmul(S_ps[:, h * D:(h + 1) * D],
                                ktm_st[:, s * E + h * D: s * E + (h + 1) * D],
                                vtm_st[:, s * E + h * D: s * E + (h + 1) * D],
                                start=(h in (0, 4)), stop=(h in (3, 7)))
            return inst
        OP("pe", group, rd=[buf("ktm_st"), buf("vtm_st")], wr=[buf("S_ps")], tag=f"p{ph}st{s}")

    def state_accum(s, ph, reset):
        if reset:
            OP("dv", lambda e: e.tensor_copy(S_acc[:, :], S_ps[:, :]),
               rd=[buf("S_ps")], wr=[buf("S_acc")], tag=f"p{ph}sa{s}")
        else:
            OP("dv", lambda e: e.tensor_add(S_acc[:, :], S_acc[:, :], S_ps[:, :]),
               rd=[buf("S_ps"), buf("S_acc")], wr=[buf("S_acc")], tag=f"p{ph}sa{s}")

    def ln1_stats(c):
        tag = f"p1c{c}"
        for tt in range(NTT):
            s = c * NTT + tt
            xt, bxt = x_tile[tt % 2], buf(f"x_tile{tt % 2}")
            dma("sy", xt[:, :], x_tm_d.rearrange("(s p) e -> p s e", p=128)[:, s, :],
                [], [bxt], tag=f"{tag}xt{tt}")
            OP("dv", (lambda xt=xt: (lambda e: e.bn_stats(bn6[:, 0:6], xt[:, 0:512])))(),
               rd=[bxt], wr=[buf("bn6")], tag=f"{tag}bnA{tt}")
            OP("dv", (lambda xt=xt: (lambda e: e.bn_stats(bn6[:, 6:12], xt[:, 512:1024])))(),
               rd=[bxt], wr=[buf("bn6")], tag=f"{tag}bnB{tt}")
            OP("dv", lambda e: e.bn_aggr(bn2[:, :], bn6[:, :]),
               rd=[buf("bn6")], wr=[buf("bn2")], tag=f"{tag}ag{tt}")
            OP("dv", (lambda s=s: (lambda e: e.tensor_copy(l1m[:, s:s + 1], bn2[:, 0:1])))(),
               rd=[buf("bn2")], wr=[buf("l1m")], tag=f"{tag}mc{tt}")
            OP("dv", (lambda s=s: (lambda e: e.tensor_copy(l1v[:, s:s + 1], bn2[:, 1:2])))(),
               rd=[buf("bn2")], wr=[buf("l1v")], tag=f"{tag}vc{tt}")
        newton_rsqrt(buf("l1v"), l1v[:, c * NTT:(c + 1) * NTT], buf("l1r"),
                     l1r[:, c * NTT:(c + 1) * NTT], LN_EPS, f"{tag}rs")
        OP("dv", (lambda c=c: (lambda e: e.tensor_mul(
            l1mr[:, c * NTT:(c + 1) * NTT], l1m[:, c * NTT:(c + 1) * NTT],
            l1r[:, c * NTT:(c + 1) * NTT])))(),
           rd=[buf("l1m"), buf("l1r")], wr=[buf("l1mr")], tag=f"{tag}mr")

    def emit_wf_amax(lo, hi):
        for ix in range(lo, hi):
            ec, qr = ix // 4, ix % 4
            dma("sy", wf32[:, :],
                w_d["wf_t"].rearrange("(ec p) o -> p ec o", p=128)[:, ec, qr * 256:(qr + 1) * 256],
                [], [buf("wf32")], tag=f"wfl{ix}")
            OP("dv", (lambda ix=ix: (lambda e: e.reduce_max(wmax[:, ix:ix + 1], wf32[:, :],
                                                            axis=AX.X, apply_absolute_value=True)))(),
               rd=[buf("wf32")], wr=[buf("wmax")], tag=f"wfm{ix}")

    def emit_wf_swh():
        OP("dv", lambda e: e.reduce_max(wmax1[:, :], wmax[:, :], axis=AX.X),
           rd=[buf("wmax")], wr=[buf("wmax1")], tag="wfmc")
        OP("pe", lambda e: e.matmul(pT2[0:1, 0:128], wmax1[:, :], ident_f32[:, :], start=True, stop=True),
           rd=[buf("wmax1"), buf("ident_f32")], wr=[buf("pT2")], tag="wfT")
        OP("dv", lambda e: e.tensor_copy(wrow[:, :], pT2[0:1, 0:128]),
           rd=[buf("pT2")], wr=[buf("wrow")], tag="wfTe")
        OP("dv", lambda e: e.reduce_max(scalar_sb[:, 0:1], wrow[:, :], axis=AX.X),
           rd=[buf("wrow")], wr=[buf("scalar_sb")], tag="wfam")
        OP("dv", lambda e: e.tensor_scalar(scalar_sb[:, 1:2], scalar_sb[:, 0:1], 1e-12, None, ALU.max),
           rd=[buf("scalar_sb")], wr=[buf("scalar_sb")], tag="swh1")
        OP("dv", lambda e: e.reciprocal(scalar_sb[:, 1:2], scalar_sb[:, 1:2]),
           rd=[buf("scalar_sb")], wr=[buf("scalar_sb")], tag="swh2")
        OP("dv", lambda e: e.tensor_scalar(scalar_sb[:, 1:2], scalar_sb[:, 1:2], FP8_MAX_HALF, None, ALU.mult),
           rd=[buf("scalar_sb")], wr=[buf("scalar_sb")], tag="swh3")
        OP("pe", lambda e: e.matmul(pT2[:, 128:129], ones11[:, :], scalar_sb[0:1, 1:2], start=True, stop=True),
           rd=[buf("scalar_sb"), buf("ones11")], wr=[buf("pT2")], tag="swhB")
        OP("dv", lambda e: e.tensor_copy(swh_col[:, :], pT2[:, 128:129]),
           rd=[buf("pT2")], wr=[buf("swh_col")], tag="swhE")

    def emit_wf_quant(lo, hi):
        for ix in range(lo, hi):
            ec, qr = ix // 4, ix % 4
            dma("sy", wf32[:, :],
                w_d["wf_t"].rearrange("(ec p) o -> p ec o", p=128)[:, ec, qr * 256:(qr + 1) * 256],
                [], [buf("wf32")], tag=f"wfl2_{ix}")
            OP("sc", (lambda ec=ec, qr=qr: (lambda e: e.activation(
                wf8[:, ec * E + qr * 256: ec * E + (qr + 1) * 256], wf32[:, :],
                AF.Identity, bias=0.0, scale=swh_col[:, :])))(),
               rd=[buf("wf32"), buf("swh_col")], wr=[buf("wf8")], tag=f"wfq{ix}")

    wload("wk", wA, "wA", w_d["wk_t"], "ldwk")
    ln1_stats(0)
    ln1_rows(0, 1)
    make_xn(0, 1)
    emit_pk("k", wA, buf("wA"))
    wload("wv", wB, "wB", w_d["wv_t"], "ldwv")
    wload("wo", wC, "wC", w_d["wo_t"], "ldwo")
    for ec in range(ECH):
        dma("gp", wq_bf_d.rearrange("(ec p) i -> p ec i", p=128)[:, ec, :],
            w_d["wq_t"].rearrange("(ec p) i -> p ec i", p=128)[:, ec, :],
            [], [buf("wq_bf_d")], tag=f"pcq{ec}")
        dma("gp", wu_bf_d.rearrange("(ec p) i -> p ec i", p=128)[:, ec, :],
            w_d["wu_t"].rearrange("(ec p) i -> p ec i", p=128)[:, ec, :],
            [], [buf("wu_bf_d")], tag=f"pcu{ec}")
    for c in range(NCH):
        pj = 0
        for tt in range(NTT):
            if tt == 0 and c + 1 < NCH:
                ln1_stats(c + 1)
            proj_tm_tile(c, tt, buf("wA"), wA, "k", ktm_st,
                         buf("ktm_st"), "elu", f"p1c{c}k", pj)
            pj += 1
            if c == 0 and tt == 0:
                emit_pk("v", wB, buf("wB"))
            proj_tm_tile(c, tt, buf("wB"), wB, "v", vtm_st,
                         buf("vtm_st"), "silu2x", f"p1c{c}v", pj)
            pj += 1
            if c + 1 < NCH:
                if tt == 1:
                    ln1_rows(c + 1, 1)
                elif tt == 3:
                    make_xn(c + 1, 1)
            slot = c * NTT + tt
            nslots = NCH * NTT
            lo = slot * (8 * ECH) // nslots
            hi = (slot + 1) * (8 * ECH) // nslots
            for ix in range(lo, hi):
                if ix < 4 * ECH:
                    emit_wf_amax(ix, ix + 1)
                else:
                    if ix == 4 * ECH:
                        emit_wf_swh()
                    emit_wf_quant(ix - 4 * ECH, ix - 4 * ECH + 1)
        for tt in range(NTT):
            s = c * NTT + tt
            state_update1(s)
    OP("dv", lambda e: e.tensor_copy(S_acc[:, :], S_ps[:, :]),
       rd=[buf("S_ps")], wr=[buf("S_acc")], tag="p1sfin")

    def wload_bf(tgt, tgt_buf, src_d, tag, replaces):
        b = buf(replaces)
        deps = list(b.r) + ([b.w] if b.w else [])
        src_buf = {id(wq_bf_d): "wq_bf_d", id(wu_bf_d): "wu_bf_d"}[id(src_d)]
        bb = buf(src_buf)
        deps += [bb.w] + list(bb.r)
        for ec in range(ECH):
            dma("sy", tgt[:, :].rearrange("p (ec i) -> p ec i", ec=ECH)[:, ec, :],
                src_d.rearrange("(ec p) i -> p ec i", p=128)[:, ec, :],
                [], [buf(tgt_buf)], deps=deps, tag=f"{tag}_{ec}")
    wload_bf(wA, "wA", wq_bf_d, "ldwq", replaces="wA")
    wload_bf(wB, "wB", wu_bf_d, "ldwu", replaces="wB")
    emit_pk("q", wA, buf("wA"))
    emit_pk("u", wB, buf("wB"))

    OP("dv", lambda e: e.tensor_scalar(ssb[0][:, :], S_acc[:, :], emask[:, :], None, ALU.mult),
       rd=[buf("S_acc"), buf("emask")], wr=[buf("ssb0")], tag="ccmask")
    dma("gp", ccs_in_d[:, :], ssb[0][:, :], [buf("ssb0")], [buf("ccs_in")], tag="ccst")
    OP("gp", lambda e: e.collective_compute(
        "AllReduce", ALU.add, replica_groups=[[i, i + 1] for i in range(0, n_cores, 2)],
        ins=[ccs_in_d.ap().opt()], outs=[ccs_out_d.ap().opt()]),
        rd=[buf("ccs_in")], wr=[buf("ccs_out")], skey=("cc", False), tag="cc1")
    dma("gp", ssb[0][:, :], ccs_out_d[:, :], [buf("ccs_out")], [buf("ssb0")], tag="ccld")
    OP("dv", lambda e: e.tensor_scalar(sinit[:, :], ssb[0][:, :], omask[:, :], None, ALU.mult),
       rd=[buf("ssb0"), buf("omask")], wr=[buf("sinit")], tag="sinit")

    ln1_rows(0, 2)
    make_xn(0, 2)
    for c in range(NCH):
        tg = f"p2c{c}"
        proj_fm(c, buf("wA"), wA, "q", q_s, buf("q_s"), "elu", f"{tg}q")
        proj_fm(c, buf("wB"), wB, "u", us_s, buf("us_s"), "silu", f"{tg}u")
        for tt in range(NTT):
            s = c * NTT + tt
            kb, bkb = kfm_sub[0], buf("kfm0")
            for half in range(2):
                def ktr(e, s=s, half=half):
                    inst = None
                    for j in range(4):
                        h = half * 4 + j
                        inst = e.matmul(pT1[:, j * 128:(j + 1) * 128],
                                        ktm_st[:, s * E + h * D: s * E + (h + 1) * D],
                                        ident_bf[:, :], start=True, stop=True)
                    return inst
                OP("pe", ktr, rd=[buf("ktm_st"), buf("ident_bf")], wr=[buf("pT1")],
                   tag=f"{tg}kT{tt}_{half}")
                OP("sc", (lambda kb=kb, half=half: (lambda e: e.activation(
                    kb[:, half * 512:(half + 1) * 512], pT1[:, 0:512], AF.Identity)))(),
                   rd=[buf("pT1")], wr=[bkb], tag=f"{tg}kE{tt}_{half}")
            sb_t, bsb = ssb[0], buf("ssb0")
            if s > 0:
                state_accum(s - 1, 2, reset=(s == 1))
                OP("sc", (lambda sb_t=sb_t: (lambda e: e.activation(sb_t[:, :], S_acc[:, :],
                                                                    AF.Identity)))(),
                   rd=[buf("S_acc")], wr=[bsb], tag=f"{tg}sc{tt}")
            def eTg(e, s=s, tt=tt, kb=kb):
                inst = None
                for h in range(H):
                    inst = e.matmul(pBig[:, h * 128:(h + 1) * 128],
                                    kb[:, h * 128:(h + 1) * 128],
                                    q_s[:, h * C + tt * 128: h * C + (tt + 1) * 128],
                                    start=True, stop=True)
                return inst
            OP("pe", eTg, rd=[bkb, buf("q_s")], wr=[buf("pBig")], tag=f"{tg}eT{tt}")
            eb, beb = eT_s[0], buf("eT0")
            OP("dv", (lambda eb=eb: (lambda e: e.tensor_tensor(
                eb[:, :].rearrange("p (h n) -> p h n", h=H),
                pBig[:, :].rearrange("p (h n) -> p h n", h=H),
                triu_f[:, :].rearrange("p (o n) -> p o n", o=1).broadcast_to([128, H, 128]),
                ALU.mult)))(),
               rd=[buf("pBig"), buf("triu_f")], wr=[beb], tag=f"{tg}msk{tt}")
            def outg(e, s=s, tt=tt, eb=eb, sb_t=sb_t):
                inst = None
                for h in range(H):
                    inst = e.matmul(pBig[:, h * 128:(h + 1) * 128],
                                    vtm_st[:, s * E + h * D: s * E + (h + 1) * D],
                                    eb[:, h * 128:(h + 1) * 128], start=True, stop=False)
                    if s > 0:
                        inst = e.matmul(pBig[:, h * 128:(h + 1) * 128],
                                        sb_t[:, h * D:(h + 1) * D],
                                        q_s[:, h * C + tt * 128: h * C + (tt + 1) * 128],
                                        start=False, stop=False)
                    inst = e.matmul(pBig[:, h * 128:(h + 1) * 128],
                                    sinit[:, h * D:(h + 1) * D],
                                    q_s[:, h * C + tt * 128: h * C + (tt + 1) * 128],
                                    start=False, stop=True)
                return inst
            rd_o = [beb, buf("vtm_st"), buf("q_s"), buf("sinit")] + ([bsb] if s > 0 else [])
            OP("pe", outg, rd=rd_o, wr=[buf("pBig")], tag=f"{tg}out{tt}")
            if s < NSUB - 1:
                state_update(s, 2)
            OP("dv", (lambda tt=tt: (lambda e: e.tensor_tensor(
                g1_s[tt][:, :].rearrange("p (h n) -> p h n", h=H),
                pBig[:, :].rearrange("p (h n) -> p h n", h=H),
                us_s[:, :].rearrange("p (h cc) -> p h cc", h=H)[:, :, tt * 128:(tt + 1) * 128],
                ALU.mult)))(),
               rd=[buf("pBig"), buf("us_s")], wr=[buf(f"g1_{tt}")], tag=f"{tg}g1{tt}")
            OP("sc", lambda e: e.activation(eT_s[0][:, :], pBig[:, :], AF.Square),
               rd=[buf("pBig")], wr=[buf("eT0")], tag=f"{tg}sq{tt}")
            def ssqg(e, tt=tt):
                inst = None
                for h in range(H):
                    inst = e.matmul(pT2[0:1, tt * 128:(tt + 1) * 128], onescol[:, :],
                                    eT_s[0][:, h * 128:(h + 1) * 128],
                                    start=(h == 0), stop=(h == H - 1))
                return inst
            OP("pe", ssqg, rd=[buf("eT0"), buf("onescol")], wr=[buf("pT2")], tag=f"{tg}ssq{tt}")
            if tt == 0 and c + 1 < NCH:
                ln1_rows(c + 1, 2)
                make_xn(c + 1, 2)
        OP("dv", lambda e: e.tensor_copy(e_scr[0:1, 0:C], pT2[0:1, 0:C]),
           rd=[buf("pT2")], wr=[buf("e_scr")], tag=f"{tg}sse")
        def ssqT(e):
            inst = None
            for tt in range(NTT):
                inst = e.matmul(pT1[:, 256 + tt:257 + tt],
                                e_scr[0:1, tt * 128:(tt + 1) * 128],
                                ones11[0:1, 0:1], start=True, stop=True)
            return inst
        OP("pe", ssqT, rd=[buf("e_scr"), buf("ones11")], wr=[buf("pT1")], tag=f"{tg}ssT")
        OP("dv", lambda e: e.tensor_copy(nsc2[:, 0:NTT], pT1[:, 256:256 + NTT]),
           rd=[buf("pT1")], wr=[buf("nsc2")], tag=f"{tg}ssTe")
        OP("dv", lambda e: e.tensor_copy(rcp_t[:, :], nsc2[:, 0:NTT]),
           rd=[buf("nsc2")], wr=[buf("rcp_t")], tag=f"{tg}sscp")
        newton_rsqrt(buf("rcp_t"), rcp_t[:, :], buf("nsc2"), nsc2[:, 0:NTT], 0.0, f"{tg}nr")
        OP("dv", lambda e: e.tensor_mul(rcp_t[:, :], rcp_t[:, :], nsc2[:, 0:NTT]),
           rd=[buf("rcp_t"), buf("nsc2")], wr=[buf("rcp_t")], tag=f"{tg}rm1")
        OP("dv", lambda e: e.tensor_scalar(rcp_t[:, :], rcp_t[:, :], 1.0 / 64.0, RMS_EPS,
                                           ALU.mult, ALU.add),
           rd=[buf("rcp_t")], wr=[buf("rcp_t")], tag=f"{tg}rm2")
        OP("dv", lambda e: e.tensor_copy(rmse_bf[:, :], rcp_t[:, :]),
           rd=[buf("rcp_t")], wr=[buf("rmse_bf")], tag=f"{tg}rmb")
        OP("dv", lambda e: e.reciprocal(rcp_t[:, :], rcp_t[:, :]),
           rd=[buf("rcp_t")], wr=[buf("rcp_t")], tag=f"{tg}rcp")
        for tt in range(NTT):
            OP("pe", (lambda tt=tt: (lambda e: e.matmul(
                pT1[0:1, tt * 128:(tt + 1) * 128], rmse_bf[:, tt:tt + 1], ident_bf[:, :],
                start=True, stop=True)))(),
               rd=[buf("rmse_bf"), buf("ident_bf")], wr=[buf("pT1")], tag=f"{tg}rT{tt}")
        OP("dv", lambda e: e.tensor_copy(rrow[0:1, :], pT1[0:1, 0:C]),
           rd=[buf("pT1")], wr=[buf("rrow")], tag=f"{tg}rTe")
        for tt in range(NTT):
            s = c * NTT + tt
            xt, bxt = x_tile[tt % 2], buf(f"x_tile{tt % 2}")
            dma("sy", xt[:, :], x_tm_d.rearrange("(s p) e -> p s e", p=128)[:, s, :],
                [], [bxt], tag=f"{tg}x2{tt}")
            def wog(e, tt=tt):
                e.matmul(pBig[:, 0:512], rrow[0:1, tt * 128:(tt + 1) * 128],
                         bo_row[0:1, 0:512], start=True, stop=False)
                e.matmul(pBig[:, 512:1024], rrow[0:1, tt * 128:(tt + 1) * 128],
                         bo_row[0:1, 512:1024], start=True, stop=False)
                inst = None
                for h in range(H):
                    for half in range(2):
                        inst = e.matmul(pBig[:, half * 512:(half + 1) * 512],
                                        g1_s[tt][:, h * 128:(h + 1) * 128],
                                        wC[:, h * E + half * 512: h * E + (half + 1) * 512],
                                        start=False, stop=(h == H - 1))
                return inst
            OP("pe", wog, rd=[buf(f"g1_{tt}"), buf("wC"), buf("rrow"), buf("bo_row")],
               wr=[buf("pBig")], tag=f"{tg}wo{tt}")
            r2, br2 = res2_s[tt % 2], buf(f"res2_{tt % 2}")
            OP("dv", (lambda tt=tt, xt=xt, r2=r2, s=s: (lambda e: e.scalar_tensor_tensor(
                r2[:, :], pBig[:, :], rcp_t[:, tt:tt + 1], xt[:, :], ALU.mult, ALU.add,
                accum_out=l2sum[:, s:s + 1])))(),
               rd=[buf("pBig"), buf("rcp_t"), bxt], wr=[br2, buf("l2sum")], tag=f"{tg}r2{tt}")
            OP("sc", (lambda r2=r2, s=s: (lambda e: e.activation(
                eT_s[0][:, :], r2[:, :], AF.Square, accum_out=l2sq[:, s:s + 1])))(),
               rd=[br2], wr=[buf("eT0"), buf("l2sq")], tag=f"{tg}l2q{tt}")
            OP("dv", (lambda r2=r2, s=s: (lambda e: e.reduce_max(
                l2max[:, s:s + 1], r2[:, :], axis=AX.X)))(),
               rd=[br2], wr=[buf("l2max")], tag=f"{tg}l2x{tt}")
            OP("dv", (lambda r2=r2, s=s: (lambda e: e.tensor_reduce(
                l2min[:, s:s + 1], r2[:, :], AX.X, ALU.min)))(),
               rd=[br2], wr=[buf("l2min")], tag=f"{tg}l2n{tt}")
            dma("sc", res2_d.rearrange("(s p) e -> p s e", p=128)[:, s, :], r2[:, :],
                [br2], [buf("res2_d")], tag=f"{tg}rs{tt}")

    OP("dv", lambda e: e.tensor_scalar(l2m[:, :], l2sum[:, :], 1.0 / float(E), None, ALU.mult),
       rd=[buf("l2sum")], wr=[buf("l2m")], tag="l2m")
    OP("dv", lambda e: e.tensor_mul(l2r[:, :], l2m[:, :], l2m[:, :]),
       rd=[buf("l2m")], wr=[buf("l2r")], tag="l2ms")
    OP("dv", lambda e: e.scalar_tensor_tensor(l2r[:, :], l2sq[:, :], 1.0 / float(E), l2r[:, :],
                                              ALU.mult, ALU.subtract),
       rd=[buf("l2sq"), buf("l2r")], wr=[buf("l2r")], tag="l2v")
    newton_rsqrt(buf("l2r"), l2r[:, :], buf("l2r"), l2r[:, :], LN_EPS, "l2rs")
    OP("dv", lambda e: e.tensor_sub(l2max[:, :], l2max[:, :], l2m[:, :]),
       rd=[buf("l2max"), buf("l2m")], wr=[buf("l2max")], tag="am1")
    OP("dv", lambda e: e.tensor_sub(l2min[:, :], l2m[:, :], l2min[:, :]),
       rd=[buf("l2min"), buf("l2m")], wr=[buf("l2min")], tag="am2")
    OP("dv", lambda e: e.tensor_max(l2max[:, :], l2max[:, :], l2min[:, :]),
       rd=[buf("l2max"), buf("l2min")], wr=[buf("l2max")], tag="am3")
    OP("dv", lambda e: e.tensor_mul(l2max[:, :], l2max[:, :], l2r[:, :]),
       rd=[buf("l2max"), buf("l2r")], wr=[buf("l2max")], tag="am4")
    OP("dv", lambda e: e.reduce_max(amrow[:, :], l2max[:, :], axis=AX.X),
       rd=[buf("l2max")], wr=[buf("amrow")], tag="am5")
    OP("pe", lambda e: e.matmul(pT2[0:1, 256:384], amrow[:, :], ident_f32[:, :], start=True, stop=True),
       rd=[buf("amrow"), buf("ident_f32")], wr=[buf("pT2")], tag="amT")
    OP("dv", lambda e: e.tensor_copy(wrow[:, :], pT2[0:1, 256:384]),
       rd=[buf("pT2")], wr=[buf("wrow")], tag="amTe")
    OP("dv", lambda e: e.reduce_max(scalar_sb[:, 2:3], wrow[:, :], axis=AX.X),
       rd=[buf("wrow")], wr=[buf("scalar_sb")], tag="aml")
    OP("dv", lambda e: e.tensor_copy(wrow[0:1, :], scalar_sb[0:1, 2:3].broadcast_to([1, 128])),
       rd=[buf("scalar_sb")], wr=[buf("wrow")], tag="amb")
    dma("gp", ccm_in_d[:, :], wrow[0:1, :], [buf("wrow")], [buf("ccm_in")], tag="ccmst")
    OP("gp", lambda e: e.collective_compute(
        "AllReduce", ALU.max, replica_groups=[list(range(n_cores))],
        ins=[ccm_in_d.ap().opt()], outs=[ccm_out_d.ap().opt()]),
        rd=[buf("ccm_in")], wr=[buf("ccm_out")], skey=("cc", False), tag="cc2")
    dma("gp", scalar_sb[0:1, 3:4], ccm_out_d[0:1, 0:1], [buf("ccm_out")], [buf("scalar_sb")], tag="ccmld")
    OP("dv", lambda e: e.tensor_scalar(scalar_sb[:, 4:5], scalar_sb[:, 3:4], 1e-12, None, ALU.max),
       rd=[buf("scalar_sb")], wr=[buf("scalar_sb")], tag="sx1")
    OP("dv", lambda e: e.reciprocal(scalar_sb[:, 4:5], scalar_sb[:, 4:5]),
       rd=[buf("scalar_sb")], wr=[buf("scalar_sb")], tag="sx2")
    OP("dv", lambda e: e.tensor_scalar(scalar_sb[:, 4:5], scalar_sb[:, 4:5], FP8_MAX_HALF, None, ALU.mult),
       rd=[buf("scalar_sb")], wr=[buf("scalar_sb")], tag="sx3")
    OP("dv", lambda e: e.tensor_mul(scalar_sb[:, 5:6], scalar_sb[:, 4:5], scalar_sb[:, 1:2]),
       rd=[buf("scalar_sb")], wr=[buf("scalar_sb")], tag="ssw")
    OP("dv", lambda e: e.tensor_scalar(bf_row[0:1, :], bf_row[0:1, :], scalar_sb[0:1, 5:6], None, ALU.mult),
       rd=[buf("scalar_sb"), buf("bf_row")], wr=[buf("bf_row")], tag="bfs")
    OP("pe", lambda e: e.matmul(pT2[:, 129:130], ones11[:, :], scalar_sb[0:1, 4:5], start=True, stop=True),
       rd=[buf("scalar_sb"), buf("ones11")], wr=[buf("pT2")], tag="sxB")
    OP("dv", lambda e: e.tensor_copy(sxcol[:, :], pT2[:, 129:130]),
       rd=[buf("pT2")], wr=[buf("sxcol")], tag="sxE")
    OP("pe", lambda e: e.matmul(pT2[:, 130:131], ones11[:, :], scalar_sb[0:1, 5:6], start=True, stop=True),
       rd=[buf("scalar_sb"), buf("ones11")], wr=[buf("pT2")], tag="dB")
    OP("dv", lambda e: e.tensor_copy(d_col[:, :], pT2[:, 130:131]),
       rd=[buf("pT2")], wr=[buf("d_col")], tag="dE")
    OP("dv", lambda e: e.reciprocal(d_col[:, :], d_col[:, :]),
       rd=[buf("d_col")], wr=[buf("d_col")], tag="dR")
    OP("dv", lambda e: e.tensor_scalar(qa[:, :], l2r[:, :], sxcol[:, :], None, ALU.mult),
       rd=[buf("l2r"), buf("sxcol")], wr=[buf("qa")], tag="qa")
    OP("dv", lambda e: e.scalar_tensor_tensor(qb[:, :], l2m[:, :], -1.0, qa[:, :], ALU.mult, ALU.mult),
       rd=[buf("l2m"), buf("qa")], wr=[buf("qb")], tag="qb")

    st_ops = []
    for s in range(NSUB):
        ri, bri = resin[s % 2], buf(f"x_tile{s % 2}")
        dma("sy", ri[:, :], res2_d.rearrange("(s p) e -> p s e", p=128)[:, s, :],
            [buf("res2_d")], [bri], tag=f"f3l{s}")
        xq, bxq = xq_s[s % 2], buf(f"xq{s % 2}")
        OP("sc", (lambda ri=ri, xq=xq, s=s: (lambda e: e.activation(
            xq[:, :], ri[:, :], AF.Identity, bias=qb[:, s:s + 1], scale=qa[:, s:s + 1])))(),
           rd=[bri, buf("qa"), buf("qb")], wr=[bxq], tag=f"f3q{s}")
        if s % 2 == 0:
            xqf, bxqf = xqf_s[0], buf("xqf0")
        else:
            xqf, bxqf = eT_s[0], buf("eT0")
        for half in range(2):
            p, pb = (pT1, buf("pT1")) if half == 0 else (pT2, buf("pT2"))
            def xtr(e, xq=xq, half=half, p=p):
                inst = None
                for j in range(4):
                    ec = half * 4 + j
                    inst = e.matmul(p[:, j * 128:(j + 1) * 128],
                                    xq[:, ec * 128:(ec + 1) * 128], ident_bf[:, :],
                                    start=True, stop=True)
                return inst
            OP("pe", xtr, rd=[bxq, buf("ident_bf")], wr=[pb], tag=f"f3T{s}_{half}")
            OP("sc", (lambda xqf=xqf, half=half, p=p: (lambda e: e.activation(
                xqf[:, half * 512:(half + 1) * 512], p[:, 0:512], AF.Identity)))(),
               rd=[pb], wr=[bxqf], tag=f"f3E{s}_{half}")
        def ffn(e, xqf=xqf):
            e.matmul(S_ps[:, 0:512], ones_bf[0:1, 0:128], bf_row[0:1, 0:512], start=True, stop=False)
            e.matmul(S_ps[:, 512:1024], ones_bf[0:1, 0:128], bf_row[0:1, 512:1024], start=True, stop=False)
            inst = None
            for ec in range(ECH):
                for half in range(2):
                    inst = e.matmul(S_ps[:, half * 512:(half + 1) * 512],
                                    xqf[:, ec * 128:(ec + 1) * 128],
                                    wf8[:, ec * E + half * 512: ec * E + (half + 1) * 512],
                                    start=False, stop=(ec == ECH - 1))
            return inst
        OP("pe", ffn, rd=[bxqf, buf("wf8"), buf("bf_row"), buf("ones_bf")], wr=[buf("S_ps")],
           tag=f"f3m{s}")
        yb, byb = res2_s[s % 2], buf(f"res2_{s % 2}")
        OP("dv", (lambda ri=ri, yb=yb: (lambda e: e.scalar_tensor_tensor(
            yb[:, :], S_ps[:, :], d_col[:, :], ri[:, :], ALU.mult, ALU.add)))(),
           rd=[buf("S_ps"), buf("d_col"), bri], wr=[byb], tag=f"f3y{s}")
        st_ops.append(dma("sc", y_d.rearrange("(s p) e -> p s e", p=128)[:, s, :], yb[:, :],
                          [byb], [buf("y_d")], tag=f"f3s{s}"))

    OP("sy", None, deps=st_ops, tag="final")

    if max_ops is not None:
        keep = sch.ops[:max_ops]
        kept = set(id(o) for o in keep)
        sch.ops = keep
        for o in sch.ops:
            o.deps = {d for d in o.deps if id(d) in kept}
    sch.finalize()
    with ExitStack() as sem_es:
        sems = {}
        for key, name in [(("pe", False), "s_pe"), (("dv", False), "s_dv"),
                          (("sc", False), "s_ac"), (("gp", False), "s_gp"),
                          (("sy", False), "s_sy"), (("cc", False), "s_cc")]:
            sems[key] = sem_es.enter_context(nc.semaphore(name))
        for q in ("sy", "sc", "gp"):
            for j in range(Sched.NDMASEM):
                sems[(q, "dma", j)] = sem_es.enter_context(nc.semaphore(f"d_{q}{j}"))
        with nc.Block() as block:
            @block.sync
            def _(h):
                sch.emit("sy", h, sems)

            @block.tensor
            def _(h):
                sch.emit("pe", h, sems)

            @block.vector
            def _(h):
                sch.emit("dv", h, sems)

            @block.scalar
            def _(h):
                sch.emit("sc", h, sems)

            @block.gpsimd
            def _(h):
                sch.emit("gp", h, sems)
    es.close()
    return nc


def make_inputs(core_id, x_full, W, T):
    b, half = core_id // 2, core_id % 2
    xs = np.ascontiguousarray(x_full[b, half * T:(half + 1) * T, :], dtype=np.float32)
    d = {
        "x_tm": xs,
        "x_fm": np.ascontiguousarray(xs.T),
        "wq_t": np.ascontiguousarray(W["Wq"].T),
        "wk_t": np.ascontiguousarray(W["Wk"].T),
        "wv_t": np.ascontiguousarray(W["Wv"].T),
        "wu_t": np.ascontiguousarray(W["Wu"].T),
        "wo_t": np.ascontiguousarray(W["Wo"].T),
        "wf_t": np.ascontiguousarray(W["Wf"].T),
        "bq_row": np.ascontiguousarray(W["bq"].reshape(1, E)),
        "bk_row": np.ascontiguousarray(W["bk"].reshape(1, E)),
        "bv_row": np.ascontiguousarray(W["bv"].reshape(1, E)),
        "bu_row": np.ascontiguousarray(W["bu"].reshape(1, E)),
        "bo_row": np.ascontiguousarray(W["bo"].reshape(1, E)),
        "bf_row": np.ascontiguousarray(W["bf"].reshape(1, E)),
        "ones_row": np.ones((1, 512), dtype=np.float32),
        "scale_pt": np.ascontiguousarray(W["scale"].reshape(H, 128).T),
        "ident": np.eye(128, dtype=np.float32),
        "triu": np.triu(np.ones((128, 128), dtype=np.float32)),
        "onescol": np.ones((128, 1), dtype=np.float32),
        "ones11": np.ones((1, 128), dtype=np.float32),
        "emask": np.full((128, 1), 1.0 - (core_id % 2), dtype=np.float32),
        "omask": np.full((128, 1), float(core_id % 2), dtype=np.float32),
    }
    return d


def kernel(L0_attn_x, L0_attn_Wq, L0_attn_bq, L0_attn_Wk, L0_attn_bk,
           L0_attn_Wv, L0_attn_bv, L0_attn_Wu, L0_attn_bu, L0_attn_Wo,
           L0_attn_bo, L0_attn_scale, L0_ffn_weight, L0_ffn_bias):
    x = np.asarray(L0_attn_x, dtype=np.float32)
    Bn, N, _ = x.shape
    T = N // 2
    W = {
        "Wq": np.asarray(L0_attn_Wq, np.float32), "bq": np.asarray(L0_attn_bq, np.float32),
        "Wk": np.asarray(L0_attn_Wk, np.float32), "bk": np.asarray(L0_attn_bk, np.float32),
        "Wv": np.asarray(L0_attn_Wv, np.float32), "bv": np.asarray(L0_attn_bv, np.float32),
        "Wu": np.asarray(L0_attn_Wu, np.float32), "bu": np.asarray(L0_attn_bu, np.float32),
        "Wo": np.asarray(L0_attn_Wo, np.float32), "bo": np.asarray(L0_attn_bo, np.float32),
        "scale": np.asarray(L0_attn_scale, np.float32),
        "Wf": np.asarray(L0_ffn_weight, np.float32), "bf": np.asarray(L0_ffn_bias, np.float32),
    }
    n_cores = 2 * Bn
    nc = build_program(T=T, C=min(512, T), n_cores=n_cores)
    in_maps = [make_inputs(cid, x, W, T) for cid in range(n_cores)]
    res = run_bass_kernel_spmd(nc, in_maps, core_ids=list(range(n_cores)))
    y = np.empty((Bn, N, E), dtype=np.float32)
    for cid in range(n_cores):
        b, half = cid // 2, cid % 2
        y[b, half * T:(half + 1) * T, :] = res.results[cid]["y"]
    return y


# version 17
# speedup vs baseline: 1.1473x; 1.0343x over previous
import numpy as np
import concourse.bass as bass
ABLATE = set()
import concourse.mybir as mybir
from concourse.bass_utils import run_bass_kernel_spmd
from contextlib import ExitStack

F32 = mybir.dt.float32
BF16 = mybir.dt.bfloat16
F8 = mybir.dt.float8e4
I32 = mybir.dt.int32
AF = mybir.ActivationFunctionType
ALU = mybir.AluOpType
AX = mybir.AxisListType

E = 1024
H = 8
D = 128
ECH = 8
ICH = 8
LN_EPS = 1e-5
RMS_EPS = 1e-8
FP8_MAX_HALF = 224.0
LN2C = float(np.log(2.0))


class _Op:
    __slots__ = ("eng", "fn", "deps", "dma", "skey", "needs_inc", "val", "tag", "rdn", "wrn")

    def __init__(self, eng, fn, deps, dma, skey, tag):
        self.eng, self.fn, self.deps, self.dma, self.skey, self.tag = eng, fn, deps, dma, skey, tag
        self.needs_inc = False
        self.val = None
        self.rdn = ()
        self.wrn = ()


class Buf:
    __slots__ = ("w", "r", "name", "psum")

    def __init__(self, name="", psum=False):
        self.w, self.r, self.name, self.psum = None, [], name, psum


class Sched:
    SERIAL = False
    SERIAL_ENGS = set()

    def __init__(self):
        self.ops = []

    def op(self, eng, fn, rd=(), wr=(), dma=False, deps=(), skey=None, tag=""):
        dep = set()
        for b in rd:
            if b.w is not None:
                dep.add(b.w)
            if b.psum:
                for r_ in b.r:
                    if r_.eng != eng:
                        dep.add(r_)
        for b in wr:
            if b.w is not None:
                dep.add(b.w)
            dep.update(b.r)
        for d in deps:
            if d is not None:
                dep.add(d)
        o = _Op(eng, fn, dep, dma, skey or (eng, dma), tag)
        o.rdn = tuple(b.name for b in rd)
        o.wrn = tuple(b.name for b in wr)
        self.ops.append(o)
        for b in rd:
            b.r.append(o)
        for b in wr:
            b.w = o
            b.r = []
        return o

    NDMASEM = 8
    RELAX_SAME_ENGINE = {("pe", False), ("sc", False)}

    def _skip(self, d, o):
        return (d.skey == o.skey and d.skey in self.RELAX_SAME_ENGINE
                and not d.dma and not o.dma)

    def finalize(self):
        if Sched.SERIAL:
            prev = None
            for o in self.ops:
                if o.fn is not None and prev is not None:
                    o.deps.add(prev)
                if o.fn is not None:
                    prev = o
        if Sched.SERIAL_ENGS:
            prev_any = None
            last_chained = None
            for o in self.ops:
                if o.fn is None:
                    continue
                if o.eng in Sched.SERIAL_ENGS:
                    if prev_any is not None:
                        o.deps.add(prev_any)
                    last_chained = o
                elif last_chained is not None:
                    o.deps.add(last_chained)
                prev_any = o
        rot = {}
        for o in self.ops:
            if o.dma:
                q = o.skey
                j = rot.get(q, 0)
                rot[q] = (j + 1) % self.NDMASEM
                o.skey = (q[0], "dma", j)
                o.needs_inc = True
        for o in self.ops:
            o.deps = {d for d in o.deps if not self._skip(d, o)}
        for o in self.ops:
            o.needs_inc = o.needs_inc and False or o.needs_inc
        for o in self.ops:
            o.needs_inc = bool(o.dma)
        for o in self.ops:
            for d in o.deps:
                d.needs_inc = True
        cnt = {}
        for o in self.ops:
            if o.needs_inc:
                cnt[o.skey] = cnt.get(o.skey, 0) + (16 if o.dma else 1)
                o.val = cnt[o.skey]

    def emit(self, eng_name, handle, sems):
        water = {}
        for o in self.ops:
            if o.eng != eng_name:
                continue
            waits = {}
            for d in o.deps:
                if d.val is None:
                    raise RuntimeError(f"dep without inc: {d.tag} <- {o.tag}")
                waits[d.skey] = max(waits.get(d.skey, 0), d.val)
            if o.dma and o.val >= 32:
                waits[o.skey] = max(waits.get(o.skey, 0), o.val - 16)
            for key in sorted(waits, key=str):
                v = waits[key]
                if water.get(key, 0) < v:
                    handle.wait_ge(sems[key], v)
                    water[key] = v
            if o.fn is None:
                continue
            inst = o.fn(handle)
            if o.needs_inc:
                inst.then_inc(sems[o.skey], 16 if o.dma else 1)


def build_program(T=2048, C=512, n_cores=8, max_ops=None):
    NCH = T // C
    NTT = C // 128
    NSUB = T // 128
    nc = bass.Bass("TRN2", target_bir_lowering=False, detect_race_conditions=False)

    x_tm_d = nc.dram_tensor("x_tm", [T, E], F32, kind="ExternalInput")
    x_fm_d = nc.dram_tensor("x_fm", [E, T], F32, kind="ExternalInput")
    w_d = {n: nc.dram_tensor(n, [E, E], F32, kind="ExternalInput")
           for n in ("wq_t", "wk_t", "wv_t", "wu_t", "wo_t", "wf_t")}
    bias_d = {n: nc.dram_tensor(n, [1, E], F32, kind="ExternalInput")
              for n in ("bq_row", "bk_row", "bv_row", "bu_row")}
    bo_row_d = nc.dram_tensor("bo_row", [1, E], F32, kind="ExternalInput")
    bf_row_d = nc.dram_tensor("bf_row", [1, E], F32, kind="ExternalInput")
    scale_d = nc.dram_tensor("scale_pt", [128, H], F32, kind="ExternalInput")
    ident_d = nc.dram_tensor("ident", [128, 128], F32, kind="ExternalInput")
    triu_d = nc.dram_tensor("triu", [128, 128], F32, kind="ExternalInput")
    ones_d = nc.dram_tensor("ones_row", [1, 512], F32, kind="ExternalInput")
    onescol_d = nc.dram_tensor("onescol", [128, 1], F32, kind="ExternalInput")
    ones11_d = nc.dram_tensor("ones11", [1, 128], F32, kind="ExternalInput")
    emask_d = nc.dram_tensor("emask", [128, 1], F32, kind="ExternalInput")
    omask_d = nc.dram_tensor("omask", [128, 1], F32, kind="ExternalInput")
    y_d = nc.dram_tensor("y", [T, E], F32, kind="ExternalOutput")

    res2_d = nc.dram_tensor("res2_scratch", [T, E], F32)
    wq_bf_d = nc.dram_tensor("wq_bf", [E, E], BF16)
    wu_bf_d = nc.dram_tensor("wu_bf", [E, E], BF16)
    ccs_in_d = nc.dram_tensor("ccs_in", [128, H * D], BF16)
    ccs_out_d = nc.dram_tensor("ccs_out", [128, H * D], BF16)
    ccm_in_d = nc.dram_tensor("ccm_in", [1, 128], F32)
    ccm_out_d = nc.dram_tensor("ccm_out", [1, 128], F32)

    es = ExitStack()
    sbt = lambda name, shape, dt: es.enter_context(nc.sbuf_tensor(name, shape, dt))
    pst = lambda name, shape: es.enter_context(nc.psum_tensor(name, shape, F32))

    ktm_st = sbt("ktm_st", [128, NSUB * E], BF16)
    vtm_st = sbt("vtm_st", [128, NSUB * E], BF16)
    wA = sbt("wA", [128, ECH * E], BF16)
    wB = sbt("wB", [128, ECH * E], BF16)
    wC = sbt("wC", [128, ECH * E], BF16)
    wf8 = sbt("wf8", [128, ECH * E], F8)
    wf32 = sbt("wf32", [128, 256], F32)
    pkA = sbt("pkA", [34, E], BF16)
    pkB = sbt("pkB", [34, E], BF16)
    pk = {"q": (pkA, 0), "k": (pkA, 32), "v": (pkB, 0), "u": (pkB, 32)}
    rows_dyn = [sbt(f"rows_dyn{i}", [34, 512], BF16) for i in range(2)]
    ones_bf = sbt("ones_bf", [1, 128], BF16)
    bo_row = sbt("bo_row_s", [1, E], BF16)
    bf_row = sbt("bf_row_s", [1, E], BF16)
    scale_pt = sbt("scale_pt_s", [128, H], F32)
    sclh = sbt("sclh", [128, H], F32)
    ident_bf = sbt("ident_bf", [128, 128], BF16)
    ident_f32 = sbt("ident_f32", [128, 128], F32)
    triu_f = sbt("triu_f", [128, 128], BF16)
    onescol = sbt("onescol_s", [128, 1], BF16)
    ones11 = sbt("ones11_s", [1, 128], F32)
    emask = sbt("emask_s", [128, 1], F32)
    omask = sbt("omask_s", [128, 1], F32)

    xf32 = [sbt(f"xf32_{i}", [128, 512], F32) for i in range(2)]
    xn = sbt("xn", [128, ECH * C], BF16)
    rbc = sbt("rbc", [128, C], BF16)
    x_tile = [sbt(f"x_tile{i}", [128, E], F32) for i in range(2)]
    q_s = sbt("q_s", [128, H * C], BF16)
    us_s = sbt("us_s", [128, H * C], BF16)
    e_scr = sbt("e_scr", [128, 512], F32)
    t2_scr = sbt("t2_scr", [128, 512], BF16)
    rrow = sbt("rrow", [1, C], BF16)
    statc = sbt("statc", [128, 1], BF16)

    kfm_sub = [sbt("kfm0", [128, E], BF16)]
    eT_s = [sbt("eT0", [128, H * 128], BF16)]
    S_acc = sbt("S_acc", [128, H * D], F32)
    ssb = [sbt("ssb0", [128, H * D], BF16)]
    sinit = sbt("sinit", [128, H * D], BF16)
    g1_s = [sbt(f"g1_{i}", [128, E], BF16) for i in range(NTT)]
    res2_s = [sbt(f"res2_{i}", [128, E], F32) for i in range(2)]
    resin = x_tile

    xq_s = [sbt(f"xq{i}", [128, E], F8) for i in range(2)]
    xqf_s = [sbt("xqf0", [128, E], F8)]

    bn6 = sbt("bn6", [128, 12], F32)
    bn2 = sbt("bn2", [128, 2], F32)
    l1m = sbt("l1m", [128, NSUB], F32)
    l1v = sbt("l1v", [128, NSUB], F32)
    l1r = sbt("l1r", [128, NSUB], F32)
    l1mr = sbt("l1mr", [128, NSUB], F32)
    l2sum = sbt("l2sum", [128, NSUB], F32)
    l2sq = sbt("l2sq", [128, NSUB], F32)
    l2max = sbt("l2max", [128, NSUB], F32)
    l2min = sbt("l2min", [128, NSUB], F32)
    l2m = sbt("l2m", [128, NSUB], F32)
    l2r = sbt("l2r", [128, NSUB], F32)
    qa = sbt("qa", [128, NSUB], F32)
    qb = sbt("qb", [128, NSUB], F32)
    nsc = sbt("nsc", [128, NSUB], F32)
    nsc2 = sbt("nsc2", [128, NSUB], F32)
    nsc3 = sbt("nsc3", [128, NSUB], F32)
    exp_in = nsc2
    rcp_t = sbt("rcp_t", [128, NTT], F32)
    rmse_bf = sbt("rmse_bf", [128, NTT], BF16)
    wmax = sbt("wmax", [128, 4 * ECH], F32)
    wmax1 = sbt("wmax1", [128, 1], F32)
    wrow = sbt("wrow", [1, 128], F32)
    scalar_sb = sbt("scalar_sb", [1, 8], F32)
    swh_col = sbt("swh_col", [128, 1], F32)
    sxcol = sbt("sxcol", [128, 1], F32)
    d_col = sbt("d_col", [128, 1], F32)
    amrow = sbt("amrow", [128, 1], F32)
    seedb = sbt("seedb", [128, 1], F32)

    S_ps = pst("S_ps", [128, 1024])
    pBig = pst("pBig", [128, 1024])
    pS1 = pst("pS1", [128, 512])
    pS2 = pst("pS2", [128, 512])
    pT1 = pst("pT1", [128, 512])
    pT2 = pst("pT2", [128, 512])

    B = {}
    PSUM_BUFS = {"S_ps", "pBig", "pS1", "pS2", "pT1", "pT2"}

    def buf(name):
        if name not in B:
            B[name] = Buf(name, psum=(name in PSUM_BUFS))
        return B[name]

    sch = Sched()
    OP = sch.op

    def dma(eng, out_ap, in_ap, rd, wr, deps=(), tag=""):
        return OP(eng, (lambda o, i: (lambda e: e.dma_start(out=o, in_=i)))(out_ap, in_ap),
                  rd=rd, wr=wr, dma=True, deps=deps, tag=tag)

    for n, key in (("q", "bq_row"), ("k", "bk_row"), ("v", "bv_row"), ("u", "bu_row")):
        t, po = pk[n]
        dma("gp", t[po + 1:po + 2, :], bias_d[key][:, :], [], [buf(f"pk_{n}_b")], tag=f"ldb{n}")
    for i in range(2):
        dma("gp", rows_dyn[i][1:2, :], ones_d[:, :], [], [buf(f"rows_ones{i}")], tag=f"ldro{i}")
        dma("gp", rows_dyn[i][33:34, :], ones_d[:, :], [], [buf(f"rows_ones{i}")], tag=f"ldro32_{i}")
    dma("gp", bo_row[:, :], bo_row_d[:, :], [], [buf("bo_row")], tag="ldbo")
    dma("gp", bf_row[:, :], bf_row_d[:, :], [], [buf("bf_row")], tag="ldbf")
    dma("gp", ident_bf[:, :], ident_d[:, :], [], [buf("ident_bf")], tag="ldidb")
    dma("gp", ones_bf[:, :], ones_d[0:1, 0:128], [], [buf("ones_bf")], tag="ldonesbf")
    dma("gp", onescol[:, :], onescol_d[:, :], [], [buf("onescol")], tag="ldoc")
    dma("sy", ident_f32[:, :], ident_d[:, :], [], [buf("ident_f32")], tag="ldidf")
    dma("gp", triu_f[:, :], triu_d[:, :], [], [buf("triu_f")], tag="ldtriu")
    dma("sy", scale_pt[:, :], scale_d[:, :], [], [buf("scale_pt")], tag="ldscale")
    dma("sy", ones11[:, :], ones11_d[:, :], [], [buf("ones11")], tag="ldo11")
    dma("sy", emask[:, :], emask_d[:, :], [], [buf("emask")], tag="ldem")
    dma("sy", omask[:, :], omask_d[:, :], [], [buf("omask")], tag="ldom")
    OP("dv", lambda e: e.tensor_scalar(sclh[:, :], scale_pt[:, :], 0.25, None, ALU.mult),
       rd=[buf("scale_pt")], wr=[buf("sclh")], tag="sclh")
    OP("dv", lambda e: e.memset(seedb[:, :], float(LN2C * 63.5)),
       rd=[], wr=[buf("seedb")], tag="seedb")

    def wload(wname, tgt, tgt_buf, src, tag, replaces=None):
        deps = []
        if replaces is not None:
            b = buf(replaces)
            deps = list(b.r) + ([b.w] if b.w else [])
        last = None
        for ec in range(ECH):
            last = dma("gp", tgt[:, :].rearrange("p (ec i) -> p ec i", ec=ECH)[:, ec, :],
                       src.rearrange("(ec p) i -> p ec i", p=128)[:, ec, :],
                       [], [buf(tgt_buf)], deps=deps, tag=f"{tag}_{ec}")
        return last

    def emit_pk(name, wt, wt_buf):
        t, po = pk[name]
        for half in range(2):
            p, pb = (pT1, buf("pT1")) if half == 0 else (pT2, buf("pT2"))
            def grp(e, half=half, p=p, wt=wt, po=po):
                inst = None
                for ec in range(ECH):
                    inst = e.matmul(p[po:po + 1, 0:512], onescol[:, :],
                                    wt[:, ec * E + half * 512: ec * E + (half + 1) * 512],
                                    start=(ec == 0), stop=(ec == ECH - 1))
                return inst
            OP("pe", grp, rd=[wt_buf, buf("onescol")], wr=[pb], tag=f"pk{name}{half}")
            OP("dv", (lambda t=t, po=po, half=half, p=p: (lambda e: e.tensor_scalar(
                t[po:po + 1, half * 512:(half + 1) * 512], p[po:po + 1, 0:512], -1.0, None, ALU.mult)))(),
               rd=[pb], wr=[buf(f"pk_{name}_c")], tag=f"pk{name}e{half}")



    def newton_rsqrt(src_buf, src_ap, out_buf, out_ap, eps_add, tag, iters=2):
        n = src_ap.shape[-1]
        OP("dv", lambda e: e.tensor_scalar(nsc3[:, 0:n], src_ap, float(eps_add), None, ALU.add),
           rd=[src_buf], wr=[buf("nsc3")], tag=f"{tag}e")
        OP("dv", lambda e: e.tensor_copy(exp_in[:, 0:n], nsc3[:, 0:n].bitcast(I32)),
           rd=[buf("nsc3")], wr=[buf("nsc2")], tag=f"{tag}b")
        OP("sc", lambda e: e.activation(nsc[:, 0:n], exp_in[:, 0:n], AF.Exp,
                                        bias=seedb[:, :],
                                        scale=float(-LN2C * 0.5 / (2 ** 23))),
           rd=[buf("nsc2"), buf("seedb")], wr=[buf("nsc")], tag=f"{tag}s")
        for it in range(iters):
            OP("dv", lambda e: e.tensor_mul(nsc2[:, 0:n], nsc[:, 0:n], nsc[:, 0:n]),
               rd=[buf("nsc")], wr=[buf("nsc2")], tag=f"{tag}{it}a")
            OP("dv", lambda e: e.tensor_mul(nsc2[:, 0:n], nsc2[:, 0:n], nsc3[:, 0:n]),
               rd=[buf("nsc2"), buf("nsc3")], wr=[buf("nsc2")], tag=f"{tag}{it}b")
            OP("dv", lambda e: e.tensor_scalar(nsc2[:, 0:n], nsc2[:, 0:n], -0.5, 1.5, ALU.mult, ALU.add),
               rd=[buf("nsc2")], wr=[buf("nsc2")], tag=f"{tag}{it}c")
            OP("dv", lambda e: e.tensor_mul(nsc[:, 0:n], nsc[:, 0:n], nsc2[:, 0:n]),
               rd=[buf("nsc"), buf("nsc2")], wr=[buf("nsc")], tag=f"{tag}{it}d")
        OP("dv", lambda e: e.tensor_copy(out_ap, nsc[:, 0:n]),
           rd=[buf("nsc")], wr=[out_buf], tag=f"{tag}o")

    def col_to_row(col_buf, col_ap, p, p_buf, c0, tag):
        OP("dv", lambda e: e.tensor_copy(statc[:, :], col_ap),
           rd=[col_buf], wr=[buf("statc")], tag=f"{tag}c")
        OP("pe", lambda e: e.matmul(p[0:1, c0:c0 + 128], statc[:, :], ident_bf[:, :],
                                    start=True, stop=True),
           rd=[buf("statc"), buf("ident_bf")], wr=[p_buf], tag=f"{tag}T")

    def ln1_rows(c, ph):
        tag = f"p{ph}c{c}"
        rdyn, brdyn = rows_dyn[c % 2], buf(f"rows_mr{c % 2}")
        for tt in range(NTT):
            s = c * NTT + tt
            col_to_row(buf("l1r"), l1r[:, s:s + 1], pS1, buf("pS1"), tt * 128, f"{tag}r{tt}")
            col_to_row(buf("l1mr"), l1mr[:, s:s + 1], pS2, buf("pS2"), tt * 128, f"{tag}g{tt}")
            OP("pe", (lambda tt=tt: (lambda e: e.matmul(
                pS2[32:33, tt * 128:(tt + 1) * 128], statc[:, :], ident_bf[:, :],
                start=True, stop=True)))(),
               rd=[buf("statc"), buf("ident_bf")], wr=[buf("pS2")], tag=f"{tag}g32_{tt}")
        OP("dv", lambda e: e.tensor_copy(rrow[0:1, :], pS1[0:1, 0:C]),
           rd=[buf("pS1")], wr=[buf("rrow")], tag=f"{tag}rev")
        OP("dv", (lambda rdyn=rdyn: (lambda e: e.tensor_copy(rdyn[0:1, :], pS2[0:1, 0:C])))(),
           rd=[buf("pS2")], wr=[brdyn], tag=f"{tag}mrev")
        OP("dv", (lambda rdyn=rdyn: (lambda e: e.tensor_copy(rdyn[32:33, :], pS2[32:33, 0:C])))(),
           rd=[buf("pS2")], wr=[brdyn], tag=f"{tag}mrev32")
        OP("pe", lambda e: e.matmul(pS1[:, 0:C], ones_bf[0:1, 0:128], rrow[0:1, :], start=True, stop=True),
           rd=[buf("rrow"), buf("ones_bf")], wr=[buf("pS1")], tag=f"{tag}rbcM")
        OP("dv", lambda e: e.tensor_copy(rbc[:, :], pS1[:, 0:C]),
           rd=[buf("pS1")], wr=[buf("rbc")], tag=f"{tag}rbc")

    def make_xn(c, ph):
        tag = f"p{ph}c{c}"
        for ec in range(ECH):
            xb, bxb = xf32[ec % 2], buf(f"xf32_{ec % 2}")
            dma("sy", xb[:, :], x_fm_d.rearrange("(ec p) t -> p ec t", p=128)[:, ec, c * C:(c + 1) * C],
                [], [bxb], tag=f"{tag}xld{ec}")
            OP("gp", (lambda xb_=xb, ec_=ec: (lambda e: e.tensor_mul(
                xn[:, ec_ * C:(ec_ + 1) * C], xb_[:, :], rbc[:, :])))(),
               rd=[bxb, buf("rbc")], wr=[buf("xn")], tag=f"{tag}xr{ec}")

    def proj_fm(c, wt_buf, wt, pkn, out_s, out_buf, kind, tag):
        rdyn, brdyn = rows_dyn[c % 2], buf(f"rows_mr{c % 2}")
        bpk = [buf(f"pk_{pkn}_c"), buf(f"pk_{pkn}_b"), brdyn, buf(f"rows_ones{c % 2}")]
        for ic in range(ICH):
            p, pb = (pS1, buf("pS1")) if ic % 2 == 0 else (pS2, buf("pS2"))
            pkt, po = pk[pkn]
            def group(e, ic=ic, p=p, rdyn=rdyn, pkt=pkt, po=po):
                e.matmul(p[:, 0:C], pkt[po:po + 2, ic * 128:(ic + 1) * 128],
                         rdyn[po:po + 2, 0:C], start=True, stop=False)
                inst = None
                for ec in range(ECH):
                    inst = e.matmul(p[:, 0:C], wt[:, ec * E + ic * 128: ec * E + (ic + 1) * 128],
                                    xn[:, ec * C:(ec + 1) * C], start=False, stop=(ec == ECH - 1))
                return inst
            OP("pe", group, rd=[wt_buf, buf("xn")] + bpk, wr=[pb], tag=f"{tag}mm{ic}")
            if kind == "elu":
                OP("sc", (lambda p=p: (lambda e: e.activation(e_scr[:, 0:C], p[:, 0:C], AF.Exp)))(),
                   rd=[pb], wr=[buf("e_scr")], tag=f"{tag}ex{ic}")
                OP("dv", lambda e: e.tensor_scalar(t2_scr[:, 0:C], e_scr[:, 0:C], 1.0, 0.0,
                                                   ALU.subtract, ALU.min),
                   rd=[buf("e_scr")], wr=[buf("t2_scr")], tag=f"{tag}mn{ic}")
                OP("dv", (lambda p=p, ic=ic: (lambda e: e.scalar_tensor_tensor(
                    out_s[:, ic * C:(ic + 1) * C], p[:, 0:C], 0.0, t2_scr[:, 0:C], ALU.max, ALU.add)))(),
                   rd=[pb, buf("t2_scr")], wr=[out_buf], tag=f"{tag}el{ic}")
            else:
                OP("sc", (lambda p=p: (lambda e: e.activation(e_scr[:, 0:C], p[:, 0:C], AF.Tanh, scale=0.5)))(),
                   rd=[pb], wr=[buf("e_scr")], tag=f"{tag}th{ic}")
                OP("dv", (lambda p=p, ic=ic: (lambda e: e.tensor_scalar(
                    t2_scr[:, 0:C], p[:, 0:C], sclh[:, ic:ic + 1], None, ALU.mult)))(),
                   rd=[pb, buf("sclh")], wr=[buf("t2_scr")], tag=f"{tag}w1{ic}")
                OP("dv", (lambda ic=ic: (lambda e: e.scalar_tensor_tensor(
                    out_s[:, ic * C:(ic + 1) * C], e_scr[:, 0:C], 1.0, t2_scr[:, 0:C], ALU.add, ALU.mult)))(),
                   rd=[buf("e_scr"), buf("t2_scr")], wr=[out_buf], tag=f"{tag}us{ic}")

    def proj_tm_tile(c, tt, wt_buf, wt, pkn, out_st, out_buf, kind, tag, pj):
        if True:
            s = c * NTT + tt
            rdyn, brdyn = rows_dyn[c % 2], buf(f"rows_mr{c % 2}")
            bpk = [buf(f"pk_{pkn}_c"), buf(f"pk_{pkn}_b"), brdyn, buf(f"rows_ones{c % 2}")]
            if pj % 2 == 0:
                halves = [(pBig[:, 0:512], buf("pBig")), (pBig[:, 512:1024], buf("pBig"))]
            else:
                halves = [(pS1[:, :], buf("pS1")), (pS2[:, :], buf("pS2"))]
            hbufs = list({id(b): b for _, b in halves}.values())
            pkt, po = pk[pkn]
            def group(e, tt=tt, halves=halves, rdyn=rdyn, pkt=pkt, po=po):
                for half in range(2):
                    e.matmul(halves[half][0], rdyn[po:po + 2, tt * 128:(tt + 1) * 128],
                             pkt[po:po + 2, half * 512:(half + 1) * 512],
                             start=True, stop=False)
                inst = None
                for ec in range(ECH):
                    for half in range(2):
                        inst = e.matmul(halves[half][0],
                                        xn[:, ec * C + tt * 128: ec * C + (tt + 1) * 128],
                                        wt[:, ec * E + half * 512: ec * E + (half + 1) * 512],
                                        start=False, stop=(ec == ECH - 1))
                return inst
            OP("pe", group, rd=[wt_buf, buf("xn")] + bpk, wr=hbufs,
               tag=f"{tag}mm{tt}")
            for hf in range(2):
                pap, pb = halves[hf]
                if kind == "elu":
                    OP("sc", (lambda pap=pap: (lambda e: e.activation(
                        e_scr[:, :], pap, AF.Exp)))(),
                       rd=[pb], wr=[buf("e_scr")], tag=f"{tag}ex{tt}_{hf}")
                    OP("dv", lambda e: e.tensor_scalar(t2_scr[:, :], e_scr[:, :], 1.0, 0.0,
                                                       ALU.subtract, ALU.min),
                       rd=[buf("e_scr")], wr=[buf("t2_scr")], tag=f"{tag}mn{tt}_{hf}")
                    OP("dv", (lambda s=s, hf=hf, pap=pap: (lambda e: e.scalar_tensor_tensor(
                        out_st[:, s * E + hf * 512: s * E + (hf + 1) * 512],
                        pap, 0.0, t2_scr[:, :], ALU.max, ALU.add)))(),
                       rd=[pb, buf("t2_scr")], wr=[out_buf], tag=f"{tag}el{tt}_{hf}")
                else:
                    OP("sc", (lambda pap=pap: (lambda e: e.activation(
                        e_scr[:, :], pap, AF.Tanh, scale=0.5)))(),
                       rd=[pb], wr=[buf("e_scr")], tag=f"{tag}th{tt}_{hf}")
                    OP("dv", (lambda s=s, hf=hf, pap=pap: (lambda e: e.scalar_tensor_tensor(
                        out_st[:, s * E + hf * 512: s * E + (hf + 1) * 512], e_scr[:, :], 1.0,
                        pap, ALU.add, ALU.mult)))(),
                       rd=[buf("e_scr"), pb], wr=[out_buf], tag=f"{tag}v{tt}_{hf}")

    def state_update1(s):
        def group(e, s=s):
            inst = None
            for h in range(H):
                inst = e.matmul(S_ps[:, h * D:(h + 1) * D],
                                ktm_st[:, s * E + h * D: s * E + (h + 1) * D],
                                vtm_st[:, s * E + h * D: s * E + (h + 1) * D],
                                start=(s == 0 and h in (0, 4)),
                                stop=(s == NSUB - 1 and h in (3, 7)))
            return inst
        OP("pe", group, rd=[buf("ktm_st"), buf("vtm_st")], wr=[buf("S_ps")], tag=f"p1st{s}")

    def state_update(s, ph):
        def group(e, s=s):
            inst = None
            for h in range(H):
                inst = e.matmul(S_ps[:, h * D:(h + 1) * D],
                                ktm_st[:, s * E + h * D: s * E + (h + 1) * D],
                                vtm_st[:, s * E + h * D: s * E + (h + 1) * D],
                                start=(h in (0, 4)), stop=(h in (3, 7)))
            return inst
        OP("pe", group, rd=[buf("ktm_st"), buf("vtm_st")], wr=[buf("S_ps")], tag=f"p{ph}st{s}")

    def state_accum(s, ph, reset):
        if reset:
            OP("dv", lambda e: e.tensor_copy(S_acc[:, :], S_ps[:, :]),
               rd=[buf("S_ps")], wr=[buf("S_acc")], tag=f"p{ph}sa{s}")
        else:
            OP("dv", lambda e: e.tensor_add(S_acc[:, :], S_acc[:, :], S_ps[:, :]),
               rd=[buf("S_ps"), buf("S_acc")], wr=[buf("S_acc")], tag=f"p{ph}sa{s}")

    def ln1_stats(c):
        tag = f"p1c{c}"
        for tt in range(NTT):
            s = c * NTT + tt
            xt, bxt = x_tile[tt % 2], buf(f"x_tile{tt % 2}")
            dma("sy", xt[:, :], x_tm_d.rearrange("(s p) e -> p s e", p=128)[:, s, :],
                [], [bxt], tag=f"{tag}xt{tt}")
            OP("dv", (lambda xt=xt: (lambda e: e.bn_stats(bn6[:, 0:6], xt[:, 0:512])))(),
               rd=[bxt], wr=[buf("bn6")], tag=f"{tag}bnA{tt}")
            OP("dv", (lambda xt=xt: (lambda e: e.bn_stats(bn6[:, 6:12], xt[:, 512:1024])))(),
               rd=[bxt], wr=[buf("bn6")], tag=f"{tag}bnB{tt}")
            OP("dv", lambda e: e.bn_aggr(bn2[:, :], bn6[:, :]),
               rd=[buf("bn6")], wr=[buf("bn2")], tag=f"{tag}ag{tt}")
            OP("dv", (lambda s=s: (lambda e: e.tensor_copy(l1m[:, s:s + 1], bn2[:, 0:1])))(),
               rd=[buf("bn2")], wr=[buf("l1m")], tag=f"{tag}mc{tt}")
            OP("dv", (lambda s=s: (lambda e: e.tensor_copy(l1v[:, s:s + 1], bn2[:, 1:2])))(),
               rd=[buf("bn2")], wr=[buf("l1v")], tag=f"{tag}vc{tt}")
        newton_rsqrt(buf("l1v"), l1v[:, c * NTT:(c + 1) * NTT], buf("l1r"),
                     l1r[:, c * NTT:(c + 1) * NTT], LN_EPS, f"{tag}rs")
        OP("dv", (lambda c=c: (lambda e: e.tensor_mul(
            l1mr[:, c * NTT:(c + 1) * NTT], l1m[:, c * NTT:(c + 1) * NTT],
            l1r[:, c * NTT:(c + 1) * NTT])))(),
           rd=[buf("l1m"), buf("l1r")], wr=[buf("l1mr")], tag=f"{tag}mr")

    def emit_wf_amax(lo, hi):
        for ix in range(lo, hi):
            ec, qr = ix // 4, ix % 4
            dma("sy", wf32[:, :],
                w_d["wf_t"].rearrange("(ec p) o -> p ec o", p=128)[:, ec, qr * 256:(qr + 1) * 256],
                [], [buf("wf32")], tag=f"wfl{ix}")
            OP("dv", (lambda ix=ix: (lambda e: e.reduce_max(wmax[:, ix:ix + 1], wf32[:, :],
                                                            axis=AX.X, apply_absolute_value=True)))(),
               rd=[buf("wf32")], wr=[buf("wmax")], tag=f"wfm{ix}")

    def emit_wf_swh():
        OP("dv", lambda e: e.reduce_max(wmax1[:, :], wmax[:, :], axis=AX.X),
           rd=[buf("wmax")], wr=[buf("wmax1")], tag="wfmc")
        OP("pe", lambda e: e.matmul(pT2[0:1, 0:128], wmax1[:, :], ident_f32[:, :], start=True, stop=True),
           rd=[buf("wmax1"), buf("ident_f32")], wr=[buf("pT2")], tag="wfT")
        OP("dv", lambda e: e.tensor_copy(wrow[:, :], pT2[0:1, 0:128]),
           rd=[buf("pT2")], wr=[buf("wrow")], tag="wfTe")
        OP("dv", lambda e: e.reduce_max(scalar_sb[:, 0:1], wrow[:, :], axis=AX.X),
           rd=[buf("wrow")], wr=[buf("scalar_sb")], tag="wfam")
        OP("dv", lambda e: e.tensor_scalar(scalar_sb[:, 1:2], scalar_sb[:, 0:1], 1e-12, None, ALU.max),
           rd=[buf("scalar_sb")], wr=[buf("scalar_sb")], tag="swh1")
        OP("dv", lambda e: e.reciprocal(scalar_sb[:, 1:2], scalar_sb[:, 1:2]),
           rd=[buf("scalar_sb")], wr=[buf("scalar_sb")], tag="swh2")
        OP("dv", lambda e: e.tensor_scalar(scalar_sb[:, 1:2], scalar_sb[:, 1:2], FP8_MAX_HALF, None, ALU.mult),
           rd=[buf("scalar_sb")], wr=[buf("scalar_sb")], tag="swh3")
        OP("pe", lambda e: e.matmul(pT2[:, 128:129], ones11[:, :], scalar_sb[0:1, 1:2], start=True, stop=True),
           rd=[buf("scalar_sb"), buf("ones11")], wr=[buf("pT2")], tag="swhB")
        OP("dv", lambda e: e.tensor_copy(swh_col[:, :], pT2[:, 128:129]),
           rd=[buf("pT2")], wr=[buf("swh_col")], tag="swhE")

    def emit_wf_quant(lo, hi):
        for ix in range(lo, hi):
            ec, qr = ix // 4, ix % 4
            dma("sy", wf32[:, :],
                w_d["wf_t"].rearrange("(ec p) o -> p ec o", p=128)[:, ec, qr * 256:(qr + 1) * 256],
                [], [buf("wf32")], tag=f"wfl2_{ix}")
            OP("sc", (lambda ec=ec, qr=qr: (lambda e: e.activation(
                wf8[:, ec * E + qr * 256: ec * E + (qr + 1) * 256], wf32[:, :],
                AF.Identity, bias=0.0, scale=swh_col[:, :])))(),
               rd=[buf("wf32"), buf("swh_col")], wr=[buf("wf8")], tag=f"wfq{ix}")

    wload("wk", wA, "wA", w_d["wk_t"], "ldwk")
    ln1_stats(0)
    ln1_rows(0, 1)
    make_xn(0, 1)
    emit_pk("k", wA, buf("wA"))
    wload("wv", wB, "wB", w_d["wv_t"], "ldwv")
    wload("wo", wC, "wC", w_d["wo_t"], "ldwo")
    for ec in range(ECH):
        dma("gp", wq_bf_d.rearrange("(ec p) i -> p ec i", p=128)[:, ec, :],
            w_d["wq_t"].rearrange("(ec p) i -> p ec i", p=128)[:, ec, :],
            [], [buf("wq_bf_d")], tag=f"pcq{ec}")
        dma("gp", wu_bf_d.rearrange("(ec p) i -> p ec i", p=128)[:, ec, :],
            w_d["wu_t"].rearrange("(ec p) i -> p ec i", p=128)[:, ec, :],
            [], [buf("wu_bf_d")], tag=f"pcu{ec}")
    for c in range(NCH):
        pj = 0
        for tt in range(NTT):
            if tt == 0 and c + 1 < NCH:
                ln1_stats(c + 1)
            proj_tm_tile(c, tt, buf("wA"), wA, "k", ktm_st,
                         buf("ktm_st"), "elu", f"p1c{c}k", pj)
            pj += 1
            if c == 0 and tt == 0:
                emit_pk("v", wB, buf("wB"))
            proj_tm_tile(c, tt, buf("wB"), wB, "v", vtm_st,
                         buf("vtm_st"), "silu2x", f"p1c{c}v", pj)
            pj += 1
            if c + 1 < NCH:
                if tt == 1:
                    ln1_rows(c + 1, 1)
                elif tt == 3:
                    make_xn(c + 1, 1)
            slot = c * NTT + tt
            nslots = NCH * NTT
            lo = slot * (8 * ECH) // nslots
            hi = (slot + 1) * (8 * ECH) // nslots
            for ix in range(lo, hi):
                if ix < 4 * ECH:
                    emit_wf_amax(ix, ix + 1)
                else:
                    if ix == 4 * ECH:
                        emit_wf_swh()
                    emit_wf_quant(ix - 4 * ECH, ix - 4 * ECH + 1)
        for tt in range(NTT):
            s = c * NTT + tt
            state_update1(s)
    OP("dv", lambda e: e.tensor_copy(S_acc[:, :], S_ps[:, :]),
       rd=[buf("S_ps")], wr=[buf("S_acc")], tag="p1sfin")

    def wload_bf(tgt, tgt_buf, src_d, tag, replaces):
        b = buf(replaces)
        deps = list(b.r) + ([b.w] if b.w else [])
        src_buf = {id(wq_bf_d): "wq_bf_d", id(wu_bf_d): "wu_bf_d"}[id(src_d)]
        bb = buf(src_buf)
        deps += [bb.w] + list(bb.r)
        for ec in range(ECH):
            dma("sy", tgt[:, :].rearrange("p (ec i) -> p ec i", ec=ECH)[:, ec, :],
                src_d.rearrange("(ec p) i -> p ec i", p=128)[:, ec, :],
                [], [buf(tgt_buf)], deps=deps, tag=f"{tag}_{ec}")
    wload_bf(wA, "wA", wq_bf_d, "ldwq", replaces="wA")
    wload_bf(wB, "wB", wu_bf_d, "ldwu", replaces="wB")
    emit_pk("q", wA, buf("wA"))
    emit_pk("u", wB, buf("wB"))

    OP("dv", lambda e: e.tensor_scalar(ssb[0][:, :], S_acc[:, :], emask[:, :], None, ALU.mult),
       rd=[buf("S_acc"), buf("emask")], wr=[buf("ssb0")], tag="ccmask")
    dma("gp", ccs_in_d[:, :], ssb[0][:, :], [buf("ssb0")], [buf("ccs_in")], tag="ccst")
    OP("gp", lambda e: e.collective_compute(
        "AllReduce", ALU.add, replica_groups=[[i, i + 1] for i in range(0, n_cores, 2)],
        ins=[ccs_in_d.ap().opt()], outs=[ccs_out_d.ap().opt()]),
        rd=[buf("ccs_in")], wr=[buf("ccs_out")], skey=("cc", False), tag="cc1")
    dma("gp", ssb[0][:, :], ccs_out_d[:, :], [buf("ccs_out")], [buf("ssb0")], tag="ccld")
    OP("dv", lambda e: e.tensor_scalar(sinit[:, :], ssb[0][:, :], omask[:, :], None, ALU.mult),
       rd=[buf("ssb0"), buf("omask")], wr=[buf("sinit")], tag="sinit")

    ln1_rows(0, 2)
    make_xn(0, 2)
    for c in range(NCH):
        tg = f"p2c{c}"
        proj_fm(c, buf("wA"), wA, "q", q_s, buf("q_s"), "elu", f"{tg}q")
        proj_fm(c, buf("wB"), wB, "u", us_s, buf("us_s"), "silu", f"{tg}u")
        for tt in range(NTT):
            s = c * NTT + tt
            kb, bkb = kfm_sub[0], buf("kfm0")
            for half in range(2):
                def ktr(e, s=s, half=half):
                    inst = None
                    for j in range(4):
                        h = half * 4 + j
                        inst = e.matmul(pT1[:, j * 128:(j + 1) * 128],
                                        ktm_st[:, s * E + h * D: s * E + (h + 1) * D],
                                        ident_bf[:, :], start=True, stop=True)
                    return inst
                OP("pe", ktr, rd=[buf("ktm_st"), buf("ident_bf")], wr=[buf("pT1")],
                   tag=f"{tg}kT{tt}_{half}")
                OP("sc", (lambda kb=kb, half=half: (lambda e: e.activation(
                    kb[:, half * 512:(half + 1) * 512], pT1[:, 0:512], AF.Identity)))(),
                   rd=[buf("pT1")], wr=[bkb], tag=f"{tg}kE{tt}_{half}")
            sb_t, bsb = ssb[0], buf("ssb0")
            if s > 0:
                state_accum(s - 1, 2, reset=(s == 1))
                OP("sc", (lambda sb_t=sb_t: (lambda e: e.activation(sb_t[:, :], S_acc[:, :],
                                                                    AF.Identity)))(),
                   rd=[buf("S_acc")], wr=[bsb], tag=f"{tg}sc{tt}")
            def eTg(e, s=s, tt=tt, kb=kb):
                inst = None
                for h in range(H):
                    inst = e.matmul(pBig[:, h * 128:(h + 1) * 128],
                                    kb[:, h * 128:(h + 1) * 128],
                                    q_s[:, h * C + tt * 128: h * C + (tt + 1) * 128],
                                    start=True, stop=True)
                return inst
            OP("pe", eTg, rd=[bkb, buf("q_s")], wr=[buf("pBig")], tag=f"{tg}eT{tt}")
            eb, beb = eT_s[0], buf("eT0")
            OP("dv", (lambda eb=eb: (lambda e: e.tensor_tensor(
                eb[:, :].rearrange("p (h n) -> p h n", h=H),
                pBig[:, :].rearrange("p (h n) -> p h n", h=H),
                triu_f[:, :].rearrange("p (o n) -> p o n", o=1).broadcast_to([128, H, 128]),
                ALU.mult)))(),
               rd=[buf("pBig"), buf("triu_f")], wr=[beb], tag=f"{tg}msk{tt}")
            def outg(e, s=s, tt=tt, eb=eb, sb_t=sb_t):
                inst = None
                for h in range(H):
                    inst = e.matmul(pBig[:, h * 128:(h + 1) * 128],
                                    vtm_st[:, s * E + h * D: s * E + (h + 1) * D],
                                    eb[:, h * 128:(h + 1) * 128], start=True, stop=False)
                    if s > 0:
                        inst = e.matmul(pBig[:, h * 128:(h + 1) * 128],
                                        sb_t[:, h * D:(h + 1) * D],
                                        q_s[:, h * C + tt * 128: h * C + (tt + 1) * 128],
                                        start=False, stop=False)
                    inst = e.matmul(pBig[:, h * 128:(h + 1) * 128],
                                    sinit[:, h * D:(h + 1) * D],
                                    q_s[:, h * C + tt * 128: h * C + (tt + 1) * 128],
                                    start=False, stop=True)
                return inst
            rd_o = [beb, buf("vtm_st"), buf("q_s"), buf("sinit")] + ([bsb] if s > 0 else [])
            OP("pe", outg, rd=rd_o, wr=[buf("pBig")], tag=f"{tg}out{tt}")
            if s < NSUB - 1:
                state_update(s, 2)
            OP("dv", (lambda tt=tt: (lambda e: e.tensor_tensor(
                g1_s[tt][:, :].rearrange("p (h n) -> p h n", h=H),
                pBig[:, :].rearrange("p (h n) -> p h n", h=H),
                us_s[:, :].rearrange("p (h cc) -> p h cc", h=H)[:, :, tt * 128:(tt + 1) * 128],
                ALU.mult)))(),
               rd=[buf("pBig"), buf("us_s")], wr=[buf(f"g1_{tt}")], tag=f"{tg}g1{tt}")
            OP("sc", lambda e: e.activation(eT_s[0][:, :], pBig[:, :], AF.Square),
               rd=[buf("pBig")], wr=[buf("eT0")], tag=f"{tg}sq{tt}")
            def ssqg(e, tt=tt):
                inst = None
                for h in range(H):
                    inst = e.matmul(pT2[0:1, tt * 128:(tt + 1) * 128], onescol[:, :],
                                    eT_s[0][:, h * 128:(h + 1) * 128],
                                    start=(h == 0), stop=(h == H - 1))
                return inst
            OP("pe", ssqg, rd=[buf("eT0"), buf("onescol")], wr=[buf("pT2")], tag=f"{tg}ssq{tt}")
            if tt == 0 and c + 1 < NCH:
                ln1_rows(c + 1, 2)
                make_xn(c + 1, 2)
        OP("dv", lambda e: e.tensor_copy(e_scr[0:1, 0:C], pT2[0:1, 0:C]),
           rd=[buf("pT2")], wr=[buf("e_scr")], tag=f"{tg}sse")
        def ssqT(e):
            inst = None
            for tt in range(NTT):
                inst = e.matmul(pT1[:, 256 + tt:257 + tt],
                                e_scr[0:1, tt * 128:(tt + 1) * 128],
                                ones11[0:1, 0:1], start=True, stop=True)
            return inst
        OP("pe", ssqT, rd=[buf("e_scr"), buf("ones11")], wr=[buf("pT1")], tag=f"{tg}ssT")
        OP("dv", lambda e: e.tensor_copy(nsc2[:, 0:NTT], pT1[:, 256:256 + NTT]),
           rd=[buf("pT1")], wr=[buf("nsc2")], tag=f"{tg}ssTe")
        OP("dv", lambda e: e.tensor_copy(rcp_t[:, :], nsc2[:, 0:NTT]),
           rd=[buf("nsc2")], wr=[buf("rcp_t")], tag=f"{tg}sscp")
        newton_rsqrt(buf("rcp_t"), rcp_t[:, :], buf("nsc2"), nsc2[:, 0:NTT], 0.0, f"{tg}nr")
        OP("dv", lambda e: e.tensor_mul(rcp_t[:, :], rcp_t[:, :], nsc2[:, 0:NTT]),
           rd=[buf("rcp_t"), buf("nsc2")], wr=[buf("rcp_t")], tag=f"{tg}rm1")
        OP("dv", lambda e: e.tensor_scalar(rcp_t[:, :], rcp_t[:, :], 1.0 / 64.0, RMS_EPS,
                                           ALU.mult, ALU.add),
           rd=[buf("rcp_t")], wr=[buf("rcp_t")], tag=f"{tg}rm2")
        OP("dv", lambda e: e.tensor_copy(rmse_bf[:, :], rcp_t[:, :]),
           rd=[buf("rcp_t")], wr=[buf("rmse_bf")], tag=f"{tg}rmb")
        OP("dv", lambda e: e.reciprocal(rcp_t[:, :], rcp_t[:, :]),
           rd=[buf("rcp_t")], wr=[buf("rcp_t")], tag=f"{tg}rcp")
        for tt in range(NTT):
            OP("pe", (lambda tt=tt: (lambda e: e.matmul(
                pT1[0:1, tt * 128:(tt + 1) * 128], rmse_bf[:, tt:tt + 1], ident_bf[:, :],
                start=True, stop=True)))(),
               rd=[buf("rmse_bf"), buf("ident_bf")], wr=[buf("pT1")], tag=f"{tg}rT{tt}")
        OP("dv", lambda e: e.tensor_copy(rrow[0:1, :], pT1[0:1, 0:C]),
           rd=[buf("pT1")], wr=[buf("rrow")], tag=f"{tg}rTe")
        for tt in range(NTT):
            s = c * NTT + tt
            xt, bxt = x_tile[tt % 2], buf(f"x_tile{tt % 2}")
            dma("sy", xt[:, :], x_tm_d.rearrange("(s p) e -> p s e", p=128)[:, s, :],
                [], [bxt], tag=f"{tg}x2{tt}")
            def wog(e, tt=tt):
                e.matmul(pBig[:, 0:512], rrow[0:1, tt * 128:(tt + 1) * 128],
                         bo_row[0:1, 0:512], start=True, stop=False)
                e.matmul(pBig[:, 512:1024], rrow[0:1, tt * 128:(tt + 1) * 128],
                         bo_row[0:1, 512:1024], start=True, stop=False)
                inst = None
                for h in range(H):
                    for half in range(2):
                        inst = e.matmul(pBig[:, half * 512:(half + 1) * 512],
                                        g1_s[tt][:, h * 128:(h + 1) * 128],
                                        wC[:, h * E + half * 512: h * E + (half + 1) * 512],
                                        start=False, stop=(h == H - 1))
                return inst
            OP("pe", wog, rd=[buf(f"g1_{tt}"), buf("wC"), buf("rrow"), buf("bo_row")],
               wr=[buf("pBig")], tag=f"{tg}wo{tt}")
            r2, br2 = res2_s[tt % 2], buf(f"res2_{tt % 2}")
            OP("dv", (lambda tt=tt, xt=xt, r2=r2, s=s: (lambda e: e.scalar_tensor_tensor(
                r2[:, :], pBig[:, :], rcp_t[:, tt:tt + 1], xt[:, :], ALU.mult, ALU.add,
                accum_out=l2sum[:, s:s + 1])))(),
               rd=[buf("pBig"), buf("rcp_t"), bxt], wr=[br2, buf("l2sum")], tag=f"{tg}r2{tt}")
            OP("sc", (lambda r2=r2, s=s: (lambda e: e.activation(
                eT_s[0][:, :], r2[:, :], AF.Square, accum_out=l2sq[:, s:s + 1])))(),
               rd=[br2], wr=[buf("eT0"), buf("l2sq")], tag=f"{tg}l2q{tt}")
            OP("dv", (lambda r2=r2, s=s: (lambda e: e.reduce_max(
                l2max[:, s:s + 1], r2[:, :], axis=AX.X)))(),
               rd=[br2], wr=[buf("l2max")], tag=f"{tg}l2x{tt}")
            OP("dv", (lambda r2=r2, s=s: (lambda e: e.tensor_reduce(
                l2min[:, s:s + 1], r2[:, :], AX.X, ALU.min)))(),
               rd=[br2], wr=[buf("l2min")], tag=f"{tg}l2n{tt}")
            dma("sc", res2_d.rearrange("(s p) e -> p s e", p=128)[:, s, :], r2[:, :],
                [br2], [buf("res2_d")], tag=f"{tg}rs{tt}")

    OP("dv", lambda e: e.tensor_scalar(l2m[:, :], l2sum[:, :], 1.0 / float(E), None, ALU.mult),
       rd=[buf("l2sum")], wr=[buf("l2m")], tag="l2m")
    OP("dv", lambda e: e.tensor_mul(l2r[:, :], l2m[:, :], l2m[:, :]),
       rd=[buf("l2m")], wr=[buf("l2r")], tag="l2ms")
    OP("dv", lambda e: e.scalar_tensor_tensor(l2r[:, :], l2sq[:, :], 1.0 / float(E), l2r[:, :],
                                              ALU.mult, ALU.subtract),
       rd=[buf("l2sq"), buf("l2r")], wr=[buf("l2r")], tag="l2v")
    newton_rsqrt(buf("l2r"), l2r[:, :], buf("l2r"), l2r[:, :], LN_EPS, "l2rs")
    OP("dv", lambda e: e.tensor_sub(l2max[:, :], l2max[:, :], l2m[:, :]),
       rd=[buf("l2max"), buf("l2m")], wr=[buf("l2max")], tag="am1")
    OP("dv", lambda e: e.tensor_sub(l2min[:, :], l2m[:, :], l2min[:, :]),
       rd=[buf("l2min"), buf("l2m")], wr=[buf("l2min")], tag="am2")
    OP("dv", lambda e: e.tensor_max(l2max[:, :], l2max[:, :], l2min[:, :]),
       rd=[buf("l2max"), buf("l2min")], wr=[buf("l2max")], tag="am3")
    OP("dv", lambda e: e.tensor_mul(l2max[:, :], l2max[:, :], l2r[:, :]),
       rd=[buf("l2max"), buf("l2r")], wr=[buf("l2max")], tag="am4")
    OP("dv", lambda e: e.reduce_max(amrow[:, :], l2max[:, :], axis=AX.X),
       rd=[buf("l2max")], wr=[buf("amrow")], tag="am5")
    OP("pe", lambda e: e.matmul(pT2[0:1, 256:384], amrow[:, :], ident_f32[:, :], start=True, stop=True),
       rd=[buf("amrow"), buf("ident_f32")], wr=[buf("pT2")], tag="amT")
    OP("dv", lambda e: e.tensor_copy(wrow[:, :], pT2[0:1, 256:384]),
       rd=[buf("pT2")], wr=[buf("wrow")], tag="amTe")
    OP("dv", lambda e: e.reduce_max(scalar_sb[:, 2:3], wrow[:, :], axis=AX.X),
       rd=[buf("wrow")], wr=[buf("scalar_sb")], tag="aml")
    OP("dv", lambda e: e.tensor_copy(wrow[0:1, :], scalar_sb[0:1, 2:3].broadcast_to([1, 128])),
       rd=[buf("scalar_sb")], wr=[buf("wrow")], tag="amb")
    dma("gp", ccm_in_d[:, :], wrow[0:1, :], [buf("wrow")], [buf("ccm_in")], tag="ccmst")
    OP("gp", lambda e: e.collective_compute(
        "AllReduce", ALU.max, replica_groups=[list(range(n_cores))],
        ins=[ccm_in_d.ap().opt()], outs=[ccm_out_d.ap().opt()]),
        rd=[buf("ccm_in")], wr=[buf("ccm_out")], skey=("cc", False), tag="cc2")
    dma("gp", scalar_sb[0:1, 3:4], ccm_out_d[0:1, 0:1], [buf("ccm_out")], [buf("scalar_sb")], tag="ccmld")
    OP("dv", lambda e: e.tensor_scalar(scalar_sb[:, 4:5], scalar_sb[:, 3:4], 1e-12, None, ALU.max),
       rd=[buf("scalar_sb")], wr=[buf("scalar_sb")], tag="sx1")
    OP("dv", lambda e: e.reciprocal(scalar_sb[:, 4:5], scalar_sb[:, 4:5]),
       rd=[buf("scalar_sb")], wr=[buf("scalar_sb")], tag="sx2")
    OP("dv", lambda e: e.tensor_scalar(scalar_sb[:, 4:5], scalar_sb[:, 4:5], FP8_MAX_HALF, None, ALU.mult),
       rd=[buf("scalar_sb")], wr=[buf("scalar_sb")], tag="sx3")
    OP("dv", lambda e: e.tensor_mul(scalar_sb[:, 5:6], scalar_sb[:, 4:5], scalar_sb[:, 1:2]),
       rd=[buf("scalar_sb")], wr=[buf("scalar_sb")], tag="ssw")
    OP("dv", lambda e: e.tensor_scalar(bf_row[0:1, :], bf_row[0:1, :], scalar_sb[0:1, 5:6], None, ALU.mult),
       rd=[buf("scalar_sb"), buf("bf_row")], wr=[buf("bf_row")], tag="bfs")
    OP("pe", lambda e: e.matmul(pT2[:, 129:130], ones11[:, :], scalar_sb[0:1, 4:5], start=True, stop=True),
       rd=[buf("scalar_sb"), buf("ones11")], wr=[buf("pT2")], tag="sxB")
    OP("dv", lambda e: e.tensor_copy(sxcol[:, :], pT2[:, 129:130]),
       rd=[buf("pT2")], wr=[buf("sxcol")], tag="sxE")
    OP("pe", lambda e: e.matmul(pT2[:, 130:131], ones11[:, :], scalar_sb[0:1, 5:6], start=True, stop=True),
       rd=[buf("scalar_sb"), buf("ones11")], wr=[buf("pT2")], tag="dB")
    OP("dv", lambda e: e.tensor_copy(d_col[:, :], pT2[:, 130:131]),
       rd=[buf("pT2")], wr=[buf("d_col")], tag="dE")
    OP("dv", lambda e: e.reciprocal(d_col[:, :], d_col[:, :]),
       rd=[buf("d_col")], wr=[buf("d_col")], tag="dR")
    OP("dv", lambda e: e.tensor_scalar(qa[:, :], l2r[:, :], sxcol[:, :], None, ALU.mult),
       rd=[buf("l2r"), buf("sxcol")], wr=[buf("qa")], tag="qa")
    OP("dv", lambda e: e.scalar_tensor_tensor(qb[:, :], l2m[:, :], -1.0, qa[:, :], ALU.mult, ALU.mult),
       rd=[buf("l2m"), buf("qa")], wr=[buf("qb")], tag="qb")

    st_ops = []
    for s in range(NSUB):
        ri, bri = resin[s % 2], buf(f"x_tile{s % 2}")
        dma("sy", ri[:, :], res2_d.rearrange("(s p) e -> p s e", p=128)[:, s, :],
            [buf("res2_d")], [bri], tag=f"f3l{s}")
        xq, bxq = xq_s[s % 2], buf(f"xq{s % 2}")
        OP("sc", (lambda ri=ri, xq=xq, s=s: (lambda e: e.activation(
            xq[:, :], ri[:, :], AF.Identity, bias=qb[:, s:s + 1], scale=qa[:, s:s + 1])))(),
           rd=[bri, buf("qa"), buf("qb")], wr=[bxq], tag=f"f3q{s}")
        if s % 2 == 0:
            xqf, bxqf = xqf_s[0], buf("xqf0")
        else:
            xqf, bxqf = eT_s[0], buf("eT0")
        for half in range(2):
            p, pb = (pT1, buf("pT1")) if half == 0 else (pT2, buf("pT2"))
            def xtr(e, xq=xq, half=half, p=p):
                inst = None
                for j in range(4):
                    ec = half * 4 + j
                    inst = e.matmul(p[:, j * 128:(j + 1) * 128],
                                    xq[:, ec * 128:(ec + 1) * 128], ident_bf[:, :],
                                    start=True, stop=True)
                return inst
            OP("pe", xtr, rd=[bxq, buf("ident_bf")], wr=[pb], tag=f"f3T{s}_{half}")
            OP("sc", (lambda xqf=xqf, half=half, p=p: (lambda e: e.activation(
                xqf[:, half * 512:(half + 1) * 512], p[:, 0:512], AF.Identity)))(),
               rd=[pb], wr=[bxqf], tag=f"f3E{s}_{half}")
        def ffn(e, xqf=xqf):
            e.matmul(S_ps[:, 0:512], ones_bf[0:1, 0:128], bf_row[0:1, 0:512], start=True, stop=False)
            e.matmul(S_ps[:, 512:1024], ones_bf[0:1, 0:128], bf_row[0:1, 512:1024], start=True, stop=False)
            inst = None
            for ec in range(ECH):
                for half in range(2):
                    inst = e.matmul(S_ps[:, half * 512:(half + 1) * 512],
                                    xqf[:, ec * 128:(ec + 1) * 128],
                                    wf8[:, ec * E + half * 512: ec * E + (half + 1) * 512],
                                    start=False, stop=(ec == ECH - 1))
            return inst
        OP("pe", ffn, rd=[bxqf, buf("wf8"), buf("bf_row"), buf("ones_bf")], wr=[buf("S_ps")],
           tag=f"f3m{s}")
        yb, byb = res2_s[s % 2], buf(f"res2_{s % 2}")
        OP("dv", (lambda ri=ri, yb=yb: (lambda e: e.scalar_tensor_tensor(
            yb[:, :], S_ps[:, :], d_col[:, :], ri[:, :], ALU.mult, ALU.add)))(),
           rd=[buf("S_ps"), buf("d_col"), bri], wr=[byb], tag=f"f3y{s}")
        st_ops.append(dma("sc", y_d.rearrange("(s p) e -> p s e", p=128)[:, s, :], yb[:, :],
                          [byb], [buf("y_d")], tag=f"f3s{s}"))

    OP("sy", None, deps=st_ops, tag="final")

    if max_ops is not None:
        keep = sch.ops[:max_ops]
        kept = set(id(o) for o in keep)
        sch.ops = keep
        for o in sch.ops:
            o.deps = {d for d in o.deps if id(d) in kept}
    sch.finalize()
    with ExitStack() as sem_es:
        sems = {}
        for key, name in [(("pe", False), "s_pe"), (("dv", False), "s_dv"),
                          (("sc", False), "s_ac"), (("gp", False), "s_gp"),
                          (("sy", False), "s_sy"), (("cc", False), "s_cc")]:
            sems[key] = sem_es.enter_context(nc.semaphore(name))
        for q in ("sy", "sc", "gp"):
            for j in range(Sched.NDMASEM):
                sems[(q, "dma", j)] = sem_es.enter_context(nc.semaphore(f"d_{q}{j}"))
        with nc.Block() as block:
            @block.sync
            def _(h):
                sch.emit("sy", h, sems)

            @block.tensor
            def _(h):
                sch.emit("pe", h, sems)

            @block.vector
            def _(h):
                sch.emit("dv", h, sems)

            @block.scalar
            def _(h):
                sch.emit("sc", h, sems)

            @block.gpsimd
            def _(h):
                sch.emit("gp", h, sems)
    es.close()
    return nc


def make_inputs(core_id, x_full, W, T):
    b, half = core_id // 2, core_id % 2
    xs = np.ascontiguousarray(x_full[b, half * T:(half + 1) * T, :], dtype=np.float32)
    d = {
        "x_tm": xs,
        "x_fm": np.ascontiguousarray(xs.T),
        "wq_t": np.ascontiguousarray(W["Wq"].T),
        "wk_t": np.ascontiguousarray(W["Wk"].T),
        "wv_t": np.ascontiguousarray(W["Wv"].T),
        "wu_t": np.ascontiguousarray(W["Wu"].T),
        "wo_t": np.ascontiguousarray(W["Wo"].T),
        "wf_t": np.ascontiguousarray(W["Wf"].T),
        "bq_row": np.ascontiguousarray(W["bq"].reshape(1, E)),
        "bk_row": np.ascontiguousarray(W["bk"].reshape(1, E)),
        "bv_row": np.ascontiguousarray(W["bv"].reshape(1, E)),
        "bu_row": np.ascontiguousarray(W["bu"].reshape(1, E)),
        "bo_row": np.ascontiguousarray(W["bo"].reshape(1, E)),
        "bf_row": np.ascontiguousarray(W["bf"].reshape(1, E)),
        "ones_row": np.ones((1, 512), dtype=np.float32),
        "scale_pt": np.ascontiguousarray(W["scale"].reshape(H, 128).T),
        "ident": np.eye(128, dtype=np.float32),
        "triu": np.triu(np.ones((128, 128), dtype=np.float32)),
        "onescol": np.ones((128, 1), dtype=np.float32),
        "ones11": np.ones((1, 128), dtype=np.float32),
        "emask": np.full((128, 1), 1.0 - (core_id % 2), dtype=np.float32),
        "omask": np.full((128, 1), float(core_id % 2), dtype=np.float32),
    }
    return d


def kernel(L0_attn_x, L0_attn_Wq, L0_attn_bq, L0_attn_Wk, L0_attn_bk,
           L0_attn_Wv, L0_attn_bv, L0_attn_Wu, L0_attn_bu, L0_attn_Wo,
           L0_attn_bo, L0_attn_scale, L0_ffn_weight, L0_ffn_bias):
    x = np.asarray(L0_attn_x, dtype=np.float32)
    Bn, N, _ = x.shape
    T = N // 2
    W = {
        "Wq": np.asarray(L0_attn_Wq, np.float32), "bq": np.asarray(L0_attn_bq, np.float32),
        "Wk": np.asarray(L0_attn_Wk, np.float32), "bk": np.asarray(L0_attn_bk, np.float32),
        "Wv": np.asarray(L0_attn_Wv, np.float32), "bv": np.asarray(L0_attn_bv, np.float32),
        "Wu": np.asarray(L0_attn_Wu, np.float32), "bu": np.asarray(L0_attn_bu, np.float32),
        "Wo": np.asarray(L0_attn_Wo, np.float32), "bo": np.asarray(L0_attn_bo, np.float32),
        "scale": np.asarray(L0_attn_scale, np.float32),
        "Wf": np.asarray(L0_ffn_weight, np.float32), "bf": np.asarray(L0_ffn_bias, np.float32),
    }
    n_cores = 2 * Bn
    nc = build_program(T=T, C=min(512, T), n_cores=n_cores)
    in_maps = [make_inputs(cid, x, W, T) for cid in range(n_cores)]
    res = run_bass_kernel_spmd(nc, in_maps, core_ids=list(range(n_cores)))
    y = np.empty((Bn, N, E), dtype=np.float32)
    for cid in range(n_cores):
        b, half = cid // 2, cid % 2
        y[b, half * T:(half + 1) * T, :] = res.results[cid]["y"]
    return y
